# version 02
# baseline (speedup 1.0000x reference)
import numpy as np
from contextlib import ExitStack
import concourse.bass as bass
import concourse.mybir as mybir
from concourse.bass_utils import run_bass_kernel_spmd

F32 = mybir.dt.float32
BF16 = mybir.dt.bfloat16
I32 = mybir.dt.int32
U8 = mybir.dt.uint8
AF = mybir.ActivationFunctionType
ALU = mybir.AluOpType
AX = mybir.AxisListType

B, S, H = 4, 2048, 4096
NCORE = 8
TLOC = 1024
INTER = 11008
EPS = 1e-5
RS = 1.0 / float(np.sqrt(128.0))


class Sem:
    def __init__(self, h):
        self.h = h
        self.n = 0
        self.eng = None

    def inc(self, instr, k=1):
        instr.then_inc(self.h, k)
        self.n += k
        if self.eng is not None and k == 16:
            self.eng.wait_ge(self.h, self.n)
        return self.n


class Prog:
    def __init__(self, name, detect_races=True):
        self.nc = bass.Bass("TRN2", target_bir_lowering=False,
                            detect_race_conditions=detect_races)
        self.es = ExitStack()
        self.root_es = self.es
        self.name = name
        self._uid = 0

    def u(self, base):
        self._uid += 1
        return f"{base}_{self._uid}"

    def sb(self, name, shape, dt, scope=None):
        return (scope or self.es).enter_context(
            self.nc.sbuf_tensor(self.u(name), shape, dt))

    def psum(self, name, shape, dt, scope):
        return scope.enter_context(self.nc.psum_tensor(self.u(name), shape, dt))

    def sem(self, name):
        return Sem(self.es.enter_context(self.nc.semaphore(name)))

    def inp(self, name, shape, dt):
        return self.nc.dram_tensor(name, shape, dt, kind="ExternalInput").ap()

    def outp(self, name, shape, dt):
        return self.nc.dram_tensor(name, shape, dt, kind="ExternalOutput").ap()

    def scratch(self, name, shape, dt, shared=False):
        if shared:
            return self.nc.dram_tensor(name, shape, dt, addr_space="Shared").ap()
        return self.nc.dram_tensor(name, shape, dt).ap()



from contextlib import contextmanager


@contextmanager
def block_scope(p):
    old = p.es
    p.es = ExitStack()
    try:
        with p.es:
            yield
    finally:
        p.es = old


def mk_sems(p, nc, tag):
    sems = {k: Sem(p_sem_raw(p, f"{tag}_{k}")) for k in
            ["qs", "qg", "qc", "a", "v", "t", "g"]}
    sems["qs"].eng = nc.sync
    sems["qg"].eng = nc.gpsimd
    sems["qc"].eng = nc.scalar
    return sems


def p_sem_raw(p, name):
    return p.root_es.enter_context(p.nc.semaphore(p.u(name)))


def barrier(p, sems):
    nc = p.nc
    for eng in [nc.sync, nc.gpsimd, nc.scalar, nc.vector, nc.tensor]:
        for s in sems.values():
            eng.wait_ge(s.h, s.n)


def emit_ln_span(p, sems, src, tok0, ntile, sx, sxn, stats, xnfm, ptr8, W):
    nc = p.nc
    qs, a, v, t = sems["qs"], sems["a"], sems["v"], sems["t"]
    st1, st2, stm, stv, stb = stats
    xv = xnfm[:].rearrange("p (kt w) -> p kt w", w=W)
    for i in range(ntile):
        r0 = tok0 + i * 128
        nc.sync.wait_ge(a.h, a.n)
        qs.inc(nc.sync.dma_start(out=sx[:], in_=src[r0:r0 + 128, :]), 16)
        nc.scalar.wait_ge(qs.h, qs.n)
        nc.scalar.wait_ge(t.h, t.n)
        a.inc(nc.scalar.memzero(st2[:]))
        nc.scalar.wait_ge(a.h, a.n)
        a.inc(nc.scalar.activation(out=sxn[:], in_=sx[:], func=AF.Square,
                                   accum_out=st2[:, 0:1]))
        nc.vector.wait_ge(a.h, a.n)

        def vstep(instr):
            v.inc(instr)
            nc.vector.wait_ge(v.h, v.n)
        vstep(nc.vector.reduce_sum(out=st1[:, 0:1], in_=sx[:], axis=AX.X))
        vstep(nc.vector.tensor_scalar(out=stm[:, 0:1], in0=st1[:, 0:1],
                                      scalar1=1.0 / H, scalar2=None, op0=ALU.mult))
        vstep(nc.vector.tensor_tensor(out=stv[:, 0:1], in0=stm[:, 0:1],
                                      in1=stm[:, 0:1], op=ALU.mult))
        vstep(nc.vector.scalar_tensor_tensor(out=stv[:, 0:1], in0=st2[:, 0:1],
                                             scalar=1.0 / H, in1=stv[:, 0:1],
                                             op0=ALU.mult, op1=ALU.subtract))
        vstep(nc.vector.tensor_scalar(out=stv[:, 0:1], in0=stv[:, 0:1],
                                      scalar1=EPS, scalar2=None, op0=ALU.add))
        nc.scalar.wait_ge(v.h, v.n)
        a.inc(nc.scalar.sqrt(out=stv[:, 0:1], in_=stv[:, 0:1]))
        nc.vector.wait_ge(a.h, a.n)
        vstep(nc.vector.reciprocal(out=stv[:, 0:1], in_=stv[:, 0:1]))
        vstep(nc.vector.scalar_tensor_tensor(out=stb[:, 0:1], in0=stm[:, 0:1],
                                             scalar=-1.0, in1=stv[:, 0:1],
                                             op0=ALU.mult, op1=ALU.mult))
        nc.scalar.wait_ge(v.h, v.n)
        nc.scalar.wait_ge(t.h, t.n)
        a.inc(nc.scalar.activation(out=sxn[:], in_=sx[:], func=AF.Identity,
                                   scale=stv[:, 0:1], bias=stb[:, 0:1]))
        nc.tensor.wait_ge(a.h, a.n)
        evfree = [v.n, v.n]
        for g in range(4):
            bnk = g % 2
            nc.tensor.wait_ge(v.h, evfree[bnk])
            for j in range(8):
                kt = g * 8 + j
                t.inc(nc.tensor.matmul(ptr8[bnk][:, j * 128:(j + 1) * 128],
                                       sxn[:, kt * 128:(kt + 1) * 128],
                                       p.seye[:], is_transpose=True,
                                       start=(j == 0), stop=(j == 7)))
            nc.vector.wait_ge(t.h, t.n)
            v.inc(nc.vector.tensor_copy(
                out=xv[:, g * 8:(g + 1) * 8, i * 128:(i + 1) * 128],
                in_=ptr8[bnk][:]))
            evfree[bnk] = v.n


def emit_proj(p, sems, wsrc, chunks, xnfm, W, tks, wbufs, pq, sev, evac_fn,
              bias_fn=None, wrows=H):
    nc = p.nc
    qs, qg, a, t = sems["qs"], sems["qg"], sems["a"], sems["t"]
    nkt = wrows // 128
    wready = [0, 0]
    wfree = [t.n, t.n]
    for idx, (c0, tag) in enumerate(chunks):
        buf = idx % 2
        nc.gpsimd.wait_ge(t.h, wfree[buf])
        qg.inc(nc.gpsimd.dma_start(
            out=wbufs[buf][:],
            in_=wsrc[:, c0:c0 + 128].rearrange("(kt p) o -> p kt o", p=128)), 16)
        wready[buf] = qg.n
        nc.tensor.wait_ge(qg.h, wready[buf])
        nc.tensor.wait_ge(a.h, a.n)
        for tki, (tk0, tw) in enumerate(tks):
            for kt in range(nkt):
                t.inc(nc.tensor.matmul(pq[tki][:, 0:tw],
                                       wbufs[buf][:, kt * 128:(kt + 1) * 128],
                                       xnfm[:, kt * W + tk0: kt * W + tk0 + tw],
                                       start=(kt == 0), stop=(kt == nkt - 1)))
        wfree[buf] = t.n
        nc.scalar.wait_ge(t.h, t.n)
        nc.scalar.wait_ge(qs.h, qs.n)
        for tki, (tk0, tw) in enumerate(tks):
            bias = bias_fn(tag) if bias_fn is not None else None
            if bias is not None:
                a.inc(nc.scalar.activation(out=sev[:, tki * 512: tki * 512 + tw],
                                           in_=pq[tki][:, 0:tw],
                                           func=AF.Identity, bias=bias, scale=1.0))
            else:
                a.inc(nc.scalar.copy(out=sev[:, tki * 512: tki * 512 + tw],
                                     in_=pq[tki][:, 0:tw]))
        nc.sync.wait_ge(a.h, a.n)
        for tki, (tk0, tw) in enumerate(tks):
            evac_fn(tag, tki, sev[:, tki * 512: tki * 512 + tw], tw)


def emit_attention(p, sems, nq_tiles, kv_heads, q_per_kv, qsrc, ksrc, vsrc, sctx,
                   sc2, SKV):
    nc = p.nc
    qs, a, v, t = sems["qs"], sems["a"], sems["v"], sems["t"]
    nkc = SKV // 512
    nkt = SKV // 128
    sq = p.sb("sq", [128, nq_tiles * 128], BF16, sc2)
    sk = p.sb("sk", [128, SKV], BF16, sc2)
    sv = p.sb("sv", [128, SKV], BF16, sc2)
    svt = p.sb("svt", [128, SKV], BF16, sc2)
    sprob = p.sb("sprob", [128, SKV], BF16, sc2)
    sprT = p.sb("sprT", [128, nkt * nq_tiles * 128], BF16, sc2)
    smax = p.sb("smax", [128, 1], F32, sc2)
    snegb = p.sb("snegb", [128, 1], F32, sc2)
    ssum = p.sb("ssum", [128, 1], F32, sc2)
    srec = p.sb("srec", [128, 1], F32, sc2)
    with nc.psum_tensor(p.u("psc"), [128, SKV], F32) as psc, \
         nc.psum_tensor(p.u("pt8a"), [128, 1024], BF16) as pt8a, \
         nc.psum_tensor(p.u("pt8b"), [128, 1024], BF16) as pt8b, \
         nc.psum_tensor(p.u("pc"), [128, 1024], F32) as pc:
        pt8 = [pt8a, pt8b]
        nc.sync.wait_ge(qs.h, qs.n)
        nc.sync.wait_ge(sems["qg"].h, sems["qg"].n)
        nc.sync.wait_ge(sems["qc"].h, sems["qc"].n)
        for kh in range(kv_heads):
            nc.sync.wait_ge(t.h, t.n)
            qs.inc(nc.sync.dma_start(out=sk[:], in_=ksrc[kh * 128:(kh + 1) * 128, :]), 16)
            qs.inc(nc.sync.dma_start(out=sv[:], in_=vsrc[kh * 128:(kh + 1) * 128, :]), 16)
            nc.tensor.wait_ge(qs.h, qs.n)
            evfree = [v.n, v.n]
            for g in range(nkt // 8):
                bnk = g % 2
                nc.tensor.wait_ge(v.h, evfree[bnk])
                for j in range(8):
                    kt = g * 8 + j
                    t.inc(nc.tensor.matmul(pt8[bnk][:, j * 128:(j + 1) * 128],
                                           sv[:, kt * 128:(kt + 1) * 128],
                                           p.seye[:], is_transpose=True,
                                           start=(j == 0), stop=(j == 7)))
                nc.vector.wait_ge(t.h, t.n)
                v.inc(nc.vector.tensor_copy(
                    out=svt[:, g * 1024:(g + 1) * 1024], in_=pt8[bnk][:]))
                evfree[bnk] = v.n
            for qq in range(q_per_kv):
                qh = kh * q_per_kv + qq
                nc.sync.wait_ge(t.h, t.n)
                qs.inc(nc.sync.dma_start(out=sq[:],
                                         in_=qsrc[qh * 128:(qh + 1) * 128, :]), 16)
                nc.tensor.wait_ge(qs.h, qs.n)
                for qt in range(nq_tiles):
                    nc.tensor.wait_ge(a.h, a.n)
                    for kc in range(nkc):
                        t.inc(nc.tensor.matmul(psc[:, kc * 512:(kc + 1) * 512],
                                               sq[:, qt * 128:(qt + 1) * 128],
                                               sk[:, kc * 512:(kc + 1) * 512],
                                               start=True, stop=True))
                    nc.vector.wait_ge(t.h, t.n)
                    v.inc(nc.vector.reduce_max(out=smax[:, 0:1], in_=psc[:],
                                               axis=AX.X))
                    nc.vector.wait_ge(v.h, v.n)
                    v.inc(nc.vector.tensor_scalar(out=snegb[:, 0:1], in0=smax[:, 0:1],
                                                  scalar1=-RS, scalar2=None,
                                                  op0=ALU.mult))
                    nc.scalar.wait_ge(v.h, v.n)
                    nc.scalar.wait_ge(t.h, t.n)
                    a.inc(nc.scalar.memzero(ssum[:]))
                    nc.scalar.wait_ge(a.h, a.n)
                    a.inc(nc.scalar.activation(out=sprob[:], in_=psc[:], func=AF.Exp,
                                               scale=RS, bias=snegb[:, 0:1],
                                               accum_out=ssum[:, 0:1]))
                    nc.vector.wait_ge(a.h, a.n)
                    v.inc(nc.vector.reciprocal(out=srec[:, 0:1], in_=ssum[:, 0:1]))
                    nc.scalar.wait_ge(v.h, v.n)
                    a.inc(nc.scalar.activation(out=sprob[:], in_=sprob[:],
                                               func=AF.Identity, scale=srec[:, 0:1]))
                    nc.scalar.wait_ge(a.h, a.n)
                    nc.tensor.wait_ge(a.h, a.n)
                    evfree = [v.n, v.n]
                    for g in range(nkt // 8):
                        bnk = g % 2
                        nc.tensor.wait_ge(v.h, evfree[bnk])
                        for j in range(8):
                            kt = g * 8 + j
                            t.inc(nc.tensor.matmul(
                                pt8[bnk][:, j * 128:(j + 1) * 128],
                                sprob[:, kt * 128:(kt + 1) * 128],
                                p.seye[:], is_transpose=True,
                                start=(j == 0), stop=(j == 7)))
                        nc.vector.wait_ge(t.h, t.n)
                        prv = sprT[:].rearrange("p (kt q) -> p kt q",
                                                q=nq_tiles * 128)
                        v.inc(nc.vector.tensor_copy(
                            out=prv[:, g * 8:(g + 1) * 8, qt * 128:(qt + 1) * 128],
                            in_=pt8[bnk][:]))
                        evfree[bnk] = v.n
                nc.tensor.wait_ge(v.h, v.n)
                nc.tensor.wait_ge(a.h, a.n)
                for tk in range(nq_tiles * 128 // 512):
                    for kt in range(nkt):
                        t.inc(nc.tensor.matmul(
                            pc[:, tk * 512:(tk + 1) * 512],
                            svt[:, kt * 128:(kt + 1) * 128],
                            sprT[:, kt * nq_tiles * 128 + tk * 512:
                                 kt * nq_tiles * 128 + (tk + 1) * 512],
                            start=(kt == 0), stop=(kt == nkt - 1)))
                nc.scalar.wait_ge(t.h, t.n)
                a.inc(nc.scalar.copy(out=sctx[:, qh * 1024:(qh + 1) * 1024],
                                     in_=pc[:]))


def emit_outproj(p, sems, wsrc, ctx_sb, nft, res_src, dst, bias_row, sones, sc2,
                 wT=False):
    nc = p.nc
    qs, qg, a, v, t = sems["qs"], sems["qg"], sems["a"], sems["v"], sems["t"]
    swo0 = p.sb("swo0", [128, 512], BF16, sc2)
    swo1 = p.sb("swo1", [128, 512], BF16, sc2)
    swo = [swo0, swo1]
    sxr = p.sb("sxr", [128, 512], F32, sc2)
    sout = p.sb("sout", [128, 512], F32, sc2)
    with ExitStack() as pes:
        po = [pes.enter_context(nc.psum_tensor(p.u(f"po{i}"), [128, 512], F32))
              for i in range(8)]
        nc.gpsimd.wait_ge(qs.h, qs.n)
        for dm in range(8):
            wready = [0, 0]
            wfree = [t.n, t.n]
            for hh in range(nft):
                buf = hh % 2
                nc.gpsimd.wait_ge(t.h, wfree[buf])
                qg.inc(nc.gpsimd.dma_start(
                    out=swo[buf][:],
                    in_=wsrc[hh * 128:(hh + 1) * 128, dm * 512:(dm + 1) * 512]), 16)
                wready[buf] = qg.n
                nc.tensor.wait_ge(qg.h, wready[buf])
                if hh == 0:
                    nc.tensor.wait_ge(v.h, v.n)
                last = (hh == nft - 1) and (bias_row is None)
                for tt in range(8):
                    t.inc(nc.tensor.matmul(po[tt][:],
                                           ctx_sb[:, hh * 1024 + tt * 128:
                                                  hh * 1024 + (tt + 1) * 128],
                                           swo[buf][:],
                                           start=(hh == 0), stop=last))
                wfree[buf] = t.n
            if bias_row is not None:
                for tt in range(8):
                    t.inc(nc.tensor.matmul(po[tt][:], sones[:],
                                           bias_row[:, dm * 512:(dm + 1) * 512],
                                           start=False, stop=True))
            for tt in range(8):
                nc.sync.wait_ge(v.h, v.n)
                qs.inc(nc.sync.dma_start(
                    out=sxr[:],
                    in_=res_src[tt * 128:(tt + 1) * 128, dm * 512:(dm + 1) * 512]), 16)
                nc.vector.wait_ge(qs.h, qs.n)
                nc.vector.wait_ge(t.h, t.n)
                v.inc(nc.vector.tensor_add(out=sout[:], in0=po[tt][:], in1=sxr[:]))
                nc.sync.wait_ge(v.h, v.n)
                qs.inc(nc.sync.dma_start(
                    out=dst[tt * 128:(tt + 1) * 128, dm * 512:(dm + 1) * 512],
                    in_=sout[:]), 16)
                nc.vector.wait_ge(qs.h, qs.n)



def emit_block_l0attn(p, sems, xloc, xfull, qkvw, qkvb2, ow, ob, ones, eye, out,
                      debug=False, dbg=None):
    nc = p.nc
    qbuf = dbg["qbuf"] if debug else p.scratch(p.u("qbuf"), [H, TLOC], BF16)
    kbuf = dbg["kbuf"] if debug else p.scratch(p.u("kbuf"), [H, S], BF16)
    vbuf = dbg["vbuf"] if debug else p.scratch(p.u("vbuf"), [H, S], BF16)
    ctxd = dbg["ctxd"] if debug else None

    with block_scope(p):
        qs, qg, qc, a, v, t = (sems[k] for k in ["qs", "qg", "qc", "a", "v", "t"])

        sev = p.sb("sev", [128, 1024], BF16)
        sbias = p.sb("sbias", [128, 96], F32)
        sones = p.sb("sones", [1, 128], F32)
        sob = p.sb("sob", [1, H], F32)
        st1 = p.sb("st1", [128, 1], F32)
        st2 = p.sb("st2", [128, 1], F32)
        stm = p.sb("stm", [128, 1], F32)
        stv = p.sb("stv", [128, 1], F32)
        stb = p.sb("stb", [128, 1], F32)
        stats = (st1, st2, stm, stv, stb)
        sctx = p.sb("sctx", [128, 32 * 1024], BF16)
        p.seye = p.sb("seye", [128, 128], BF16)
        p.seye32 = p.sb("seye32", [128, 128], F32)

        qg.inc(nc.gpsimd.dma_start(out=p.seye[:], in_=eye), 16)
        qs.inc(nc.sync.dma_start(out=p.seye32[:], in_=eye), 16)

        with nc.psum_tensor("pb", [128, 128], F32) as pb, \
             nc.sbuf_tensor("sbq", [128, 128], F32) as sbq:
            qs.inc(nc.sync.dma_start(out=sbq[0:96, :], in_=qkvb2), 16)
            qs.inc(nc.sync.dma_start(out=sones[:], in_=ones), 16)
            qs.inc(nc.sync.dma_start(out=sob[:], in_=ob), 16)
            nc.vector.wait_ge(qs.h, qs.n)
            v.inc(nc.vector.memset(sbq[96:128, :], 0.0))
            nc.tensor.wait_ge(qs.h, qs.n)
            nc.tensor.wait_ge(v.h, v.n)
            nc.tensor.wait_ge(qs.h, qs.n)
            nc.tensor.wait_ge(qg.h, qg.n)
            t.inc(nc.tensor.transpose(pb[:], sbq[:], p.seye32[:]))
            nc.vector.wait_ge(t.h, t.n)
            v.inc(nc.vector.tensor_copy(out=sbias[:, 0:96], in_=pb[:, 0:96]))
            nc.tensor.wait_ge(v.h, v.n)

        barrier(p, sems)
        with ExitStack() as sc1:
            sx = p.sb("sx", [128, H], F32, sc1)
            sxn = p.sb("sxn", [128, H], BF16, sc1)
            xnfm = p.sb("xnfm", [128, 32 * 1024], BF16, sc1)
            wb0 = p.sb("wb0", [128, 32 * 128], BF16, sc1)
            wb1 = p.sb("wb1", [128, 32 * 128], BF16, sc1)
            with nc.psum_tensor("pt8a", [128, 1024], BF16) as pt8a, \
                 nc.psum_tensor("pt8b", [128, 1024], BF16) as pt8b, \
                 nc.psum_tensor("pq0", [128, 512], F32) as pq0, \
                 nc.psum_tensor("pq1", [128, 512], F32) as pq1:
                pq = [pq0, pq1]
                tks = [(0, 512), (512, 512)]

                def bias_of(oc):
                    return sbias[:, oc:oc + 1]

                def ev_q(tag, tki, sv_, tw):
                    qs.inc(nc.sync.dma_start(
                        out=qbuf[tag * 128:(tag + 1) * 128,
                                 tki * 512:(tki + 1) * 512], in_=sv_), 16)

                emit_ln_span(p, sems, xloc, 0, 8, sx, sxn, stats, xnfm,
                             [pt8a, pt8b], 1024)
                nc.tensor.wait_ge(v.h, v.n)
                emit_proj(p, sems, qkvw, [(oc * 128, oc) for oc in range(32)],
                          xnfm, 1024, tks, [wb0, wb1], pq, sev, ev_q, bias_of)

                for sp in range(2):
                    def ev_kv(tag, tki, sv_, tw, sp=sp):
                        dst = kbuf if tag < 64 else vbuf
                        r = (tag - 32) if tag < 64 else (tag - 64)
                        qs.inc(nc.sync.dma_start(
                            out=dst[r * 128:(r + 1) * 128,
                                    sp * 1024 + tki * 512:
                                    sp * 1024 + (tki + 1) * 512], in_=sv_), 16)
                    nc.sync.wait_ge(t.h, t.n)
                    emit_ln_span(p, sems, xfull, sp * 1024, 8, sx, sxn, stats,
                                 xnfm, [pt8a, pt8b], 1024)
                    nc.tensor.wait_ge(v.h, v.n)
                    emit_proj(p, sems, qkvw,
                              [(oc * 128, oc) for oc in range(32, 96)],
                              xnfm, 1024, tks, [wb0, wb1], pq, sev, ev_kv, bias_of)

        barrier(p, sems)
        with ExitStack() as sc2:
            emit_attention(p, sems, 8, 32, 1, qbuf, kbuf, vbuf, sctx, sc2, S)
        if debug:
            nc.sync.wait_ge(sems["a"].h, sems["a"].n)
            for hh in range(32):
                sems["qs"].inc(nc.sync.dma_start(
                    out=ctxd[hh * 128:(hh + 1) * 128, :],
                    in_=sctx[:, hh * 1024:(hh + 1) * 1024]), 16)

        barrier(p, sems)
        with ExitStack() as sc3:
            emit_outproj(p, sems, ow, sctx, 32, xloc, out, sob, sones, sc3)

        nc.sync.wait_ge(qs.h, qs.n)
        nc.sync.wait_ge(qg.h, qg.n)


def build_l0_attn(debug=False):
    p = Prog("l0attn")
    nc = p.nc
    xloc = p.inp("xloc", [TLOC, H], F32)
    xfull = p.inp("xfull", [S, H], F32)
    qkvw = p.inp("qkvw", [H, 3 * H], F32)
    qkvb2 = p.inp("qkvb2", [96, 128], F32)
    ow = p.inp("ow", [H, H], F32)
    ob = p.inp("ob", [1, H], F32)
    ones = p.inp("ones", [1, 128], F32)
    eye = p.inp("eye", [128, 128], F32)
    out = p.outp("out", [TLOC, H], F32)
    dbg = None
    if debug:
        dbg = dict(qbuf=p.outp("qbuf", [H, TLOC], BF16),
                   kbuf=p.outp("kbuf", [H, S], BF16),
                   vbuf=p.outp("vbuf", [H, S], BF16),
                   ctxd=p.outp("ctxd", [H, TLOC], BF16))
    with p.es:
        block = p.es.enter_context(nc.Block())
        sems = mk_sems(p, nc, "a")
        emit_block_l0attn(p, sems, xloc, xfull, qkvw, qkvb2, ow, ob, ones, eye,
                          out, debug, dbg)
    return p



_ONES = np.ones((1, 128), np.float32)
_EYE = np.eye(128, dtype=np.float32)


def _run_l0_attn(x, qkvw, qkvb, ow, ob, debug=False):
    p = build_l0_attn(debug)
    qkvb2 = np.ascontiguousarray(qkvb.reshape(96, 128))
    obr = np.ascontiguousarray(ob.reshape(1, H))
    in_maps = []
    for c in range(NCORE):
        b, h = c // 2, c % 2
        in_maps.append(dict(
            xloc=np.ascontiguousarray(x[b, h * TLOC:(h + 1) * TLOC]),
            xfull=np.ascontiguousarray(x[b]),
            qkvw=qkvw, qkvb2=qkvb2, ow=ow, ob=obr, ones=_ONES, eye=_EYE))
    res = run_bass_kernel_spmd(p.nc, in_maps, core_ids=list(range(NCORE)))
    out = np.empty((B, S, H), np.float32)
    for c in range(NCORE):
        b, h = c // 2, c % 2
        out[b, h * TLOC:(h + 1) * TLOC] = res.results[c]["out"]
    if debug:
        return out, res
    return out


def kernel(L0_attn_x, L0_attn_attn_qkvw, L0_attn_attn_qkvb, L0_attn_attn_ow,
           L0_attn_attn_ob, L0_ffn_qweight_gu, L0_ffn_qzeros_gu, L0_ffn_scales_gu,
           L0_ffn_qweight_dn, L0_ffn_qzeros_dn, L0_ffn_scales_dn, L1_attn_Wq,
           L1_attn_Wk, L1_attn_Wv, L1_attn_Wo, L1_ffn_qweight_gu, L1_ffn_qzeros_gu,
           L1_ffn_scales_gu, L1_ffn_g_idx_gu, L1_ffn_qweight_down,
           L1_ffn_qzeros_down, L1_ffn_scales_down, L1_ffn_g_idx_down):
    return kernel_fused(dict(
        L0_attn_x=L0_attn_x, L0_attn_attn_qkvw=L0_attn_attn_qkvw,
        L0_attn_attn_qkvb=L0_attn_attn_qkvb, L0_attn_attn_ow=L0_attn_attn_ow,
        L0_attn_attn_ob=L0_attn_attn_ob, L0_ffn_qweight_gu=L0_ffn_qweight_gu,
        L0_ffn_qzeros_gu=L0_ffn_qzeros_gu, L0_ffn_scales_gu=L0_ffn_scales_gu,
        L0_ffn_qweight_dn=L0_ffn_qweight_dn, L0_ffn_qzeros_dn=L0_ffn_qzeros_dn,
        L0_ffn_scales_dn=L0_ffn_scales_dn, L1_attn_Wq=L1_attn_Wq,
        L1_attn_Wk=L1_attn_Wk, L1_attn_Wv=L1_attn_Wv, L1_attn_Wo=L1_attn_Wo,
        L1_ffn_qweight_gu=L1_ffn_qweight_gu, L1_ffn_qzeros_gu=L1_ffn_qzeros_gu,
        L1_ffn_scales_gu=L1_ffn_scales_gu, L1_ffn_g_idx_gu=L1_ffn_g_idx_gu,
        L1_ffn_qweight_down=L1_ffn_qweight_down,
        L1_ffn_qzeros_down=L1_ffn_qzeros_down,
        L1_ffn_scales_down=L1_ffn_scales_down,
        L1_ffn_g_idx_down=L1_ffn_g_idx_down))



def emit_unpack_n(p, sems, packed_words, nib_u8, w):
    nc = p.nc
    v = sems["v"]
    pk8 = packed_words.bitcast(U8)
    nc.vector.tensor_scalar(out=nib_u8[:, 0:w:2], in0=pk8, scalar1=15,
                            scalar2=None, op0=ALU.bitwise_and)
    v.inc(nc.vector.tensor_scalar(out=nib_u8[:, 1:w:2], in0=pk8, scalar1=4,
                                  scalar2=None, op0=ALU.logical_shift_right))


def emit_block_l0ffn(p, sems, xin, qw_gu, qz_gu, sc_gu, qw_dn, qz_dn, sc_dn,
                     eye, out):
    nc = p.nc
    hgbuf = p.scratch(p.u("hgbuf"), [INTER, TLOC], BF16)
    hbuf = p.scratch(p.u("hbuf"), [INTER, TLOC], BF16)

    with block_scope(p):
        qs, qg, qc, a, v, t = (sems[k] for k in ["qs", "qg", "qc", "a", "v", "t"])

        p.seye = p.sb("seye", [128, 128], BF16)
        st1 = p.sb("st1", [128, 1], F32)
        st2 = p.sb("st2", [128, 1], F32)
        stm = p.sb("stm", [128, 1], F32)
        stv = p.sb("stv", [128, 1], F32)
        stb = p.sb("stb", [128, 1], F32)
        stats = (st1, st2, stm, stv, stb)
        xnfm = p.sb("xnfm", [128, 32 * 1024], BF16)
        pk_gu_s = p.sb("pk_gu_s", [32, 2752], I32)
        pk_dn_s = p.sb("pk_dn_s", [86, 512], I32)
        snz = p.sb("snz", [128, 512], U8)
        ssc2 = p.sb("ssc2", [128, 512], F32)
        sm2 = p.sb("sm2", [128, 512], BF16)
        sxg = p.sb("sxg", [32, TLOC], BF16)
        sxh = p.sb("sxh", [86, TLOC], BF16)
        sioc = p.sb("sioc", [128, 86], F32)
        sonehot = p.sb("sonehot", [128, 86], BF16)

        qg.inc(nc.gpsimd.dma_start(out=p.seye[:], in_=eye), 16)
        sems["g"].inc(nc.gpsimd.iota(out=sioc[:], pattern=[[1, 86]], base=0,
                                     channel_multiplier=0,
                                     allow_small_or_imprecise_dtypes=True))

        qs.inc(nc.sync.dma_start(out=pk_gu_s[:], in_=qz_gu), 16)
        qs.inc(nc.sync.dma_start(out=pk_dn_s[:], in_=qz_dn), 16)

        def emit_m2_slice(pk_sb, nrow, sc_src, c0, w):
            nc.scalar.wait_ge(v.h, v.n)
            qc.inc(nc.scalar.dma_start(out=ssc2[0:nrow, 0:w],
                                       in_=sc_src[:, c0:c0 + w]), 16)
            nc.vector.wait_ge(qc.h, qc.n)
            nc.vector.wait_ge(t.h, t.n)
            pk8 = pk_sb[0:nrow, c0 // 8:(c0 + w) // 8].bitcast(U8)
            nc.vector.tensor_scalar(out=snz[0:nrow, 0:w:2], in0=pk8, scalar1=15,
                                    scalar2=None, op0=ALU.bitwise_and)
            v.inc(nc.vector.tensor_scalar(out=snz[0:nrow, 1:w:2], in0=pk8,
                                          scalar1=4, scalar2=None,
                                          op0=ALU.logical_shift_right))
            nc.vector.wait_ge(v.h, v.n)
            v.inc(nc.vector.scalar_tensor_tensor(out=ssc2[0:nrow, 0:w],
                                                 in0=snz[0:nrow, 0:w],
                                                 scalar=8.0, in1=ssc2[0:nrow, 0:w],
                                                 op0=ALU.subtract, op1=ALU.mult))
            nc.vector.wait_ge(v.h, v.n)
            v.inc(nc.vector.tensor_scalar(out=sm2[0:nrow, 0:w],
                                          in0=ssc2[0:nrow, 0:w], scalar1=-1.0,
                                          scalar2=None, op0=ALU.mult))

        barrier(p, sems)
        with ExitStack() as sc1:
            sx = p.sb("sx", [128, H], F32, sc1)
            sxn = p.sb("sxn", [128, H], BF16, sc1)
            soh32 = p.sb("soh32", [128, 32], BF16, sc1)
            with nc.psum_tensor(p.u("pt8a"), [128, 1024], BF16) as pt8a, \
                 nc.psum_tensor(p.u("pt8b"), [128, 1024], BF16) as pt8b, \
                 nc.psum_tensor(p.u("pxg"), [32, 1024], F32) as pxg:
                emit_ln_span(p, sems, xin, 0, 8, sx, sxn, stats, xnfm,
                             [pt8a, pt8b], 1024)
                nc.vector.wait_ge(sems["g"].h, sems["g"].n)
                for kt in range(32):
                    nc.vector.wait_ge(t.h, t.n)
                    v.inc(nc.vector.tensor_scalar(out=soh32[:, 0:32],
                                                  in0=sioc[:, 0:32],
                                                  scalar1=float(kt), scalar2=None,
                                                  op0=ALU.is_equal))
                    nc.tensor.wait_ge(v.h, v.n)
                    for tk in range(2):
                        t.inc(nc.tensor.matmul(pxg[:, tk * 512:(tk + 1) * 512],
                                               soh32[:, 0:32],
                                               xnfm[:, kt * 1024 + tk * 512:
                                                    kt * 1024 + (tk + 1) * 512],
                                               start=(kt == 0), stop=(kt == 31)))
                nc.scalar.wait_ge(t.h, t.n)
                a.inc(nc.scalar.copy(out=sxg[:], in_=pxg[:]))
                nc.tensor.wait_ge(a.h, a.n)

        barrier(p, sems)
        with ExitStack() as sc2:
            spk = p.sb("spk", [128, 32 * 64], I32, sc2)
            spl0 = p.sb("spl0", [128, 8 * 512], F32, sc2)
            spl1 = p.sb("spl1", [128, 8 * 512], F32, sc2)
            snib = p.sb("snib", [128, 512], U8, sc2)
            sw = p.sb("sw", [128, 512], BF16, sc2)
            shg = p.sb("shg", [128, 512], BF16, sc2)
            spl = [spl0, spl1]

            def gu_slab(j0, w, is_gate):
                nj = w // 128
                with ExitStack() as pes:
                    pg = [[p.psum(f"pg{jc}{tk}", [128, 512], F32, pes)
                           for tk in range(2)] for jc in range(nj)]
                    nc.sync.wait_ge(v.h, v.n)
                    qs.inc(nc.sync.dma_start(
                        out=spk[:, 0:32 * (w // 8)],
                        in_=qw_gu[:, j0 // 8:(j0 + w) // 8]
                        .rearrange("(kt p) c -> p kt c", p=128)), 16)
                    plready = [0, 0]
                    for ktg in range(4):
                        bnk = ktg % 2
                        nc.scalar.wait_ge(v.h, v.n)
                        qc.inc(nc.scalar.dma_start(
                            out=spl[bnk][:, 0:8 * w],
                            in_=sc_gu[ktg * 8:(ktg + 1) * 8, j0:j0 + w]
                            .unsqueeze(0).broadcast_to([128, 8, w])), 16)
                        plready[bnk] = qc.n
                        for kj in range(8):
                            kt = ktg * 8 + kj
                            nc.vector.wait_ge(qs.h, qs.n)
                            nc.vector.wait_ge(qc.h, plready[bnk])
                            nc.vector.wait_ge(t.h, t.n)
                            emit_unpack_n(p, sems,
                                          spk[:, kt * (w // 8):(kt + 1) * (w // 8)],
                                          snib, w)
                            nc.vector.wait_ge(v.h, v.n)
                            v.inc(nc.vector.scalar_tensor_tensor(
                                out=sw[:, 0:w], in0=snib[:, 0:w], scalar=8.0,
                                in1=spl[bnk][:, kj * w:(kj + 1) * w],
                                op0=ALU.subtract, op1=ALU.mult))
                            nc.tensor.wait_ge(v.h, v.n)
                            for jc in range(nj):
                                for tk in range(2):
                                    t.inc(nc.tensor.matmul(
                                        pg[jc][tk][:],
                                        sw[:, jc * 128:(jc + 1) * 128],
                                        xnfm[:, kt * 1024 + tk * 512:
                                             kt * 1024 + (tk + 1) * 512],
                                        start=(kt == 0), stop=False))
                    emit_m2_slice(pk_gu_s, 32, sc_gu, j0, w)
                    nc.tensor.wait_ge(v.h, v.n)
                    for jc in range(nj):
                        for tk in range(2):
                            t.inc(nc.tensor.matmul(
                                pg[jc][tk][:],
                                sm2[0:32, jc * 128:(jc + 1) * 128],
                                sxg[:, tk * 512:(tk + 1) * 512],
                                start=False, stop=True))
                    for jc in range(nj):
                        jj = j0 + jc * 128
                        for tk in range(2):
                            if is_gate:
                                nc.scalar.wait_ge(t.h, t.n)
                                nc.scalar.wait_ge(qs.h, qs.n)
                                a.inc(nc.scalar.activation(out=shg[:],
                                                           in_=pg[jc][tk][:],
                                                           func=AF.Silu))
                                nc.sync.wait_ge(a.h, a.n)
                                qs.inc(nc.sync.dma_start(
                                    out=hgbuf[jj:jj + 128,
                                              tk * 512:(tk + 1) * 512],
                                    in_=shg[:]), 16)
                            else:
                                jp = jj - INTER
                                nc.scalar.wait_ge(qs.h, qs.n)
                                qc.inc(nc.scalar.dma_start(
                                    out=shg[:],
                                    in_=hgbuf[jp:jp + 128,
                                              tk * 512:(tk + 1) * 512]), 16)
                                nc.vector.wait_ge(qc.h, qc.n)
                                nc.vector.wait_ge(t.h, t.n)
                                nc.vector.wait_ge(qs.h, qs.n)
                                v.inc(nc.vector.tensor_tensor(
                                    out=sw[:, 0:512], in0=pg[jc][tk][:],
                                    in1=shg[:], op=ALU.mult))
                                nc.sync.wait_ge(v.h, v.n)
                                qs.inc(nc.sync.dma_start(
                                    out=hbuf[jp:jp + 128,
                                             tk * 512:(tk + 1) * 512],
                                    in_=sw[:, 0:512]), 16)
                    nc.tensor.wait_ge(a.h, a.n)
                    nc.tensor.wait_ge(v.h, v.n)

            slabs = [(i * 512, 512) for i in range(21)] + [(10752, 256)]
            for j0, w in slabs:
                gu_slab(j0, w, True)
            for j0, w in slabs:
                gu_slab(INTER + j0, w, False)

        barrier(p, sems)
        with ExitStack() as sc3:
            sht = p.sb("sht", [128, TLOC], BF16, sc3)
            with nc.psum_tensor(p.u("pxh"), [86, 1024], F32) as pxh:
                nc.sync.wait_ge(qs.h, qs.n)
                for g in range(86):
                    nc.sync.wait_ge(t.h, t.n)
                    qs.inc(nc.sync.dma_start(out=sht[:],
                                             in_=hbuf[g * 128:(g + 1) * 128, :]), 16)
                    nc.vector.wait_ge(t.h, t.n)
                    nc.vector.wait_ge(sems["g"].h, sems["g"].n)
                    v.inc(nc.vector.tensor_scalar(out=sonehot[:], in0=sioc[:],
                                                  scalar1=float(g), scalar2=None,
                                                  op0=ALU.is_equal))
                    nc.tensor.wait_ge(v.h, v.n)
                    nc.tensor.wait_ge(qs.h, qs.n)
                    for tk in range(2):
                        t.inc(nc.tensor.matmul(pxh[:, tk * 512:(tk + 1) * 512],
                                               sonehot[:],
                                               sht[:, tk * 512:(tk + 1) * 512],
                                               start=(g == 0), stop=(g == 85)))
                nc.scalar.wait_ge(t.h, t.n)
                a.inc(nc.scalar.copy(out=sxh[:], in_=pxh[:]))
                nc.tensor.wait_ge(a.h, a.n)

        barrier(p, sems)
        with ExitStack() as sc4:
            spkd = p.sb("spkd", [128, 64], I32, sc4)
            spld = p.sb("spld", [128, 512], F32, sc4)
            snibd = p.sb("snibd", [128, 512], U8, sc4)
            swd0 = p.sb("swd0", [128, 512], BF16, sc4)
            swd1 = p.sb("swd1", [128, 512], BF16, sc4)
            swd = [swd0, swd1]
            sht0 = p.sb("sht0", [128, TLOC], BF16, sc4)
            sht1 = p.sb("sht1", [128, TLOC], BF16, sc4)
            shts = [sht0, sht1]
            sxr = p.sb("sxr", [128, 512], F32, sc4)
            sout = p.sb("sout", [128, 512], F32, sc4)
            with ExitStack() as pes:
                pd = [p.psum(f"pd{i}", [128, 512], F32, pes) for i in range(8)]
                for dm in range(8):
                    hready = [0, 0]
                    hfree = [t.n, t.n]
                    wfree = t.n
                    for kt in range(86):
                        buf = kt % 2
                        nc.sync.wait_ge(t.h, hfree[buf])
                        qs.inc(nc.sync.dma_start(
                            out=shts[buf][:],
                            in_=hbuf[kt * 128:(kt + 1) * 128, :]), 16)
                        hready[buf] = qs.n
                        nc.scalar.wait_ge(v.h, v.n)
                        qc.inc(nc.scalar.dma_start(
                            out=spkd[:],
                            in_=qw_dn[kt * 128:(kt + 1) * 128,
                                      dm * 64:(dm + 1) * 64]), 16)
                        qc.inc(nc.scalar.dma_start(
                            out=spld[:],
                            in_=sc_dn[kt:kt + 1, dm * 512:(dm + 1) * 512]
                            .partition_broadcast(128)), 16)
                        nc.vector.wait_ge(qc.h, qc.n)
                        nc.vector.wait_ge(t.h, wfree)
                        emit_unpack_n(p, sems, spkd[:], snibd, 512)
                        nc.vector.wait_ge(v.h, v.n)
                        v.inc(nc.vector.scalar_tensor_tensor(
                            out=swd[buf][:], in0=snibd[:], scalar=8.0,
                            in1=spld[:], op0=ALU.subtract, op1=ALU.mult))
                        nc.tensor.wait_ge(v.h, v.n)
                        nc.tensor.wait_ge(qs.h, hready[buf])
                        if kt == 0:
                            nc.tensor.wait_ge(a.h, a.n)
                            nc.tensor.wait_ge(qg.h, qg.n)
                        for tt in range(8):
                            t.inc(nc.tensor.matmul(
                                pd[tt][:],
                                shts[buf][:, tt * 128:(tt + 1) * 128],
                                swd[buf][:], start=(kt == 0), stop=False))
                        hfree[buf] = t.n
                        wfree = t.n
                    emit_m2_slice(pk_dn_s, 86, sc_dn, dm * 512, 512)
                    nc.tensor.wait_ge(v.h, v.n)
                    for tt in range(8):
                        t.inc(nc.tensor.matmul(
                            pd[tt][:], sxh[:, tt * 128:(tt + 1) * 128],
                            sm2[0:86, 0:512],
                            start=False, stop=True))
                    for tt in range(8):
                        nc.gpsimd.wait_ge(v.h, v.n)
                        qg.inc(nc.gpsimd.dma_start(
                            out=sxr[:],
                            in_=xin[tt * 128:(tt + 1) * 128,
                                    dm * 512:(dm + 1) * 512]), 16)
                        nc.vector.wait_ge(qg.h, qg.n)
                        nc.vector.wait_ge(t.h, t.n)
                        v.inc(nc.vector.tensor_add(out=sout[:], in0=pd[tt][:],
                                                   in1=sxr[:]))
                        nc.gpsimd.wait_ge(v.h, v.n)
                        qg.inc(nc.gpsimd.dma_start(
                            out=out[tt * 128:(tt + 1) * 128,
                                    dm * 512:(dm + 1) * 512], in_=sout[:]), 16)
                        nc.vector.wait_ge(qg.h, qg.n)

        nc.sync.wait_ge(qs.h, qs.n)
        nc.sync.wait_ge(qg.h, qg.n)
        nc.sync.wait_ge(qc.h, qc.n)


def build_l0_ffn():
    p = Prog("l0ffn")
    nc = p.nc
    xin = p.inp("xin", [TLOC, H], F32)
    qw_gu = p.inp("qw_gu", [H, 2752], I32)
    qz_gu = p.inp("qz_gu", [32, 2752], I32)
    sc_gu = p.inp("sc_gu", [32, 2 * INTER], F32)
    qw_dn = p.inp("qw_dn", [INTER, 512], I32)
    qz_dn = p.inp("qz_dn", [86, 512], I32)
    sc_dn = p.inp("sc_dn", [86, H], F32)
    eye = p.inp("eye", [128, 128], F32)
    out = p.outp("out", [TLOC, H], F32)
    with p.es:
        block = p.es.enter_context(nc.Block())
        sems = mk_sems(p, nc, "b")
        emit_block_l0ffn(p, sems, xin, qw_gu, qz_gu, sc_gu, qw_dn, qz_dn, sc_dn,
                         eye, out)
    return p


def _run_l0_ffn(x1, qw_gu, qz_gu, sc_gu, qw_dn, qz_dn, sc_dn):
    p = build_l0_ffn()
    in_maps = []
    for c in range(NCORE):
        b, h = c // 2, c % 2
        in_maps.append(dict(
            xin=np.ascontiguousarray(x1[b, h * TLOC:(h + 1) * TLOC]),
            qw_gu=qw_gu, qz_gu=qz_gu, sc_gu=sc_gu,
            qw_dn=qw_dn, qz_dn=qz_dn, sc_dn=sc_dn, eye=_EYE))
    res = run_bass_kernel_spmd(p.nc, in_maps, core_ids=list(range(NCORE)))
    out = np.empty((B, S, H), np.float32)
    for c in range(NCORE):
        b, h = c // 2, c % 2
        out[b, h * TLOC:(h + 1) * TLOC] = res.results[c]["out"]
    return out



ROPE_C = 12582912.0


def emit_block_l1attn(p, sems, xloc, xfull, wq, wk, wv, wo, eye, invf, rotm,
                      posoff, out, span_off=None, debug=False, dbg=None):
    nc = p.nc
    qbuf = dbg["qbuf"] if debug else p.scratch(p.u("qbuf"), [H, TLOC], BF16)
    kbuf = dbg["kbuf"] if debug else p.scratch(p.u("kbuf"), [1024, S], BF16)
    vbuf = dbg["vbuf"] if debug else p.scratch(p.u("vbuf"), [1024, S], BF16)
    wtbuf = dbg["wtbuf"] if debug else p.scratch(p.u("wtbuf"), [H, 10240], BF16)
    ctxd = dbg["ctxd"] if debug else None
    tabd = dbg["tabd"] if debug else None

    with block_scope(p):
        qs, qg, qc, a, v, t = (sems[k] for k in ["qs", "qg", "qc", "a", "v", "t"])

        p.seye = p.sb("seye", [128, 128], BF16)
        srotm = p.sb("srotm", [128, 128], BF16)
        sinvf = p.sb("sinvf", [128, 1], F32)
        soff = p.sb("soff", [128, 1], F32)
        st1 = p.sb("st1", [128, 1], F32)
        st2 = p.sb("st2", [128, 1], F32)
        stm = p.sb("stm", [128, 1], F32)
        stv = p.sb("stv", [128, 1], F32)
        stb = p.sb("stb", [128, 1], F32)
        stats = (st1, st2, stm, stv, stb)
        scos = p.sb("scos", [128, 1024], F32)
        ssin = p.sb("ssin", [128, 1024], F32)
        sang = p.sb("sang", [128, 1024], F32)
        srnd = p.sb("srnd", [128, 1024], F32)
        sctx = p.sb("sctx", [128, 32 * 1024], BF16)

        qg.inc(nc.gpsimd.dma_start(out=p.seye[:], in_=eye), 16)
        qg.inc(nc.gpsimd.dma_start(out=srotm[:], in_=rotm), 16)
        qs.inc(nc.sync.dma_start(out=sinvf[:], in_=invf), 16)
        qs.inc(nc.sync.dma_start(out=soff[:], in_=posoff.partition_broadcast(128)), 16)

        def rope_tables(off_ap, off_imm):
            def vstep(i):
                v.inc(i)
                nc.vector.wait_ge(v.h, v.n)
            nc.gpsimd.wait_ge(v.h, v.n)
            sems["g"].inc(nc.gpsimd.iota(out=sang[:], pattern=[[1, 1024]], base=0,
                                         channel_multiplier=0,
                                         allow_small_or_imprecise_dtypes=True))
            nc.vector.wait_ge(sems["g"].h, sems["g"].n)
            nc.vector.wait_ge(a.h, a.n)
            if off_ap is not None:
                vstep(nc.vector.tensor_scalar(out=sang[:], in0=sang[:],
                                              scalar1=off_ap, scalar2=None,
                                              op0=ALU.add))
            elif off_imm != 0.0:
                vstep(nc.vector.tensor_scalar(out=sang[:], in0=sang[:],
                                              scalar1=float(off_imm), scalar2=None,
                                              op0=ALU.add))
            vstep(nc.vector.tensor_scalar(out=sang[:], in0=sang[:],
                                          scalar1=sinvf[:, 0:1],
                                          scalar2=1.0 / (2 * np.pi),
                                          op0=ALU.mult, op1=ALU.mult))
            vstep(nc.vector.tensor_scalar(out=srnd[:], in0=sang[:], scalar1=ROPE_C,
                                          scalar2=ROPE_C, op0=ALU.add,
                                          op1=ALU.subtract))
            vstep(nc.vector.tensor_tensor(out=srnd[:], in0=sang[:], in1=srnd[:],
                                          op=ALU.subtract))
            vstep(nc.vector.tensor_scalar(out=srnd[:], in0=srnd[:],
                                          scalar1=2 * np.pi, scalar2=None,
                                          op0=ALU.mult))
            nc.scalar.wait_ge(v.h, v.n)
            a.inc(nc.scalar.activation(out=ssin[:], in_=srnd[:], func=AF.Sin))
            nc.vector.wait_ge(a.h, a.n)
            vstep(nc.vector.tensor_scalar(out=sang[:], in0=sang[:], scalar1=0.25,
                                          scalar2=None, op0=ALU.add))
            vstep(nc.vector.tensor_scalar(out=srnd[:], in0=sang[:], scalar1=ROPE_C,
                                          scalar2=ROPE_C, op0=ALU.add,
                                          op1=ALU.subtract))
            vstep(nc.vector.tensor_tensor(out=srnd[:], in0=sang[:], in1=srnd[:],
                                          op=ALU.subtract))
            vstep(nc.vector.tensor_scalar(out=srnd[:], in0=srnd[:],
                                          scalar1=2 * np.pi, scalar2=None,
                                          op0=ALU.mult))
            nc.scalar.wait_ge(v.h, v.n)
            a.inc(nc.scalar.activation(out=scos[:], in_=srnd[:], func=AF.Sin))
            nc.vector.wait_ge(a.h, a.n)

        barrier(p, sems)
        with ExitStack() as sc0:
            swr = p.sb("swr", [128, H], BF16, sc0)
            swt = p.sb("swt", [128, 32 * 128], BF16, sc0)
            with nc.psum_tensor(p.u("pt8a"), [128, 1024], BF16) as pt8a, \
                 nc.psum_tensor(p.u("pt8b"), [128, 1024], BF16) as pt8b:
                pt8 = [pt8a, pt8b]
                srcs = [(wq, 32, 0), (wk, 8, 4096), (wv, 8, 5120), (wo, 32, 6144)]
                for wsrc, noc, colbase in srcs:
                    for oc in range(noc):
                        nc.gpsimd.wait_ge(t.h, t.n)
                        qg.inc(nc.gpsimd.dma_start(
                            out=swr[:], in_=wsrc[oc * 128:(oc + 1) * 128, :]), 16)
                        nc.tensor.wait_ge(qg.h, qg.n)
                        evfree = [v.n, v.n]
                        stored = qs.n
                        for g in range(4):
                            bnk = g % 2
                            nc.tensor.wait_ge(v.h, evfree[bnk])
                            for j in range(8):
                                kt = g * 8 + j
                                t.inc(nc.tensor.matmul(
                                    pt8[bnk][:, j * 128:(j + 1) * 128],
                                    swr[:, kt * 128:(kt + 1) * 128],
                                    p.seye[:], is_transpose=True,
                                    start=(j == 0), stop=(j == 7)))
                            nc.vector.wait_ge(t.h, t.n)
                            nc.vector.wait_ge(qs.h, stored)
                            v.inc(nc.vector.tensor_copy(
                                out=swt[:, g * 1024:(g + 1) * 1024],
                                in_=pt8[bnk][:]))
                            evfree[bnk] = v.n
                        nc.sync.wait_ge(v.h, v.n)
                        qs.inc(nc.sync.dma_start(
                            out=wtbuf.rearrange("(kt pp) o -> pp kt o", pp=128)
                            [:, :, colbase + oc * 128:colbase + (oc + 1) * 128],
                            in_=swt[:].rearrange("pp (kt o) -> pp kt o", o=128)), 16)

        barrier(p, sems)
        with ExitStack() as sc1:
            sx = p.sb("sx", [128, H], F32, sc1)
            sxn = p.sb("sxn", [128, H], BF16, sc1)
            xnfm = p.sb("xnfm", [128, 32 * 1024], BF16, sc1)
            wb0 = p.sb("wb0", [128, 32 * 128], BF16, sc1)
            wb1 = p.sb("wb1", [128, 32 * 128], BF16, sc1)
            sev = p.sb("sev", [128, 1024], BF16, sc1)
            sev2 = p.sb("sev2", [128, 512], BF16, sc1)
            sro1 = p.sb("sro1", [128, 512], F32, sc1)
            sro2 = p.sb("sro2", [128, 512], F32, sc1)
            with nc.psum_tensor(p.u("pt8a"), [128, 1024], BF16) as pt8a, \
                 nc.psum_tensor(p.u("pt8b"), [128, 1024], BF16) as pt8b, \
                 nc.psum_tensor(p.u("pq0"), [128, 512], F32) as pq0, \
                 nc.psum_tensor(p.u("pq1"), [128, 512], F32) as pq1, \
                 nc.psum_tensor(p.u("prot"), [128, 512], F32) as prot:
                pq = [pq0, pq1]
                tks = [(0, 512), (512, 512)]

                def rope_store(dst, row0, col0):
                    def f(tag, tki, sev_sl, tw):
                        nc.tensor.wait_ge(a.h, a.n)
                        nc.tensor.wait_ge(v.h, v.n)
                        t.inc(nc.tensor.matmul(prot[:], srotm[:], sev_sl,
                                               start=True, stop=True))
                        nc.vector.wait_ge(t.h, t.n)
                        v.inc(nc.vector.tensor_tensor(
                            out=sro1[:], in0=sev_sl, in1=scos[:, tki * 512:(tki + 1) * 512],
                            op=ALU.mult))
                        v.inc(nc.vector.tensor_tensor(
                            out=sro2[:], in0=prot[:], in1=ssin[:, tki * 512:(tki + 1) * 512],
                            op=ALU.mult))
                        nc.vector.wait_ge(v.h, v.n)
                        nc.vector.wait_ge(qs.h, qs.n)
                        v.inc(nc.vector.tensor_add(out=sev2[:], in0=sro1[:],
                                                   in1=sro2[:]))
                        nc.sync.wait_ge(v.h, v.n)
                        qs.inc(nc.sync.dma_start(
                            out=dst[row0 + tag * 128:row0 + (tag + 1) * 128,
                                    col0 + tki * 512:col0 + (tki + 1) * 512],
                            in_=sev2[:]), 16)
                    return f

                def plain_store(dst, row0, col0):
                    def f(tag, tki, sev_sl, tw):
                        qs.inc(nc.sync.dma_start(
                            out=dst[row0 + tag * 128:row0 + (tag + 1) * 128,
                                    col0 + tki * 512:col0 + (tki + 1) * 512],
                            in_=sev_sl), 16)
                    return f

                wt_ap = wtbuf

                nc.gpsimd.wait_ge(qs.h, qs.n)
                emit_ln_span(p, sems, xloc, 0, 8, sx, sxn, stats, xnfm,
                             [pt8a, pt8b], 1024)
                rope_tables(soff[:, 0:1], 0.0)
                if debug:
                    nc.sync.wait_ge(a.h, a.n)
                    qs.inc(nc.sync.dma_start(out=tabd[:, 0:1024], in_=scos[:]), 16)
                    qs.inc(nc.sync.dma_start(out=tabd[:, 1024:2048], in_=ssin[:]), 16)
                nc.tensor.wait_ge(v.h, v.n)
                emit_proj(p, sems, wt_ap, [(oc * 128, oc) for oc in range(32)],
                          xnfm, 1024, tks, [wb0, wb1], pq, sev,
                          rope_store(qbuf, 0, 0), None)

                for sp in range(2):
                    nc.sync.wait_ge(t.h, t.n)
                    emit_ln_span(p, sems, xfull, sp * 1024, 8, sx, sxn, stats,
                                 xnfm, [pt8a, pt8b], 1024)
                    if span_off is None:
                        rope_tables(None, float(sp * 1024))
                    else:
                        rope_tables(span_off(sp), 0.0)
                    nc.tensor.wait_ge(v.h, v.n)
                    emit_proj(p, sems, wt_ap,
                              [(4096 + oc * 128, oc) for oc in range(8)],
                              xnfm, 1024, tks, [wb0, wb1], pq, sev,
                              rope_store(kbuf, 0, sp * 1024), None)
                    emit_proj(p, sems, wt_ap,
                              [(5120 + oc * 128, oc) for oc in range(8)],
                              xnfm, 1024, tks, [wb0, wb1], pq, sev,
                              plain_store(vbuf, 0, sp * 1024), None)

        barrier(p, sems)
        with ExitStack() as sc2:
            emit_attention(p, sems, 8, 8, 4, qbuf, kbuf, vbuf, sctx, sc2, S)
        if debug:
            nc.sync.wait_ge(sems["a"].h, sems["a"].n)
            for hh in range(32):
                qs.inc(nc.sync.dma_start(
                    out=ctxd[hh * 128:(hh + 1) * 128, :],
                    in_=sctx[:, hh * 1024:(hh + 1) * 1024]), 16)

        barrier(p, sems)
        with ExitStack() as sc3:
            emit_outproj(p, sems, wtbuf[:, 6144:10240], sctx, 32, xloc, out,
                         None, None, sc3)

        nc.sync.wait_ge(qs.h, qs.n)
        nc.sync.wait_ge(qg.h, qg.n)
        nc.sync.wait_ge(qc.h, qc.n)


def build_l1_attn(debug=False):
    p = Prog("l1attn")
    nc = p.nc
    xloc = p.inp("xloc", [TLOC, H], F32)
    xfull = p.inp("xfull", [S, H], F32)
    wq = p.inp("wq", [H, H], F32)
    wk = p.inp("wk", [1024, H], F32)
    wv = p.inp("wv", [1024, H], F32)
    wo = p.inp("wo", [H, H], F32)
    eye = p.inp("eye", [128, 128], F32)
    invf = p.inp("invf", [128, 1], F32)
    rotm = p.inp("rotm", [128, 128], F32)
    posoff = p.inp("posoff", [1, 1], F32)
    out = p.outp("out", [TLOC, H], F32)
    dbg = None
    if debug:
        dbg = dict(qbuf=p.outp("qbuf", [H, TLOC], BF16),
                   kbuf=p.outp("kbuf", [1024, S], BF16),
                   vbuf=p.outp("vbuf", [1024, S], BF16),
                   wtbuf=p.outp("wtbuf", [H, 10240], BF16),
                   ctxd=p.outp("ctxd", [H, TLOC], BF16),
                   tabd=p.outp("tabd", [128, 2048], F32))
    with p.es:
        block = p.es.enter_context(nc.Block())
        sems = mk_sems(p, nc, "c")
        emit_block_l1attn(p, sems, xloc, xfull, wq, wk, wv, wo, eye, invf, rotm,
                          posoff, out, None, debug, dbg)
    return p


_INVF = (1.0 / (500000.0 ** (np.arange(128, dtype=np.float32) % 64 * 2.0 / 128.0))
         ).reshape(128, 1).astype(np.float32)
_ROTM = np.zeros((128, 128), np.float32)
_ROTM[np.arange(64) + 64, np.arange(64)] = -1.0
_ROTM[np.arange(64), np.arange(64) + 64] = 1.0


def _run_l1_attn(x2, wq, wk, wv, wo, debug=False):
    p = build_l1_attn(debug)
    in_maps = []
    for c in range(NCORE):
        b, h = c // 2, c % 2
        in_maps.append(dict(
            xloc=np.ascontiguousarray(x2[b, h * TLOC:(h + 1) * TLOC]),
            xfull=np.ascontiguousarray(x2[b]),
            wq=wq, wk=wk, wv=wv, wo=wo, eye=_EYE, invf=_INVF, rotm=_ROTM,
            posoff=np.full((1, 1), h * TLOC, np.float32)))
    res = run_bass_kernel_spmd(p.nc, in_maps, core_ids=list(range(NCORE)))
    out = np.empty((B, S, H), np.float32)
    for c in range(NCORE):
        b, h = c // 2, c % 2
        out[b, h * TLOC:(h + 1) * TLOC] = res.results[c]["out"]
    if debug:
        return out, res
    return out



def emit_block_l1ffn(p, sems, xin, qw_gu, qz_gu, sc_gu, gix_gu, qw_dn, qz_dn,
                     sc_dn, gix_dn, shifts, eye, out):
    nc = p.nc
    hgbuf = p.scratch(p.u("hgbuf"), [INTER, TLOC], BF16)
    hbuf = p.scratch(p.u("hbuf"), [INTER, TLOC], BF16)
    scsl_gu = p.scratch(p.u("scsl_gu"), [86, 32, 256], F32)
    scsl_dn = p.scratch(p.u("scsl_dn"), [16, 86, 256], F32)

    with block_scope(p):
        qs, qg, qc, a, v, t = (sems[k] for k in ["qs", "qg", "qc", "a", "v", "t"])

        p.seye = p.sb("seye", [128, 128], BF16)
        st1 = p.sb("st1", [128, 1], F32)
        st2 = p.sb("st2", [128, 1], F32)
        stm = p.sb("stm", [128, 1], F32)
        stv = p.sb("stv", [128, 1], F32)
        stb = p.sb("stb", [128, 1], F32)
        stats = (st1, st2, stm, stv, stb)
        xnfm = p.sb("xnfm", [128, 32 * 1024], BF16)
        pk_gu_s = p.sb("pk_gu_s", [32, 2752], I32)
        pk_dn_s = p.sb("pk_dn_s", [86, 512], I32)
        snz = p.sb("snz", [128, 512], U8)
        ssc2 = p.sb("ssc2", [128, 512], F32)
        sm2 = p.sb("sm2", [128, 512], BF16)
        sxg = p.sb("sxg", [32, TLOC], BF16)
        sxh = p.sb("sxh", [86, TLOC], BF16)
        sioc = p.sb("sioc", [128, 86], F32)
        sonehot = p.sb("sonehot", [128, 86], BF16)
        sshift = p.sb("sshift", [128, 1], I32)
        sgix = p.sb("sgix", [128, 1], I32)
        sgixf = p.sb("sgixf", [128, 1], F32)

        qg.inc(nc.gpsimd.dma_start(out=p.seye[:], in_=eye), 16)
        qs.inc(nc.sync.dma_start(out=sshift[:], in_=shifts), 16)
        sems["g"].inc(nc.gpsimd.iota(out=sioc[:], pattern=[[1, 86]], base=0,
                                     channel_multiplier=0,
                                     allow_small_or_imprecise_dtypes=True))
        qs.inc(nc.sync.dma_start(out=pk_gu_s[:], in_=qz_gu.bitcast(I32)), 16)
        qs.inc(nc.sync.dma_start(out=pk_dn_s[:], in_=qz_dn.bitcast(I32)), 16)
        qs.inc(nc.sync.dma_start(out=scsl_gu,
                                 in_=sc_gu.rearrange("g (s w) -> s g w", w=256)), 16)
        qs.inc(nc.sync.dma_start(out=scsl_dn,
                                 in_=sc_dn.rearrange("g (s w) -> s g w", w=256)), 16)
        sc_gu_flat = scsl_gu.rearrange("s g w -> (s g) w")
        sc_dn_flat = scsl_dn.rearrange("s g w -> (s g) w")
        nc.gpsimd.wait_ge(qs.h, qs.n)

        def emit_m2_slice_d(pk_sb, nrow, sc_src, c0, w):
            nc.scalar.wait_ge(v.h, v.n)
            qc.inc(nc.scalar.dma_start(out=ssc2[0:nrow, 0:w],
                                       in_=sc_src[:, c0:c0 + w]), 16)
            nc.vector.wait_ge(qc.h, qc.n)
            nc.vector.wait_ge(t.h, t.n)
            pk8 = pk_sb[0:nrow, c0 // 8:(c0 + w) // 8].bitcast(U8)
            nc.vector.tensor_scalar(out=snz[0:nrow, 0:w:2], in0=pk8, scalar1=15,
                                    scalar2=None, op0=ALU.bitwise_and)
            v.inc(nc.vector.tensor_scalar(out=snz[0:nrow, 1:w:2], in0=pk8,
                                          scalar1=4, scalar2=None,
                                          op0=ALU.logical_shift_right))
            nc.vector.wait_ge(v.h, v.n)
            v.inc(nc.vector.scalar_tensor_tensor(out=ssc2[0:nrow, 0:w],
                                                 in0=snz[0:nrow, 0:w],
                                                 scalar=7.0, in1=ssc2[0:nrow, 0:w],
                                                 op0=ALU.subtract, op1=ALU.mult))
            nc.vector.wait_ge(v.h, v.n)
            v.inc(nc.vector.tensor_scalar(out=sm2[0:nrow, 0:w],
                                          in0=ssc2[0:nrow, 0:w], scalar1=-1.0,
                                          scalar2=None, op0=ALU.mult))

        def onehot_gidx(gix_src, k0, ncols):
            nc.sync.wait_ge(v.h, v.n)
            qs.inc(nc.sync.dma_start(out=sgix[:], in_=gix_src[k0:k0 + 128, :]), 16)
            nc.vector.wait_ge(qs.h, qs.n)
            v.inc(nc.vector.tensor_copy(out=sgixf[:], in_=sgix[:]))
            nc.vector.wait_ge(v.h, v.n)
            nc.vector.wait_ge(t.h, t.n)
            nc.vector.wait_ge(sems["g"].h, sems["g"].n)
            v.inc(nc.vector.tensor_scalar(out=sonehot[:, 0:ncols],
                                          in0=sioc[:, 0:ncols],
                                          scalar1=sgixf[:, 0:1], scalar2=None,
                                          op0=ALU.is_equal))

        barrier(p, sems)
        with ExitStack() as sc1:
            sx = p.sb("sx", [128, H], F32, sc1)
            sxn = p.sb("sxn", [128, H], BF16, sc1)
            with nc.psum_tensor(p.u("pt8a"), [128, 1024], BF16) as pt8a, \
                 nc.psum_tensor(p.u("pt8b"), [128, 1024], BF16) as pt8b, \
                 nc.psum_tensor(p.u("pxg"), [32, 1024], F32) as pxg:
                emit_ln_span(p, sems, xin, 0, 8, sx, sxn, stats, xnfm,
                             [pt8a, pt8b], 1024)
                for kt in range(32):
                    onehot_gidx(gix_gu, kt * 128, 32)
                    nc.tensor.wait_ge(v.h, v.n)
                    for tk in range(2):
                        t.inc(nc.tensor.matmul(pxg[:, tk * 512:(tk + 1) * 512],
                                               sonehot[:, 0:32],
                                               xnfm[:, kt * 1024 + tk * 512:
                                                    kt * 1024 + (tk + 1) * 512],
                                               start=(kt == 0), stop=(kt == 31)))
                nc.scalar.wait_ge(t.h, t.n)
                a.inc(nc.scalar.copy(out=sxg[:], in_=pxg[:]))
                nc.tensor.wait_ge(a.h, a.n)

        barrier(p, sems)
        with ExitStack() as sc2:
            spk = p.sb("spk", [128, 512], I32, sc2)
            spl = p.sb("spl", [128, 512], F32, sc2)
            snib = p.sb("snib", [128, 512], I32, sc2)
            sw = p.sb("sw", [128, 512], BF16, sc2)
            shg = p.sb("shg", [128, 512], BF16, sc2)

            def gu_slab_d(j0, w, is_gate):
                nj = w // 128
                with ExitStack() as pes:
                    pg = [[p.psum(f"pg{jc}{tk}", [128, 512], F32, pes)
                           for tk in range(2)] for jc in range(nj)]
                    for kt in range(32):
                        nc.gpsimd.wait_ge(v.h, v.n)
                        qg.inc(nc.gpsimd.dma_start(
                            out=spk[:, 0:w],
                            in_=qw_gu.bitcast(I32)[kt * 16:(kt + 1) * 16, j0:j0 + w]
                            .unsqueeze(1).broadcast_to([16, 8, w])), 16)
                        nc.gpsimd.wait_ge(v.h, v.n)
                        qg.inc(nc.gpsimd.dma_start(
                            out=sgix[:], in_=gix_gu[kt * 128:(kt + 1) * 128, :]), 16)
                        for gi in range(w // 256):
                            ci = (j0 + gi * 256) // 256
                            qg.inc(nc.gpsimd.indirect_dma_start(
                                out=spl[:, gi * 256:(gi + 1) * 256], out_offset=None,
                                in_=sc_gu_flat,
                                in_offset=bass.IndirectOffsetOnAxis(sgix[:, 0:1], 0),
                                element_offset=ci * 32 * 256), 16)
                        nc.vector.wait_ge(qg.h, qg.n)
                        nc.vector.wait_ge(t.h, t.n)
                        nc.vector.wait_ge(qs.h, qs.n)
                        v.inc(nc.vector.tensor_scalar(
                            out=snib[:, 0:w], in0=spk[:, 0:w],
                            scalar1=sshift[:, 0:1], scalar2=15,
                            op0=ALU.logical_shift_right, op1=ALU.bitwise_and))
                        nc.vector.wait_ge(v.h, v.n)
                        v.inc(nc.vector.scalar_tensor_tensor(
                            out=sw[:, 0:w], in0=snib[:, 0:w], scalar=8.0,
                            in1=spl[:, 0:w], op0=ALU.subtract, op1=ALU.mult))
                        nc.tensor.wait_ge(v.h, v.n)
                        if kt == 0:
                            nc.tensor.wait_ge(a.h, a.n)
                        for jc in range(nj):
                            for tk in range(2):
                                t.inc(nc.tensor.matmul(
                                    pg[jc][tk][:],
                                    sw[:, jc * 128:(jc + 1) * 128],
                                    xnfm[:, kt * 1024 + tk * 512:
                                         kt * 1024 + (tk + 1) * 512],
                                    start=(kt == 0), stop=False))
                    emit_m2_slice_d(pk_gu_s, 32, sc_gu, j0, w)
                    nc.tensor.wait_ge(v.h, v.n)
                    for jc in range(nj):
                        for tk in range(2):
                            t.inc(nc.tensor.matmul(
                                pg[jc][tk][:],
                                sm2[0:32, jc * 128:(jc + 1) * 128],
                                sxg[:, tk * 512:(tk + 1) * 512],
                                start=False, stop=True))
                    for jc in range(nj):
                        jj = j0 + jc * 128
                        for tk in range(2):
                            if is_gate:
                                nc.scalar.wait_ge(t.h, t.n)
                                nc.scalar.wait_ge(qs.h, qs.n)
                                a.inc(nc.scalar.activation(out=shg[:],
                                                           in_=pg[jc][tk][:],
                                                           func=AF.Silu))
                                nc.sync.wait_ge(a.h, a.n)
                                qs.inc(nc.sync.dma_start(
                                    out=hgbuf[jj:jj + 128,
                                              tk * 512:(tk + 1) * 512],
                                    in_=shg[:]), 16)
                            else:
                                jp = jj - INTER
                                nc.scalar.wait_ge(qs.h, qs.n)
                                qc.inc(nc.scalar.dma_start(
                                    out=shg[:],
                                    in_=hgbuf[jp:jp + 128,
                                              tk * 512:(tk + 1) * 512]), 16)
                                nc.vector.wait_ge(qc.h, qc.n)
                                nc.vector.wait_ge(t.h, t.n)
                                nc.vector.wait_ge(qs.h, qs.n)
                                v.inc(nc.vector.tensor_tensor(
                                    out=sw[:, 0:512], in0=pg[jc][tk][:],
                                    in1=shg[:], op=ALU.mult))
                                nc.sync.wait_ge(v.h, v.n)
                                qs.inc(nc.sync.dma_start(
                                    out=hbuf[jp:jp + 128,
                                             tk * 512:(tk + 1) * 512],
                                    in_=sw[:, 0:512]), 16)
                    nc.tensor.wait_ge(a.h, a.n)
                    nc.tensor.wait_ge(v.h, v.n)

            slabs = [(i * 512, 512) for i in range(21)] + [(10752, 256)]
            for j0, w in slabs:
                gu_slab_d(j0, w, True)
            for j0, w in slabs:
                gu_slab_d(INTER + j0, w, False)

        barrier(p, sems)
        with ExitStack() as sc3:
            sht = p.sb("sht", [128, TLOC], BF16, sc3)
            with nc.psum_tensor(p.u("pxh"), [86, 1024], F32) as pxh:
                nc.sync.wait_ge(qs.h, qs.n)
                for g in range(86):
                    nc.sync.wait_ge(t.h, t.n)
                    qs.inc(nc.sync.dma_start(out=sht[:],
                                             in_=hbuf[g * 128:(g + 1) * 128, :]), 16)
                    onehot_gidx(gix_dn, g * 128, 86)
                    nc.tensor.wait_ge(v.h, v.n)
                    nc.tensor.wait_ge(qs.h, qs.n)
                    for tk in range(2):
                        t.inc(nc.tensor.matmul(pxh[:, tk * 512:(tk + 1) * 512],
                                               sonehot[:],
                                               sht[:, tk * 512:(tk + 1) * 512],
                                               start=(g == 0), stop=(g == 85)))
                nc.scalar.wait_ge(t.h, t.n)
                a.inc(nc.scalar.copy(out=sxh[:], in_=pxh[:]))
                nc.tensor.wait_ge(a.h, a.n)

        barrier(p, sems)
        with ExitStack() as sc4:
            spkd = p.sb("spkd", [128, 512], I32, sc4)
            spld = p.sb("spld", [128, 512], F32, sc4)
            snibd = p.sb("snibd", [128, 512], I32, sc4)
            swd0 = p.sb("swd0", [128, 512], BF16, sc4)
            swd1 = p.sb("swd1", [128, 512], BF16, sc4)
            swd = [swd0, swd1]
            sht0 = p.sb("sht0", [128, TLOC], BF16, sc4)
            sht1 = p.sb("sht1", [128, TLOC], BF16, sc4)
            shts = [sht0, sht1]
            sxr = p.sb("sxr", [128, 512], F32, sc4)
            sout = p.sb("sout", [128, 512], F32, sc4)
            with ExitStack() as pes:
                pd = [p.psum(f"pd{i}", [128, 512], F32, pes) for i in range(8)]
                for dm in range(8):
                    for kt in range(86):
                        buf = kt % 2
                        nc.sync.wait_ge(t.h, t.n)
                        qs.inc(nc.sync.dma_start(
                            out=shts[buf][:],
                            in_=hbuf[kt * 128:(kt + 1) * 128, :]), 16)
                        nc.gpsimd.wait_ge(v.h, v.n)
                        qg.inc(nc.gpsimd.dma_start(
                            out=spkd[:],
                            in_=qw_dn.bitcast(I32)[kt * 16:(kt + 1) * 16,
                                                   dm * 512:(dm + 1) * 512]
                            .unsqueeze(1).broadcast_to([16, 8, 512])), 16)
                        qg.inc(nc.gpsimd.dma_start(
                            out=sgix[:], in_=gix_dn[kt * 128:(kt + 1) * 128, :]), 16)
                        for gi in range(2):
                            ci = dm * 2 + gi
                            qg.inc(nc.gpsimd.indirect_dma_start(
                                out=spld[:, gi * 256:(gi + 1) * 256], out_offset=None,
                                in_=sc_dn_flat,
                                in_offset=bass.IndirectOffsetOnAxis(sgix[:, 0:1], 0),
                                element_offset=ci * 86 * 256), 16)
                        nc.vector.wait_ge(qg.h, qg.n)
                        nc.vector.wait_ge(t.h, t.n)
                        v.inc(nc.vector.tensor_scalar(
                            out=snibd[:], in0=spkd[:],
                            scalar1=sshift[:, 0:1], scalar2=15,
                            op0=ALU.logical_shift_right, op1=ALU.bitwise_and))
                        nc.vector.wait_ge(v.h, v.n)
                        v.inc(nc.vector.scalar_tensor_tensor(
                            out=swd[buf][:], in0=snibd[:], scalar=8.0,
                            in1=spld[:], op0=ALU.subtract, op1=ALU.mult))
                        nc.tensor.wait_ge(v.h, v.n)
                        nc.tensor.wait_ge(qs.h, qs.n)
                        if kt == 0:
                            nc.tensor.wait_ge(a.h, a.n)
                            nc.tensor.wait_ge(qg.h, qg.n)
                        for tt in range(8):
                            t.inc(nc.tensor.matmul(
                                pd[tt][:],
                                shts[buf][:, tt * 128:(tt + 1) * 128],
                                swd[buf][:], start=(kt == 0), stop=False))
                    emit_m2_slice_d(pk_dn_s, 86, sc_dn, dm * 512, 512)
                    nc.tensor.wait_ge(v.h, v.n)
                    for tt in range(8):
                        t.inc(nc.tensor.matmul(
                            pd[tt][:], sxh[:, tt * 128:(tt + 1) * 128],
                            sm2[0:86, 0:512],
                            start=False, stop=True))
                    for tt in range(8):
                        nc.gpsimd.wait_ge(v.h, v.n)
                        qg.inc(nc.gpsimd.dma_start(
                            out=sxr[:],
                            in_=xin[tt * 128:(tt + 1) * 128,
                                    dm * 512:(dm + 1) * 512]), 16)
                        nc.vector.wait_ge(qg.h, qg.n)
                        nc.vector.wait_ge(t.h, t.n)
                        v.inc(nc.vector.tensor_add(out=sout[:], in0=pd[tt][:],
                                                   in1=sxr[:]))
                        nc.gpsimd.wait_ge(v.h, v.n)
                        qg.inc(nc.gpsimd.dma_start(
                            out=out[tt * 128:(tt + 1) * 128,
                                    dm * 512:(dm + 1) * 512], in_=sout[:]), 16)
                        nc.vector.wait_ge(qg.h, qg.n)

        nc.sync.wait_ge(qs.h, qs.n)
        nc.sync.wait_ge(qg.h, qg.n)
        nc.sync.wait_ge(qc.h, qc.n)


def build_l1_ffn():
    p = Prog("l1ffn")
    nc = p.nc
    U32_ = mybir.dt.uint32
    xin = p.inp("xin", [TLOC, H], F32)
    qw_gu = p.inp("qw_gu", [512, 2 * INTER], U32_)
    qz_gu = p.inp("qz_gu", [32, 2752], U32_)
    sc_gu = p.inp("sc_gu", [32, 2 * INTER], F32)
    gix_gu = p.inp("gix_gu", [H, 1], I32)
    qw_dn = p.inp("qw_dn", [1376, H], U32_)
    qz_dn = p.inp("qz_dn", [86, 512], U32_)
    sc_dn = p.inp("sc_dn", [86, H], F32)
    gix_dn = p.inp("gix_dn", [INTER, 1], I32)
    shifts = p.inp("shifts", [128, 1], I32)
    eye = p.inp("eye", [128, 128], F32)
    out = p.outp("out", [TLOC, H], F32)
    with p.es:
        block = p.es.enter_context(nc.Block())
        sems = mk_sems(p, nc, "d")
        emit_block_l1ffn(p, sems, xin, qw_gu, qz_gu, sc_gu, gix_gu, qw_dn, qz_dn,
                         sc_dn, gix_dn, shifts, eye, out)
    return p


_SHIFTS = ((np.arange(128) % 8) * 4).astype(np.int32).reshape(128, 1)


def _run_l1_ffn(x3, qw_gu, qz_gu, sc_gu, gix_gu, qw_dn, qz_dn, sc_dn, gix_dn):
    p = build_l1_ffn()
    in_maps = []
    for c in range(NCORE):
        b, h = c // 2, c % 2
        in_maps.append(dict(
            xin=np.ascontiguousarray(x3[b, h * TLOC:(h + 1) * TLOC]),
            qw_gu=qw_gu, qz_gu=qz_gu, sc_gu=sc_gu,
            gix_gu=np.ascontiguousarray(gix_gu.reshape(H, 1)),
            qw_dn=qw_dn, qz_dn=qz_dn, sc_dn=sc_dn,
            gix_dn=np.ascontiguousarray(gix_dn.reshape(INTER, 1)),
            shifts=_SHIFTS, eye=_EYE))
    res = run_bass_kernel_spmd(p.nc, in_maps, core_ids=list(range(NCORE)))
    out = np.empty((B, S, H), np.float32)
    for c in range(NCORE):
        b, h = c // 2, c % 2
        out[b, h * TLOC:(h + 1) * TLOC] = res.results[c]["out"]
    return out



def emit_exchange(p, xdone, x2loc, x2sh, x2meta, x2flag, nonce, hsel):
    nc = p.nc
    with block_scope(p):
        snon = p.sb("snon", [1, 1], I32)
        shsel = p.sb("shsel", [1, 1], I32)
        sflagin = p.sb("sflagin", [1, 1], I32)
        xq = Sem(p_sem_raw(p, "xq"))
        pollq = Sem(p_sem_raw(p, "pollq"))
        g = nc.gpsimd
        xq.inc(g.dma_start(out=snon[:], in_=nonce), 16)
        xq.inc(g.dma_start(out=shsel[:], in_=hsel), 16)
        g.wait_ge(xq.h, xq.n)
        with g.register("rf") as rf, g.register("rc") as rc, \
             g.register("rn") as rn, g.register("nd") as nd:
            g.reg_load(rn, snon[0:1, 0:1])
            g.reg_mov(rc, 0)
            xq_base = xq.n
            for arm in g.switch_core_id(n=128):
                xq.n = xq_base
                par = (int(arm) % 8) % 2
                xq.inc(g.dma_start(out=x2sh[par * 1024:(par + 1) * 1024, :],
                                   in_=x2loc), 16)
                g.wait_ge(xq.h, xq.n)
                xq.inc(g.dma_start(out=x2meta[par:par + 1, :], in_=shsel[:]), 16)
                g.wait_ge(xq.h, xq.n)
                xq.inc(g.dma_start(out=x2flag[par:par + 1, :], in_=snon[:]), 16)
                g.wait_ge(xq.h, xq.n)
                g.reg_add(rc, rc, 16)
                g.dma_start(out=sflagin[:],
                            in_=x2flag[1 - par:2 - par, :]).then_inc(pollq.h, 16)
                g.wait_ge(pollq.h, rc)
                g.reg_load(rf, sflagin[0:1, 0:1])
                g.reg_alu(nd, rf, rn, ALU.not_equal)
                with g.While(nd):
                    g.reg_add(rc, rc, 16)
                    g.dma_start(out=sflagin[:],
                                in_=x2flag[1 - par:2 - par, :]).then_inc(pollq.h, 16)
                    g.wait_ge(pollq.h, rc)
                    g.reg_load(rf, sflagin[0:1, 0:1])
                    g.reg_alu(nd, rf, rn, ALU.not_equal)
            xq.n = xq_base + 48
        g.sem_inc(xdone.h, 1)
        xdone.n += 1


def build_fused(detect_races=True):
    p = Prog("fused", detect_races)
    nc = p.nc
    U32_ = mybir.dt.uint32
    xloc = p.inp("xloc", [TLOC, H], F32)
    xfull = p.inp("xfull", [S, H], F32)
    qkvw = p.inp("qkvw", [H, 3 * H], F32)
    qkvb2 = p.inp("qkvb2", [96, 128], F32)
    ow = p.inp("ow", [H, H], F32)
    ob = p.inp("ob", [1, H], F32)
    b_qw_gu = p.inp("b_qw_gu", [H, 2752], I32)
    b_qz_gu = p.inp("b_qz_gu", [32, 2752], I32)
    b_sc_gu = p.inp("b_sc_gu", [32, 2 * INTER], F32)
    b_qw_dn = p.inp("b_qw_dn", [INTER, 512], I32)
    b_qz_dn = p.inp("b_qz_dn", [86, 512], I32)
    b_sc_dn = p.inp("b_sc_dn", [86, H], F32)
    wq = p.inp("wq", [H, H], F32)
    wk = p.inp("wk", [1024, H], F32)
    wv = p.inp("wv", [1024, H], F32)
    wo = p.inp("wo", [H, H], F32)
    d_qw_gu = p.inp("d_qw_gu", [512, 2 * INTER], U32_)
    d_qz_gu = p.inp("d_qz_gu", [32, 2752], U32_)
    d_sc_gu = p.inp("d_sc_gu", [32, 2 * INTER], F32)
    d_gix_gu = p.inp("d_gix_gu", [H, 1], I32)
    d_qw_dn = p.inp("d_qw_dn", [1376, H], U32_)
    d_qz_dn = p.inp("d_qz_dn", [86, 512], U32_)
    d_sc_dn = p.inp("d_sc_dn", [86, H], F32)
    d_gix_dn = p.inp("d_gix_dn", [INTER, 1], I32)
    ones = p.inp("ones", [1, 128], F32)
    eye = p.inp("eye", [128, 128], F32)
    invf = p.inp("invf", [128, 1], F32)
    rotm = p.inp("rotm", [128, 128], F32)
    posoff = p.inp("posoff", [1, 1], F32)
    hsel = p.inp("hsel", [1, 1], I32)
    nonce = p.inp("nonce", [1, 1], I32)
    shifts = p.inp("shifts", [128, 1], I32)
    out = p.outp("out", [TLOC, H], F32)
    x1loc = p.scratch("x1loc", [TLOC, H], F32)
    x2loc = p.scratch("x2loc", [TLOC, H], F32)
    x3loc = p.scratch("x3loc", [TLOC, H], F32)
    x2sh = p.scratch("x2sh", [S, H], F32, shared=True)
    x2meta = p.scratch("x2meta", [2, 1], I32, shared=True)
    x2flag = p.scratch("x2flag", [2, 1], I32, shared=True)

    with p.es:
        block = p.es.enter_context(nc.Block())
        sa = mk_sems(p, nc, "a")
        sb_ = mk_sems(p, nc, "b")
        sc_ = mk_sems(p, nc, "c")
        sd_ = mk_sems(p, nc, "d")
        xdone = Sem(p_sem_raw(p, "xdone"))

        emit_block_l0attn(p, sa, xloc, xfull, qkvw, qkvb2, ow, ob, ones, eye,
                          x1loc)
        barrier(p, sa)
        emit_block_l0ffn(p, sb_, x1loc, b_qw_gu, b_qz_gu, b_sc_gu, b_qw_dn,
                         b_qz_dn, b_sc_dn, eye, x2loc)
        barrier(p, sb_)
        emit_exchange(p, xdone, x2loc, x2sh, x2meta, x2flag, nonce, hsel)
        for eng in [nc.sync, nc.gpsimd, nc.scalar, nc.vector, nc.tensor]:
            eng.wait_ge(xdone.h, 1)
        soff_sp = [p.sb(f"soffsp{i}", [128, 1], F32) for i in range(2)]
        smeta_i = p.sb("smeta_i", [128, 1], I32)
        for sp in range(2):
            sc_["qs"].inc(nc.sync.dma_start(
                out=smeta_i[:], in_=x2meta[sp:sp + 1, :].partition_broadcast(128)),
                16)
            nc.vector.wait_ge(sc_["qs"].h, sc_["qs"].n)
            nc.vector.wait_ge(sc_["v"].h, sc_["v"].n)
            sc_["v"].inc(nc.vector.tensor_scalar(
                out=soff_sp[sp][:], in0=smeta_i[:], scalar1=1024.0, scalar2=None,
                op0=ALU.mult))
        nc.vector.wait_ge(sc_["v"].h, sc_["v"].n)
        emit_block_l1attn(p, sc_, x2loc, x2sh, wq, wk, wv, wo, eye, invf, rotm,
                          posoff, x3loc,
                          span_off=lambda sp: soff_sp[sp][:, 0:1])
        barrier(p, sc_)
        emit_block_l1ffn(p, sd_, x3loc, d_qw_gu, d_qz_gu, d_sc_gu, d_gix_gu,
                         d_qw_dn, d_qz_dn, d_sc_dn, d_gix_dn, shifts, eye, out)
    return p


def kernel_fused(inp):
    p = build_fused()
    x = np.asarray(inp["L0_attn_x"], np.float32)
    qkvb2 = np.ascontiguousarray(np.asarray(inp["L0_attn_attn_qkvb"]).reshape(96, 128))
    obr = np.ascontiguousarray(np.asarray(inp["L0_attn_attn_ob"]).reshape(1, H))
    nonce = np.array([[np.random.randint(1, 2**31 - 1)]], np.int32)
    in_maps = []
    for c in range(NCORE):
        b, h = c // 2, c % 2
        in_maps.append(dict(
            xloc=np.ascontiguousarray(x[b, h * TLOC:(h + 1) * TLOC]),
            xfull=np.ascontiguousarray(x[b]),
            qkvw=np.asarray(inp["L0_attn_attn_qkvw"]), qkvb2=qkvb2,
            ow=np.asarray(inp["L0_attn_attn_ow"]), ob=obr,
            b_qw_gu=np.asarray(inp["L0_ffn_qweight_gu"]),
            b_qz_gu=np.asarray(inp["L0_ffn_qzeros_gu"]),
            b_sc_gu=np.asarray(inp["L0_ffn_scales_gu"]),
            b_qw_dn=np.asarray(inp["L0_ffn_qweight_dn"]),
            b_qz_dn=np.asarray(inp["L0_ffn_qzeros_dn"]),
            b_sc_dn=np.asarray(inp["L0_ffn_scales_dn"]),
            wq=np.asarray(inp["L1_attn_Wq"]), wk=np.asarray(inp["L1_attn_Wk"]),
            wv=np.asarray(inp["L1_attn_Wv"]), wo=np.asarray(inp["L1_attn_Wo"]),
            d_qw_gu=np.asarray(inp["L1_ffn_qweight_gu"]),
            d_qz_gu=np.asarray(inp["L1_ffn_qzeros_gu"]),
            d_sc_gu=np.asarray(inp["L1_ffn_scales_gu"]),
            d_gix_gu=np.ascontiguousarray(
                np.asarray(inp["L1_ffn_g_idx_gu"]).reshape(H, 1)),
            d_qw_dn=np.asarray(inp["L1_ffn_qweight_down"]),
            d_qz_dn=np.asarray(inp["L1_ffn_qzeros_down"]),
            d_sc_dn=np.asarray(inp["L1_ffn_scales_down"]),
            d_gix_dn=np.ascontiguousarray(
                np.asarray(inp["L1_ffn_g_idx_down"]).reshape(INTER, 1)),
            ones=_ONES, eye=_EYE, invf=_INVF, rotm=_ROTM,
            posoff=np.full((1, 1), h * TLOC, np.float32),
            hsel=np.full((1, 1), h, np.int32),
            nonce=nonce, shifts=_SHIFTS))
    res = run_bass_kernel_spmd(p.nc, in_maps, core_ids=list(range(NCORE)))
    outv = np.empty((B, S, H), np.float32)
    for c in range(NCORE):
        b, h = c // 2, c % 2
        outv[b, h * TLOC:(h + 1) * TLOC] = res.results[c]["out"]
    return outv


# version 03
# speedup vs baseline: 1.4377x; 1.4377x over previous
import numpy as np
from contextlib import ExitStack
import concourse.bass as bass
import concourse.mybir as mybir
from concourse.bass_utils import run_bass_kernel_spmd

F32 = mybir.dt.float32
BF16 = mybir.dt.bfloat16
I32 = mybir.dt.int32
U8 = mybir.dt.uint8
AF = mybir.ActivationFunctionType
ALU = mybir.AluOpType
AX = mybir.AxisListType

B, S, H = 4, 2048, 4096
NCORE = 8
TLOC = 1024
INTER = 11008
EPS = 1e-5
RS = 1.0 / float(np.sqrt(128.0))


class Sem:
    def __init__(self, h):
        self.h = h
        self.n = 0
        self.eng = None

    def inc(self, instr, k=1):
        instr.then_inc(self.h, k)
        self.n += k
        if self.eng is not None and k == 16:
            self.eng.wait_ge(self.h, self.n)
        return self.n


class Prog:
    def __init__(self, name, detect_races=True):
        self.nc = bass.Bass("TRN2", target_bir_lowering=False,
                            detect_race_conditions=detect_races)
        self.es = ExitStack()
        self.root_es = self.es
        self.name = name
        self._uid = 0

    def u(self, base):
        self._uid += 1
        return f"{base}_{self._uid}"

    def sb(self, name, shape, dt, scope=None):
        return (scope or self.es).enter_context(
            self.nc.sbuf_tensor(self.u(name), shape, dt))

    def psum(self, name, shape, dt, scope):
        return scope.enter_context(self.nc.psum_tensor(self.u(name), shape, dt))

    def sem(self, name):
        return Sem(self.es.enter_context(self.nc.semaphore(name)))

    def inp(self, name, shape, dt):
        return self.nc.dram_tensor(name, shape, dt, kind="ExternalInput").ap()

    def outp(self, name, shape, dt):
        return self.nc.dram_tensor(name, shape, dt, kind="ExternalOutput").ap()

    def scratch(self, name, shape, dt, shared=False):
        if shared:
            return self.nc.dram_tensor(name, shape, dt, addr_space="Shared").ap()
        return self.nc.dram_tensor(name, shape, dt).ap()



from contextlib import contextmanager


@contextmanager
def block_scope(p):
    old = p.es
    p.es = ExitStack()
    try:
        with p.es:
            yield
    finally:
        p.es = old


def mk_sems(p, nc, tag):
    sems = {k: Sem(p_sem_raw(p, f"{tag}_{k}")) for k in
            ["qs", "qg", "qc", "a", "v", "t", "g", "st"]}
    sems["qs"].eng = nc.sync
    sems["qg"].eng = nc.gpsimd
    sems["qc"].eng = nc.scalar
    return sems


def p_sem_raw(p, name):
    return p.root_es.enter_context(p.nc.semaphore(p.u(name)))


def barrier(p, sems):
    nc = p.nc
    for eng in [nc.sync, nc.gpsimd, nc.scalar, nc.vector, nc.tensor]:
        for s in sems.values():
            eng.wait_ge(s.h, s.n)


def emit_ln_span(p, sems, src, tok0, ntile, sx, sxn, stats, xnfm, ptr8, W):
    nc = p.nc
    qs, a, v, t = sems["qs"], sems["a"], sems["v"], sems["t"]
    st1, st2, stm, stv, stb = stats
    xv = xnfm[:].rearrange("p (kt w) -> p kt w", w=W)
    for i in range(ntile):
        r0 = tok0 + i * 128
        nc.sync.wait_ge(a.h, a.n)
        qs.inc(nc.sync.dma_start(out=sx[:], in_=src[r0:r0 + 128, :]), 16)
        nc.scalar.wait_ge(qs.h, qs.n)
        nc.scalar.wait_ge(t.h, t.n)
        a.inc(nc.scalar.memzero(st2[:]))
        nc.scalar.wait_ge(a.h, a.n)
        a.inc(nc.scalar.activation(out=sxn[:], in_=sx[:], func=AF.Square,
                                   accum_out=st2[:, 0:1]))
        nc.vector.wait_ge(a.h, a.n)

        def vstep(instr):
            v.inc(instr)
            nc.vector.wait_ge(v.h, v.n)
        vstep(nc.vector.reduce_sum(out=st1[:, 0:1], in_=sx[:], axis=AX.X))
        vstep(nc.vector.tensor_scalar(out=stm[:, 0:1], in0=st1[:, 0:1],
                                      scalar1=1.0 / H, scalar2=None, op0=ALU.mult))
        vstep(nc.vector.tensor_tensor(out=stv[:, 0:1], in0=stm[:, 0:1],
                                      in1=stm[:, 0:1], op=ALU.mult))
        vstep(nc.vector.scalar_tensor_tensor(out=stv[:, 0:1], in0=st2[:, 0:1],
                                             scalar=1.0 / H, in1=stv[:, 0:1],
                                             op0=ALU.mult, op1=ALU.subtract))
        vstep(nc.vector.tensor_scalar(out=stv[:, 0:1], in0=stv[:, 0:1],
                                      scalar1=EPS, scalar2=None, op0=ALU.add))
        nc.scalar.wait_ge(v.h, v.n)
        a.inc(nc.scalar.sqrt(out=stv[:, 0:1], in_=stv[:, 0:1]))
        nc.vector.wait_ge(a.h, a.n)
        vstep(nc.vector.reciprocal(out=stv[:, 0:1], in_=stv[:, 0:1]))
        vstep(nc.vector.scalar_tensor_tensor(out=stb[:, 0:1], in0=stm[:, 0:1],
                                             scalar=-1.0, in1=stv[:, 0:1],
                                             op0=ALU.mult, op1=ALU.mult))
        nc.scalar.wait_ge(v.h, v.n)
        nc.scalar.wait_ge(t.h, t.n)
        a.inc(nc.scalar.activation(out=sxn[:], in_=sx[:], func=AF.Identity,
                                   scale=stv[:, 0:1], bias=stb[:, 0:1]))
        nc.tensor.wait_ge(a.h, a.n)
        evfree = [v.n, v.n]
        for g in range(4):
            bnk = g % 2
            nc.tensor.wait_ge(v.h, evfree[bnk])
            for j in range(8):
                kt = g * 8 + j
                t.inc(nc.tensor.matmul(ptr8[bnk][:, j * 128:(j + 1) * 128],
                                       sxn[:, kt * 128:(kt + 1) * 128],
                                       p.seye[:], is_transpose=True,
                                       start=(j == 0), stop=(j == 7)))
            nc.vector.wait_ge(t.h, t.n)
            v.inc(nc.vector.tensor_copy(
                out=xv[:, g * 8:(g + 1) * 8, i * 128:(i + 1) * 128],
                in_=ptr8[bnk][:]))
            evfree[bnk] = v.n


def emit_proj(p, sems, wsrc, chunks, xnfm, W, tks, wbufs, pq, sev, evac_fn,
              bias_fn=None, wrows=H):
    nc = p.nc
    qs, qg, a, t = sems["qs"], sems["qg"], sems["a"], sems["t"]
    st = sems["st"]
    nkt = wrows // 128
    ws = [Sem(p_sem_raw(p, "wsA")), Sem(p_sem_raw(p, "wsB"))]
    wfree = [t.n, t.n]
    for idx, (c0, tag) in enumerate(chunks):
        buf = idx % 2
        nc.gpsimd.wait_ge(t.h, wfree[buf])
        ws[buf].inc(nc.gpsimd.dma_start(
            out=wbufs[buf][:],
            in_=wsrc[:, c0:c0 + 128].rearrange("(kt p) o -> p kt o", p=128)), 16)
        nc.tensor.wait_ge(ws[buf].h, ws[buf].n)
        nc.tensor.wait_ge(a.h, a.n)
        for tki, (tk0, tw) in enumerate(tks):
            for kt in range(nkt):
                t.inc(nc.tensor.matmul(pq[tki][:, 0:tw],
                                       wbufs[buf][:, kt * 128:(kt + 1) * 128],
                                       xnfm[:, kt * W + tk0: kt * W + tk0 + tw],
                                       start=(kt == 0), stop=(kt == nkt - 1)))
        wfree[buf] = t.n
        nc.scalar.wait_ge(t.h, t.n)
        nc.scalar.wait_ge(st.h, st.n)
        for tki, (tk0, tw) in enumerate(tks):
            bias = bias_fn(tag) if bias_fn is not None else None
            if bias is not None:
                a.inc(nc.scalar.activation(out=sev[:, tki * 512: tki * 512 + tw],
                                           in_=pq[tki][:, 0:tw],
                                           func=AF.Identity, bias=bias, scale=1.0))
            else:
                a.inc(nc.scalar.copy(out=sev[:, tki * 512: tki * 512 + tw],
                                     in_=pq[tki][:, 0:tw]))
        nc.sync.wait_ge(a.h, a.n)
        for tki, (tk0, tw) in enumerate(tks):
            evac_fn(tag, tki, sev[:, tki * 512: tki * 512 + tw], tw)


def emit_attention(p, sems, nq_tiles, kv_heads, q_per_kv, qsrc, ksrc, vsrc, sctx,
                   sc2, SKV):
    nc = p.nc
    qs, a, v, t = sems["qs"], sems["a"], sems["v"], sems["t"]
    nkc = SKV // 512
    nkt = SKV // 128
    sq = p.sb("sq", [128, nq_tiles * 128], BF16, sc2)
    sk = p.sb("sk", [128, SKV], BF16, sc2)
    sv = p.sb("sv", [128, SKV], BF16, sc2)
    svt = p.sb("svt", [128, SKV], BF16, sc2)
    sprob = p.sb("sprob", [128, SKV], BF16, sc2)
    sprT = p.sb("sprT", [128, nkt * nq_tiles * 128], BF16, sc2)
    smax = p.sb("smax", [128, 1], F32, sc2)
    snegb = p.sb("snegb", [128, 1], F32, sc2)
    ssum = p.sb("ssum", [128, 1], F32, sc2)
    srec = p.sb("srec", [128, 1], F32, sc2)
    with nc.psum_tensor(p.u("psc"), [128, SKV], F32) as psc, \
         nc.psum_tensor(p.u("pt8a"), [128, 1024], BF16) as pt8a, \
         nc.psum_tensor(p.u("pt8b"), [128, 1024], BF16) as pt8b, \
         nc.psum_tensor(p.u("pc"), [128, 1024], F32) as pc:
        pt8 = [pt8a, pt8b]
        nc.sync.wait_ge(qs.h, qs.n)
        nc.sync.wait_ge(sems["qg"].h, sems["qg"].n)
        nc.sync.wait_ge(sems["qc"].h, sems["qc"].n)
        nc.sync.wait_ge(sems["st"].h, sems["st"].n)
        for kh in range(kv_heads):
            nc.sync.wait_ge(t.h, t.n)
            qs.inc(nc.sync.dma_start(out=sk[:], in_=ksrc[kh * 128:(kh + 1) * 128, :]), 16)
            qs.inc(nc.sync.dma_start(out=sv[:], in_=vsrc[kh * 128:(kh + 1) * 128, :]), 16)
            nc.tensor.wait_ge(qs.h, qs.n)
            evfree = [v.n, v.n]
            for g in range(nkt // 8):
                bnk = g % 2
                nc.tensor.wait_ge(v.h, evfree[bnk])
                for j in range(8):
                    kt = g * 8 + j
                    t.inc(nc.tensor.matmul(pt8[bnk][:, j * 128:(j + 1) * 128],
                                           sv[:, kt * 128:(kt + 1) * 128],
                                           p.seye[:], is_transpose=True,
                                           start=(j == 0), stop=(j == 7)))
                nc.vector.wait_ge(t.h, t.n)
                v.inc(nc.vector.tensor_copy(
                    out=svt[:, g * 1024:(g + 1) * 1024], in_=pt8[bnk][:]))
                evfree[bnk] = v.n
            for qq in range(q_per_kv):
                qh = kh * q_per_kv + qq
                nc.sync.wait_ge(t.h, t.n)
                qs.inc(nc.sync.dma_start(out=sq[:],
                                         in_=qsrc[qh * 128:(qh + 1) * 128, :]), 16)
                nc.tensor.wait_ge(qs.h, qs.n)
                for qt in range(nq_tiles):
                    nc.tensor.wait_ge(a.h, a.n)
                    for kc in range(nkc):
                        t.inc(nc.tensor.matmul(psc[:, kc * 512:(kc + 1) * 512],
                                               sq[:, qt * 128:(qt + 1) * 128],
                                               sk[:, kc * 512:(kc + 1) * 512],
                                               start=True, stop=True))
                    nc.vector.wait_ge(t.h, t.n)
                    v.inc(nc.vector.reduce_max(out=smax[:, 0:1], in_=psc[:],
                                               axis=AX.X))
                    nc.vector.wait_ge(v.h, v.n)
                    v.inc(nc.vector.tensor_scalar(out=snegb[:, 0:1], in0=smax[:, 0:1],
                                                  scalar1=-RS, scalar2=None,
                                                  op0=ALU.mult))
                    nc.scalar.wait_ge(v.h, v.n)
                    nc.scalar.wait_ge(t.h, t.n)
                    a.inc(nc.scalar.memzero(ssum[:]))
                    nc.scalar.wait_ge(a.h, a.n)
                    a.inc(nc.scalar.activation(out=sprob[:], in_=psc[:], func=AF.Exp,
                                               scale=RS, bias=snegb[:, 0:1],
                                               accum_out=ssum[:, 0:1]))
                    nc.vector.wait_ge(a.h, a.n)
                    v.inc(nc.vector.reciprocal(out=srec[:, 0:1], in_=ssum[:, 0:1]))
                    nc.scalar.wait_ge(v.h, v.n)
                    a.inc(nc.scalar.activation(out=sprob[:], in_=sprob[:],
                                               func=AF.Identity, scale=srec[:, 0:1]))
                    nc.scalar.wait_ge(a.h, a.n)
                    nc.tensor.wait_ge(a.h, a.n)
                    evfree = [v.n, v.n]
                    for g in range(nkt // 8):
                        bnk = g % 2
                        nc.tensor.wait_ge(v.h, evfree[bnk])
                        for j in range(8):
                            kt = g * 8 + j
                            t.inc(nc.tensor.matmul(
                                pt8[bnk][:, j * 128:(j + 1) * 128],
                                sprob[:, kt * 128:(kt + 1) * 128],
                                p.seye[:], is_transpose=True,
                                start=(j == 0), stop=(j == 7)))
                        nc.vector.wait_ge(t.h, t.n)
                        prv = sprT[:].rearrange("p (kt q) -> p kt q",
                                                q=nq_tiles * 128)
                        v.inc(nc.vector.tensor_copy(
                            out=prv[:, g * 8:(g + 1) * 8, qt * 128:(qt + 1) * 128],
                            in_=pt8[bnk][:]))
                        evfree[bnk] = v.n
                nc.tensor.wait_ge(v.h, v.n)
                nc.tensor.wait_ge(a.h, a.n)
                for tk in range(nq_tiles * 128 // 512):
                    for kt in range(nkt):
                        t.inc(nc.tensor.matmul(
                            pc[:, tk * 512:(tk + 1) * 512],
                            svt[:, kt * 128:(kt + 1) * 128],
                            sprT[:, kt * nq_tiles * 128 + tk * 512:
                                 kt * nq_tiles * 128 + (tk + 1) * 512],
                            start=(kt == 0), stop=(kt == nkt - 1)))
                nc.scalar.wait_ge(t.h, t.n)
                a.inc(nc.scalar.copy(out=sctx[:, qh * 1024:(qh + 1) * 1024],
                                     in_=pc[:]))


def emit_outproj(p, sems, wsrc, ctx_sb, nft, res_src, dst, bias_row, sones, sc2,
                 wT=False):
    nc = p.nc
    qs, qg, a, v, t = sems["qs"], sems["qg"], sems["a"], sems["v"], sems["t"]
    st = sems["st"]
    ws = [Sem(p_sem_raw(p, "wsA")), Sem(p_sem_raw(p, "wsB"))]
    swo0 = p.sb("swo0", [128, 512], BF16, sc2)
    swo1 = p.sb("swo1", [128, 512], BF16, sc2)
    swo = [swo0, swo1]
    sxr = p.sb("sxr", [128, 512], F32, sc2)
    sout = p.sb("sout", [128, 512], F32, sc2)
    with ExitStack() as pes:
        po = [pes.enter_context(nc.psum_tensor(p.u(f"po{i}"), [128, 512], F32))
              for i in range(8)]
        nc.gpsimd.wait_ge(qs.h, qs.n)
        nc.gpsimd.wait_ge(st.h, st.n)
        for dm in range(8):
            wfree = [t.n, t.n]
            for hh in range(nft):
                buf = hh % 2
                nc.gpsimd.wait_ge(t.h, wfree[buf])
                ws[buf].inc(nc.gpsimd.dma_start(
                    out=swo[buf][:],
                    in_=wsrc[hh * 128:(hh + 1) * 128, dm * 512:(dm + 1) * 512]), 16)
                nc.tensor.wait_ge(ws[buf].h, ws[buf].n)
                if hh == 0:
                    nc.tensor.wait_ge(v.h, v.n)
                last = (hh == nft - 1) and (bias_row is None)
                for tt in range(8):
                    t.inc(nc.tensor.matmul(po[tt][:],
                                           ctx_sb[:, hh * 1024 + tt * 128:
                                                  hh * 1024 + (tt + 1) * 128],
                                           swo[buf][:],
                                           start=(hh == 0), stop=last))
                wfree[buf] = t.n
            if bias_row is not None:
                for tt in range(8):
                    t.inc(nc.tensor.matmul(po[tt][:], sones[:],
                                           bias_row[:, dm * 512:(dm + 1) * 512],
                                           start=False, stop=True))
            for tt in range(8):
                nc.sync.wait_ge(v.h, v.n)
                qs.inc(nc.sync.dma_start(
                    out=sxr[:],
                    in_=res_src[tt * 128:(tt + 1) * 128, dm * 512:(dm + 1) * 512]), 16)
                nc.vector.wait_ge(qs.h, qs.n)
                nc.vector.wait_ge(t.h, t.n)
                nc.vector.wait_ge(st.h, st.n)
                v.inc(nc.vector.tensor_add(out=sout[:], in0=po[tt][:], in1=sxr[:]))
                nc.sync.wait_ge(v.h, v.n)
                st.inc(nc.sync.dma_start(
                    out=dst[tt * 128:(tt + 1) * 128, dm * 512:(dm + 1) * 512],
                    in_=sout[:]), 16)



def emit_block_l0attn(p, sems, xloc, xfull, qkvw, qkvb2, ow, ob, ones, eye, out,
                      debug=False, dbg=None):
    nc = p.nc
    qbuf = dbg["qbuf"] if debug else p.scratch(p.u("qbuf"), [H, TLOC], BF16)
    kbuf = dbg["kbuf"] if debug else p.scratch(p.u("kbuf"), [H, S], BF16)
    vbuf = dbg["vbuf"] if debug else p.scratch(p.u("vbuf"), [H, S], BF16)
    ctxd = dbg["ctxd"] if debug else None

    with block_scope(p):
        qs, qg, qc, a, v, t = (sems[k] for k in ["qs", "qg", "qc", "a", "v", "t"])

        sev = p.sb("sev", [128, 1024], BF16)
        sbias = p.sb("sbias", [128, 96], F32)
        sones = p.sb("sones", [1, 128], F32)
        sob = p.sb("sob", [1, H], F32)
        st1 = p.sb("st1", [128, 1], F32)
        st2 = p.sb("st2", [128, 1], F32)
        stm = p.sb("stm", [128, 1], F32)
        stv = p.sb("stv", [128, 1], F32)
        stb = p.sb("stb", [128, 1], F32)
        stats = (st1, st2, stm, stv, stb)
        sctx = p.sb("sctx", [128, 32 * 1024], BF16)
        p.seye = p.sb("seye", [128, 128], BF16)
        p.seye32 = p.sb("seye32", [128, 128], F32)

        qg.inc(nc.gpsimd.dma_start(out=p.seye[:], in_=eye), 16)
        qs.inc(nc.sync.dma_start(out=p.seye32[:], in_=eye), 16)

        with nc.psum_tensor("pb", [128, 128], F32) as pb, \
             nc.sbuf_tensor("sbq", [128, 128], F32) as sbq:
            qs.inc(nc.sync.dma_start(out=sbq[0:96, :], in_=qkvb2), 16)
            qs.inc(nc.sync.dma_start(out=sones[:], in_=ones), 16)
            qs.inc(nc.sync.dma_start(out=sob[:], in_=ob), 16)
            nc.vector.wait_ge(qs.h, qs.n)
            v.inc(nc.vector.memset(sbq[96:128, :], 0.0))
            nc.tensor.wait_ge(qs.h, qs.n)
            nc.tensor.wait_ge(v.h, v.n)
            nc.tensor.wait_ge(qs.h, qs.n)
            nc.tensor.wait_ge(qg.h, qg.n)
            t.inc(nc.tensor.transpose(pb[:], sbq[:], p.seye32[:]))
            nc.vector.wait_ge(t.h, t.n)
            v.inc(nc.vector.tensor_copy(out=sbias[:, 0:96], in_=pb[:, 0:96]))
            nc.tensor.wait_ge(v.h, v.n)

        barrier(p, sems)
        with ExitStack() as sc1:
            sx = p.sb("sx", [128, H], F32, sc1)
            sxn = p.sb("sxn", [128, H], BF16, sc1)
            xnfm = p.sb("xnfm", [128, 32 * 1024], BF16, sc1)
            wb0 = p.sb("wb0", [128, 32 * 128], BF16, sc1)
            wb1 = p.sb("wb1", [128, 32 * 128], BF16, sc1)
            with nc.psum_tensor("pt8a", [128, 1024], BF16) as pt8a, \
                 nc.psum_tensor("pt8b", [128, 1024], BF16) as pt8b, \
                 nc.psum_tensor("pq0", [128, 512], F32) as pq0, \
                 nc.psum_tensor("pq1", [128, 512], F32) as pq1:
                pq = [pq0, pq1]
                tks = [(0, 512), (512, 512)]

                def bias_of(oc):
                    return sbias[:, oc:oc + 1]

                def ev_q(tag, tki, sv_, tw):
                    sems["st"].inc(nc.sync.dma_start(
                        out=qbuf[tag * 128:(tag + 1) * 128,
                                 tki * 512:(tki + 1) * 512], in_=sv_), 16)

                emit_ln_span(p, sems, xloc, 0, 8, sx, sxn, stats, xnfm,
                             [pt8a, pt8b], 1024)
                nc.tensor.wait_ge(v.h, v.n)
                emit_proj(p, sems, qkvw, [(oc * 128, oc) for oc in range(32)],
                          xnfm, 1024, tks, [wb0, wb1], pq, sev, ev_q, bias_of)

                for sp in range(2):
                    def ev_kv(tag, tki, sv_, tw, sp=sp):
                        dst = kbuf if tag < 64 else vbuf
                        r = (tag - 32) if tag < 64 else (tag - 64)
                        sems["st"].inc(nc.sync.dma_start(
                            out=dst[r * 128:(r + 1) * 128,
                                    sp * 1024 + tki * 512:
                                    sp * 1024 + (tki + 1) * 512], in_=sv_), 16)
                    nc.sync.wait_ge(t.h, t.n)
                    emit_ln_span(p, sems, xfull, sp * 1024, 8, sx, sxn, stats,
                                 xnfm, [pt8a, pt8b], 1024)
                    nc.tensor.wait_ge(v.h, v.n)
                    emit_proj(p, sems, qkvw,
                              [(oc * 128, oc) for oc in range(32, 96)],
                              xnfm, 1024, tks, [wb0, wb1], pq, sev, ev_kv, bias_of)

        barrier(p, sems)
        with ExitStack() as sc2:
            emit_attention(p, sems, 8, 32, 1, qbuf, kbuf, vbuf, sctx, sc2, S)
        if debug:
            nc.sync.wait_ge(sems["a"].h, sems["a"].n)
            for hh in range(32):
                sems["qs"].inc(nc.sync.dma_start(
                    out=ctxd[hh * 128:(hh + 1) * 128, :],
                    in_=sctx[:, hh * 1024:(hh + 1) * 1024]), 16)

        barrier(p, sems)
        with ExitStack() as sc3:
            emit_outproj(p, sems, ow, sctx, 32, xloc, out, sob, sones, sc3)

        nc.sync.wait_ge(qs.h, qs.n)
        nc.sync.wait_ge(qg.h, qg.n)
        nc.sync.wait_ge(sems["st"].h, sems["st"].n)


def build_l0_attn(debug=False):
    p = Prog("l0attn")
    nc = p.nc
    xloc = p.inp("xloc", [TLOC, H], F32)
    xfull = p.inp("xfull", [S, H], F32)
    qkvw = p.inp("qkvw", [H, 3 * H], F32)
    qkvb2 = p.inp("qkvb2", [96, 128], F32)
    ow = p.inp("ow", [H, H], F32)
    ob = p.inp("ob", [1, H], F32)
    ones = p.inp("ones", [1, 128], F32)
    eye = p.inp("eye", [128, 128], F32)
    out = p.outp("out", [TLOC, H], F32)
    dbg = None
    if debug:
        dbg = dict(qbuf=p.outp("qbuf", [H, TLOC], BF16),
                   kbuf=p.outp("kbuf", [H, S], BF16),
                   vbuf=p.outp("vbuf", [H, S], BF16),
                   ctxd=p.outp("ctxd", [H, TLOC], BF16))
    with p.es:
        block = p.es.enter_context(nc.Block())
        sems = mk_sems(p, nc, "a")
        emit_block_l0attn(p, sems, xloc, xfull, qkvw, qkvb2, ow, ob, ones, eye,
                          out, debug, dbg)
    return p



_ONES = np.ones((1, 128), np.float32)
_EYE = np.eye(128, dtype=np.float32)


def _run_l0_attn(x, qkvw, qkvb, ow, ob, debug=False):
    p = build_l0_attn(debug)
    qkvb2 = np.ascontiguousarray(qkvb.reshape(96, 128))
    obr = np.ascontiguousarray(ob.reshape(1, H))
    in_maps = []
    for c in range(NCORE):
        b, h = c // 2, c % 2
        in_maps.append(dict(
            xloc=np.ascontiguousarray(x[b, h * TLOC:(h + 1) * TLOC]),
            xfull=np.ascontiguousarray(x[b]),
            qkvw=qkvw, qkvb2=qkvb2, ow=ow, ob=obr, ones=_ONES, eye=_EYE))
    res = run_bass_kernel_spmd(p.nc, in_maps, core_ids=list(range(NCORE)))
    out = np.empty((B, S, H), np.float32)
    for c in range(NCORE):
        b, h = c // 2, c % 2
        out[b, h * TLOC:(h + 1) * TLOC] = res.results[c]["out"]
    if debug:
        return out, res
    return out


def kernel(L0_attn_x, L0_attn_attn_qkvw, L0_attn_attn_qkvb, L0_attn_attn_ow,
           L0_attn_attn_ob, L0_ffn_qweight_gu, L0_ffn_qzeros_gu, L0_ffn_scales_gu,
           L0_ffn_qweight_dn, L0_ffn_qzeros_dn, L0_ffn_scales_dn, L1_attn_Wq,
           L1_attn_Wk, L1_attn_Wv, L1_attn_Wo, L1_ffn_qweight_gu, L1_ffn_qzeros_gu,
           L1_ffn_scales_gu, L1_ffn_g_idx_gu, L1_ffn_qweight_down,
           L1_ffn_qzeros_down, L1_ffn_scales_down, L1_ffn_g_idx_down):
    return kernel_fused(dict(
        L0_attn_x=L0_attn_x, L0_attn_attn_qkvw=L0_attn_attn_qkvw,
        L0_attn_attn_qkvb=L0_attn_attn_qkvb, L0_attn_attn_ow=L0_attn_attn_ow,
        L0_attn_attn_ob=L0_attn_attn_ob, L0_ffn_qweight_gu=L0_ffn_qweight_gu,
        L0_ffn_qzeros_gu=L0_ffn_qzeros_gu, L0_ffn_scales_gu=L0_ffn_scales_gu,
        L0_ffn_qweight_dn=L0_ffn_qweight_dn, L0_ffn_qzeros_dn=L0_ffn_qzeros_dn,
        L0_ffn_scales_dn=L0_ffn_scales_dn, L1_attn_Wq=L1_attn_Wq,
        L1_attn_Wk=L1_attn_Wk, L1_attn_Wv=L1_attn_Wv, L1_attn_Wo=L1_attn_Wo,
        L1_ffn_qweight_gu=L1_ffn_qweight_gu, L1_ffn_qzeros_gu=L1_ffn_qzeros_gu,
        L1_ffn_scales_gu=L1_ffn_scales_gu, L1_ffn_g_idx_gu=L1_ffn_g_idx_gu,
        L1_ffn_qweight_down=L1_ffn_qweight_down,
        L1_ffn_qzeros_down=L1_ffn_qzeros_down,
        L1_ffn_scales_down=L1_ffn_scales_down,
        L1_ffn_g_idx_down=L1_ffn_g_idx_down))



def emit_unpack_n(p, sems, packed_words, nib_u8, w):
    nc = p.nc
    v = sems["v"]
    pk8 = packed_words.bitcast(U8)
    nc.vector.tensor_scalar(out=nib_u8[:, 0:w:2], in0=pk8, scalar1=15,
                            scalar2=None, op0=ALU.bitwise_and)
    v.inc(nc.vector.tensor_scalar(out=nib_u8[:, 1:w:2], in0=pk8, scalar1=4,
                                  scalar2=None, op0=ALU.logical_shift_right))


def emit_block_l0ffn(p, sems, xin, qw_gu, qz_gu, sc_gu, qw_dn, qz_dn, sc_dn,
                     eye, out):
    nc = p.nc
    hgbuf = p.scratch(p.u("hgbuf"), [INTER, TLOC], BF16)
    hbuf = p.scratch(p.u("hbuf"), [INTER, TLOC], BF16)

    with block_scope(p):
        qs, qg, qc, a, v, t = (sems[k] for k in ["qs", "qg", "qc", "a", "v", "t"])

        p.seye = p.sb("seye", [128, 128], BF16)
        st1 = p.sb("st1", [128, 1], F32)
        st2 = p.sb("st2", [128, 1], F32)
        stm = p.sb("stm", [128, 1], F32)
        stv = p.sb("stv", [128, 1], F32)
        stb = p.sb("stb", [128, 1], F32)
        stats = (st1, st2, stm, stv, stb)
        xnfm = p.sb("xnfm", [128, 32 * 1024], BF16)
        pk_gu_s = p.sb("pk_gu_s", [32, 2752], I32)
        pk_dn_s = p.sb("pk_dn_s", [86, 512], I32)
        snz = p.sb("snz", [128, 512], U8)
        ssc2 = p.sb("ssc2", [128, 512], F32)
        sm2 = p.sb("sm2", [128, 512], BF16)
        sxg = p.sb("sxg", [32, TLOC], BF16)
        sxh = p.sb("sxh", [86, TLOC], BF16)
        sioc = p.sb("sioc", [128, 86], F32)
        sonehot = p.sb("sonehot", [128, 86], BF16)

        qg.inc(nc.gpsimd.dma_start(out=p.seye[:], in_=eye), 16)
        sems["g"].inc(nc.gpsimd.iota(out=sioc[:], pattern=[[1, 86]], base=0,
                                     channel_multiplier=0,
                                     allow_small_or_imprecise_dtypes=True))

        qs.inc(nc.sync.dma_start(out=pk_gu_s[:], in_=qz_gu), 16)
        qs.inc(nc.sync.dma_start(out=pk_dn_s[:], in_=qz_dn), 16)

        def emit_m2_slice(pk_sb, nrow, sc_src, c0, w):
            nc.scalar.wait_ge(v.h, v.n)
            qc.inc(nc.scalar.dma_start(out=ssc2[0:nrow, 0:w],
                                       in_=sc_src[:, c0:c0 + w]), 16)
            nc.vector.wait_ge(qc.h, qc.n)
            nc.vector.wait_ge(t.h, t.n)
            pk8 = pk_sb[0:nrow, c0 // 8:(c0 + w) // 8].bitcast(U8)
            nc.vector.tensor_scalar(out=snz[0:nrow, 0:w:2], in0=pk8, scalar1=15,
                                    scalar2=None, op0=ALU.bitwise_and)
            v.inc(nc.vector.tensor_scalar(out=snz[0:nrow, 1:w:2], in0=pk8,
                                          scalar1=4, scalar2=None,
                                          op0=ALU.logical_shift_right))
            nc.vector.wait_ge(v.h, v.n)
            v.inc(nc.vector.scalar_tensor_tensor(out=ssc2[0:nrow, 0:w],
                                                 in0=snz[0:nrow, 0:w],
                                                 scalar=8.0, in1=ssc2[0:nrow, 0:w],
                                                 op0=ALU.subtract, op1=ALU.mult))
            nc.vector.wait_ge(v.h, v.n)
            v.inc(nc.vector.tensor_scalar(out=sm2[0:nrow, 0:w],
                                          in0=ssc2[0:nrow, 0:w], scalar1=-1.0,
                                          scalar2=None, op0=ALU.mult))

        barrier(p, sems)
        with ExitStack() as sc1:
            sx = p.sb("sx", [128, H], F32, sc1)
            sxn = p.sb("sxn", [128, H], BF16, sc1)
            soh32 = p.sb("soh32", [128, 32], BF16, sc1)
            with nc.psum_tensor(p.u("pt8a"), [128, 1024], BF16) as pt8a, \
                 nc.psum_tensor(p.u("pt8b"), [128, 1024], BF16) as pt8b, \
                 nc.psum_tensor(p.u("pxg"), [32, 1024], F32) as pxg:
                emit_ln_span(p, sems, xin, 0, 8, sx, sxn, stats, xnfm,
                             [pt8a, pt8b], 1024)
                nc.vector.wait_ge(sems["g"].h, sems["g"].n)
                for kt in range(32):
                    nc.vector.wait_ge(t.h, t.n)
                    v.inc(nc.vector.tensor_scalar(out=soh32[:, 0:32],
                                                  in0=sioc[:, 0:32],
                                                  scalar1=float(kt), scalar2=None,
                                                  op0=ALU.is_equal))
                    nc.tensor.wait_ge(v.h, v.n)
                    for tk in range(2):
                        t.inc(nc.tensor.matmul(pxg[:, tk * 512:(tk + 1) * 512],
                                               soh32[:, 0:32],
                                               xnfm[:, kt * 1024 + tk * 512:
                                                    kt * 1024 + (tk + 1) * 512],
                                               start=(kt == 0), stop=(kt == 31)))
                nc.scalar.wait_ge(t.h, t.n)
                a.inc(nc.scalar.copy(out=sxg[:], in_=pxg[:]))
                nc.tensor.wait_ge(a.h, a.n)

        barrier(p, sems)
        with ExitStack() as sc2:
            spk = p.sb("spk", [128, 32 * 64], I32, sc2)
            spl0 = p.sb("spl0", [128, 8 * 512], F32, sc2)
            spl1 = p.sb("spl1", [128, 8 * 512], F32, sc2)
            snib = p.sb("snib", [128, 512], U8, sc2)
            sw = p.sb("sw", [128, 512], BF16, sc2)
            shg = p.sb("shg", [128, 512], BF16, sc2)
            spl = [spl0, spl1]
            pls = [Sem(p_sem_raw(p, "plsA")), Sem(p_sem_raw(p, "plsB"))]
            plfree = [0, 0]

            def gu_slab(j0, w, is_gate):
                nj = w // 128
                with ExitStack() as pes:
                    pg = [[p.psum(f"pg{jc}{tk}", [128, 512], F32, pes)
                           for tk in range(2)] for jc in range(nj)]
                    nc.sync.wait_ge(v.h, v.n)
                    qs.inc(nc.sync.dma_start(
                        out=spk[:, 0:32 * (w // 8)],
                        in_=qw_gu[:, j0 // 8:(j0 + w) // 8]
                        .rearrange("(kt p) c -> p kt c", p=128)), 16)
                    for ktg in range(4):
                        bnk = ktg % 2
                        nc.scalar.wait_ge(v.h, plfree[bnk])
                        pls[bnk].inc(nc.scalar.dma_start(
                            out=spl[bnk][:, 0:8 * w],
                            in_=sc_gu[ktg * 8:(ktg + 1) * 8, j0:j0 + w]
                            .unsqueeze(0).broadcast_to([128, 8, w])), 16)
                        for kj in range(8):
                            kt = ktg * 8 + kj
                            nc.vector.wait_ge(qs.h, qs.n)
                            nc.vector.wait_ge(pls[bnk].h, pls[bnk].n)
                            nc.vector.wait_ge(t.h, t.n)
                            emit_unpack_n(p, sems,
                                          spk[:, kt * (w // 8):(kt + 1) * (w // 8)],
                                          snib, w)
                            nc.vector.wait_ge(v.h, v.n)
                            v.inc(nc.vector.scalar_tensor_tensor(
                                out=sw[:, 0:w], in0=snib[:, 0:w], scalar=8.0,
                                in1=spl[bnk][:, kj * w:(kj + 1) * w],
                                op0=ALU.subtract, op1=ALU.mult))
                            nc.tensor.wait_ge(v.h, v.n)
                            for jc in range(nj):
                                for tk in range(2):
                                    t.inc(nc.tensor.matmul(
                                        pg[jc][tk][:],
                                        sw[:, jc * 128:(jc + 1) * 128],
                                        xnfm[:, kt * 1024 + tk * 512:
                                             kt * 1024 + (tk + 1) * 512],
                                        start=(kt == 0), stop=False))
                            if kj == 7:
                                plfree[bnk] = v.n
                    emit_m2_slice(pk_gu_s, 32, sc_gu, j0, w)
                    nc.tensor.wait_ge(v.h, v.n)
                    for jc in range(nj):
                        for tk in range(2):
                            t.inc(nc.tensor.matmul(
                                pg[jc][tk][:],
                                sm2[0:32, jc * 128:(jc + 1) * 128],
                                sxg[:, tk * 512:(tk + 1) * 512],
                                start=False, stop=True))
                    for jc in range(nj):
                        jj = j0 + jc * 128
                        for tk in range(2):
                            if is_gate:
                                nc.scalar.wait_ge(t.h, t.n)
                                nc.scalar.wait_ge(qs.h, qs.n)
                                a.inc(nc.scalar.activation(out=shg[:],
                                                           in_=pg[jc][tk][:],
                                                           func=AF.Silu))
                                nc.sync.wait_ge(a.h, a.n)
                                qs.inc(nc.sync.dma_start(
                                    out=hgbuf[jj:jj + 128,
                                              tk * 512:(tk + 1) * 512],
                                    in_=shg[:]), 16)
                            else:
                                jp = jj - INTER
                                nc.scalar.wait_ge(qs.h, qs.n)
                                qc.inc(nc.scalar.dma_start(
                                    out=shg[:],
                                    in_=hgbuf[jp:jp + 128,
                                              tk * 512:(tk + 1) * 512]), 16)
                                nc.vector.wait_ge(qc.h, qc.n)
                                nc.vector.wait_ge(t.h, t.n)
                                nc.vector.wait_ge(qs.h, qs.n)
                                v.inc(nc.vector.tensor_tensor(
                                    out=sw[:, 0:512], in0=pg[jc][tk][:],
                                    in1=shg[:], op=ALU.mult))
                                nc.sync.wait_ge(v.h, v.n)
                                qs.inc(nc.sync.dma_start(
                                    out=hbuf[jp:jp + 128,
                                             tk * 512:(tk + 1) * 512],
                                    in_=sw[:, 0:512]), 16)
                    nc.tensor.wait_ge(a.h, a.n)
                    nc.tensor.wait_ge(v.h, v.n)

            slabs = [(i * 512, 512) for i in range(21)] + [(10752, 256)]
            for j0, w in slabs:
                gu_slab(j0, w, True)
            for j0, w in slabs:
                gu_slab(INTER + j0, w, False)

        barrier(p, sems)
        with ExitStack() as sc3:
            sht = p.sb("sht", [128, TLOC], BF16, sc3)
            with nc.psum_tensor(p.u("pxh"), [86, 1024], F32) as pxh:
                nc.sync.wait_ge(qs.h, qs.n)
                nc.sync.wait_ge(sems["st"].h, sems["st"].n)
                for g in range(86):
                    nc.sync.wait_ge(t.h, t.n)
                    qs.inc(nc.sync.dma_start(out=sht[:],
                                             in_=hbuf[g * 128:(g + 1) * 128, :]), 16)
                    nc.vector.wait_ge(t.h, t.n)
                    nc.vector.wait_ge(sems["g"].h, sems["g"].n)
                    v.inc(nc.vector.tensor_scalar(out=sonehot[:], in0=sioc[:],
                                                  scalar1=float(g), scalar2=None,
                                                  op0=ALU.is_equal))
                    nc.tensor.wait_ge(v.h, v.n)
                    nc.tensor.wait_ge(qs.h, qs.n)
                    for tk in range(2):
                        t.inc(nc.tensor.matmul(pxh[:, tk * 512:(tk + 1) * 512],
                                               sonehot[:],
                                               sht[:, tk * 512:(tk + 1) * 512],
                                               start=(g == 0), stop=(g == 85)))
                nc.scalar.wait_ge(t.h, t.n)
                a.inc(nc.scalar.copy(out=sxh[:], in_=pxh[:]))
                nc.tensor.wait_ge(a.h, a.n)

        barrier(p, sems)
        with ExitStack() as sc4:
            spkd = p.sb("spkd", [128, 64], I32, sc4)
            spld = p.sb("spld", [128, 512], F32, sc4)
            snibd = p.sb("snibd", [128, 512], U8, sc4)
            swd0 = p.sb("swd0", [128, 512], BF16, sc4)
            swd1 = p.sb("swd1", [128, 512], BF16, sc4)
            swd = [swd0, swd1]
            sht0 = p.sb("sht0", [128, TLOC], BF16, sc4)
            sht1 = p.sb("sht1", [128, TLOC], BF16, sc4)
            shts = [sht0, sht1]
            sxr = p.sb("sxr", [128, 512], F32, sc4)
            sout = p.sb("sout", [128, 512], F32, sc4)
            hs = [Sem(p_sem_raw(p, "hsA")), Sem(p_sem_raw(p, "hsB"))]
            pks = Sem(p_sem_raw(p, "pks"))
            pls2 = Sem(p_sem_raw(p, "pls2"))
            with ExitStack() as pes:
                pd = [p.psum(f"pd{i}", [128, 512], F32, pes) for i in range(8)]
                for dm in range(8):
                    hfree = [t.n, t.n]
                    wfree = t.n
                    for kt in range(86):
                        buf = kt % 2
                        nc.sync.wait_ge(t.h, hfree[buf])
                        hs[buf].inc(nc.sync.dma_start(
                            out=shts[buf][:],
                            in_=hbuf[kt * 128:(kt + 1) * 128, :]), 16)
                        nc.scalar.wait_ge(v.h, v.n)
                        pks.inc(nc.scalar.dma_start(
                            out=spkd[:],
                            in_=qw_dn[kt * 128:(kt + 1) * 128,
                                      dm * 64:(dm + 1) * 64]), 16)
                        pls2.inc(nc.scalar.dma_start(
                            out=spld[:],
                            in_=sc_dn[kt:kt + 1, dm * 512:(dm + 1) * 512]
                            .partition_broadcast(128)), 16)
                        nc.vector.wait_ge(pks.h, pks.n)
                        nc.vector.wait_ge(pls2.h, pls2.n)
                        nc.vector.wait_ge(t.h, wfree)
                        emit_unpack_n(p, sems, spkd[:], snibd, 512)
                        nc.vector.wait_ge(v.h, v.n)
                        v.inc(nc.vector.scalar_tensor_tensor(
                            out=swd[buf][:], in0=snibd[:], scalar=8.0,
                            in1=spld[:], op0=ALU.subtract, op1=ALU.mult))
                        nc.tensor.wait_ge(v.h, v.n)
                        nc.tensor.wait_ge(hs[buf].h, hs[buf].n)
                        if kt == 0:
                            nc.tensor.wait_ge(a.h, a.n)
                            nc.tensor.wait_ge(qg.h, qg.n)
                        for tt in range(8):
                            t.inc(nc.tensor.matmul(
                                pd[tt][:],
                                shts[buf][:, tt * 128:(tt + 1) * 128],
                                swd[buf][:], start=(kt == 0), stop=False))
                        hfree[buf] = t.n
                        wfree = t.n
                    emit_m2_slice(pk_dn_s, 86, sc_dn, dm * 512, 512)
                    nc.tensor.wait_ge(v.h, v.n)
                    for tt in range(8):
                        t.inc(nc.tensor.matmul(
                            pd[tt][:], sxh[:, tt * 128:(tt + 1) * 128],
                            sm2[0:86, 0:512],
                            start=False, stop=True))
                    for tt in range(8):
                        nc.gpsimd.wait_ge(v.h, v.n)
                        qg.inc(nc.gpsimd.dma_start(
                            out=sxr[:],
                            in_=xin[tt * 128:(tt + 1) * 128,
                                    dm * 512:(dm + 1) * 512]), 16)
                        nc.vector.wait_ge(qg.h, qg.n)
                        nc.vector.wait_ge(t.h, t.n)
                        v.inc(nc.vector.tensor_add(out=sout[:], in0=pd[tt][:],
                                                   in1=sxr[:]))
                        nc.gpsimd.wait_ge(v.h, v.n)
                        qg.inc(nc.gpsimd.dma_start(
                            out=out[tt * 128:(tt + 1) * 128,
                                    dm * 512:(dm + 1) * 512], in_=sout[:]), 16)
                        nc.vector.wait_ge(qg.h, qg.n)

        nc.sync.wait_ge(qs.h, qs.n)
        nc.sync.wait_ge(qg.h, qg.n)
        nc.sync.wait_ge(qc.h, qc.n)
        nc.sync.wait_ge(sems["st"].h, sems["st"].n)


def build_l0_ffn():
    p = Prog("l0ffn")
    nc = p.nc
    xin = p.inp("xin", [TLOC, H], F32)
    qw_gu = p.inp("qw_gu", [H, 2752], I32)
    qz_gu = p.inp("qz_gu", [32, 2752], I32)
    sc_gu = p.inp("sc_gu", [32, 2 * INTER], F32)
    qw_dn = p.inp("qw_dn", [INTER, 512], I32)
    qz_dn = p.inp("qz_dn", [86, 512], I32)
    sc_dn = p.inp("sc_dn", [86, H], F32)
    eye = p.inp("eye", [128, 128], F32)
    out = p.outp("out", [TLOC, H], F32)
    with p.es:
        block = p.es.enter_context(nc.Block())
        sems = mk_sems(p, nc, "b")
        emit_block_l0ffn(p, sems, xin, qw_gu, qz_gu, sc_gu, qw_dn, qz_dn, sc_dn,
                         eye, out)
    return p


def _run_l0_ffn(x1, qw_gu, qz_gu, sc_gu, qw_dn, qz_dn, sc_dn):
    p = build_l0_ffn()
    in_maps = []
    for c in range(NCORE):
        b, h = c // 2, c % 2
        in_maps.append(dict(
            xin=np.ascontiguousarray(x1[b, h * TLOC:(h + 1) * TLOC]),
            qw_gu=qw_gu, qz_gu=qz_gu, sc_gu=sc_gu,
            qw_dn=qw_dn, qz_dn=qz_dn, sc_dn=sc_dn, eye=_EYE))
    res = run_bass_kernel_spmd(p.nc, in_maps, core_ids=list(range(NCORE)))
    out = np.empty((B, S, H), np.float32)
    for c in range(NCORE):
        b, h = c // 2, c % 2
        out[b, h * TLOC:(h + 1) * TLOC] = res.results[c]["out"]
    return out



ROPE_C = 12582912.0


def emit_block_l1attn(p, sems, xloc, xfull, wq, wk, wv, wo, eye, invf, rotm,
                      posoff, out, span_off=None, debug=False, dbg=None):
    nc = p.nc
    qbuf = dbg["qbuf"] if debug else p.scratch(p.u("qbuf"), [H, TLOC], BF16)
    kbuf = dbg["kbuf"] if debug else p.scratch(p.u("kbuf"), [1024, S], BF16)
    vbuf = dbg["vbuf"] if debug else p.scratch(p.u("vbuf"), [1024, S], BF16)
    wtbuf = dbg["wtbuf"] if debug else p.scratch(p.u("wtbuf"), [H, 10240], BF16)
    ctxd = dbg["ctxd"] if debug else None
    tabd = dbg["tabd"] if debug else None

    with block_scope(p):
        qs, qg, qc, a, v, t = (sems[k] for k in ["qs", "qg", "qc", "a", "v", "t"])

        p.seye = p.sb("seye", [128, 128], BF16)
        srotm = p.sb("srotm", [128, 128], BF16)
        sinvf = p.sb("sinvf", [128, 1], F32)
        soff = p.sb("soff", [128, 1], F32)
        st1 = p.sb("st1", [128, 1], F32)
        st2 = p.sb("st2", [128, 1], F32)
        stm = p.sb("stm", [128, 1], F32)
        stv = p.sb("stv", [128, 1], F32)
        stb = p.sb("stb", [128, 1], F32)
        stats = (st1, st2, stm, stv, stb)
        scos = p.sb("scos", [128, 1024], F32)
        ssin = p.sb("ssin", [128, 1024], F32)
        sang = p.sb("sang", [128, 1024], F32)
        srnd = p.sb("srnd", [128, 1024], F32)
        sctx = p.sb("sctx", [128, 32 * 1024], BF16)

        qg.inc(nc.gpsimd.dma_start(out=p.seye[:], in_=eye), 16)
        qg.inc(nc.gpsimd.dma_start(out=srotm[:], in_=rotm), 16)
        qs.inc(nc.sync.dma_start(out=sinvf[:], in_=invf), 16)
        qs.inc(nc.sync.dma_start(out=soff[:], in_=posoff.partition_broadcast(128)), 16)

        def rope_tables(off_ap, off_imm):
            def vstep(i):
                v.inc(i)
                nc.vector.wait_ge(v.h, v.n)
            nc.gpsimd.wait_ge(v.h, v.n)
            sems["g"].inc(nc.gpsimd.iota(out=sang[:], pattern=[[1, 1024]], base=0,
                                         channel_multiplier=0,
                                         allow_small_or_imprecise_dtypes=True))
            nc.vector.wait_ge(sems["g"].h, sems["g"].n)
            nc.vector.wait_ge(a.h, a.n)
            if off_ap is not None:
                vstep(nc.vector.tensor_scalar(out=sang[:], in0=sang[:],
                                              scalar1=off_ap, scalar2=None,
                                              op0=ALU.add))
            elif off_imm != 0.0:
                vstep(nc.vector.tensor_scalar(out=sang[:], in0=sang[:],
                                              scalar1=float(off_imm), scalar2=None,
                                              op0=ALU.add))
            vstep(nc.vector.tensor_scalar(out=sang[:], in0=sang[:],
                                          scalar1=sinvf[:, 0:1],
                                          scalar2=1.0 / (2 * np.pi),
                                          op0=ALU.mult, op1=ALU.mult))
            vstep(nc.vector.tensor_scalar(out=srnd[:], in0=sang[:], scalar1=ROPE_C,
                                          scalar2=ROPE_C, op0=ALU.add,
                                          op1=ALU.subtract))
            vstep(nc.vector.tensor_tensor(out=srnd[:], in0=sang[:], in1=srnd[:],
                                          op=ALU.subtract))
            vstep(nc.vector.tensor_scalar(out=srnd[:], in0=srnd[:],
                                          scalar1=2 * np.pi, scalar2=None,
                                          op0=ALU.mult))
            nc.scalar.wait_ge(v.h, v.n)
            a.inc(nc.scalar.activation(out=ssin[:], in_=srnd[:], func=AF.Sin))
            nc.vector.wait_ge(a.h, a.n)
            vstep(nc.vector.tensor_scalar(out=sang[:], in0=sang[:], scalar1=0.25,
                                          scalar2=None, op0=ALU.add))
            vstep(nc.vector.tensor_scalar(out=srnd[:], in0=sang[:], scalar1=ROPE_C,
                                          scalar2=ROPE_C, op0=ALU.add,
                                          op1=ALU.subtract))
            vstep(nc.vector.tensor_tensor(out=srnd[:], in0=sang[:], in1=srnd[:],
                                          op=ALU.subtract))
            vstep(nc.vector.tensor_scalar(out=srnd[:], in0=srnd[:],
                                          scalar1=2 * np.pi, scalar2=None,
                                          op0=ALU.mult))
            nc.scalar.wait_ge(v.h, v.n)
            a.inc(nc.scalar.activation(out=scos[:], in_=srnd[:], func=AF.Sin))
            nc.vector.wait_ge(a.h, a.n)

        barrier(p, sems)
        with ExitStack() as sc0:
            swr = p.sb("swr", [128, H], BF16, sc0)
            swt = p.sb("swt", [128, 32 * 128], BF16, sc0)
            with nc.psum_tensor(p.u("pt8a"), [128, 1024], BF16) as pt8a, \
                 nc.psum_tensor(p.u("pt8b"), [128, 1024], BF16) as pt8b:
                pt8 = [pt8a, pt8b]
                srcs = [(wq, 32, 0), (wk, 8, 4096), (wv, 8, 5120), (wo, 32, 6144)]
                for wsrc, noc, colbase in srcs:
                    for oc in range(noc):
                        nc.gpsimd.wait_ge(t.h, t.n)
                        qg.inc(nc.gpsimd.dma_start(
                            out=swr[:], in_=wsrc[oc * 128:(oc + 1) * 128, :]), 16)
                        nc.tensor.wait_ge(qg.h, qg.n)
                        evfree = [v.n, v.n]
                        stored = sems["st"].n
                        for g in range(4):
                            bnk = g % 2
                            nc.tensor.wait_ge(v.h, evfree[bnk])
                            for j in range(8):
                                kt = g * 8 + j
                                t.inc(nc.tensor.matmul(
                                    pt8[bnk][:, j * 128:(j + 1) * 128],
                                    swr[:, kt * 128:(kt + 1) * 128],
                                    p.seye[:], is_transpose=True,
                                    start=(j == 0), stop=(j == 7)))
                            nc.vector.wait_ge(t.h, t.n)
                            nc.vector.wait_ge(sems["st"].h, stored)
                            v.inc(nc.vector.tensor_copy(
                                out=swt[:, g * 1024:(g + 1) * 1024],
                                in_=pt8[bnk][:]))
                            evfree[bnk] = v.n
                        nc.sync.wait_ge(v.h, v.n)
                        sems["st"].inc(nc.sync.dma_start(
                            out=wtbuf.rearrange("(kt pp) o -> pp kt o", pp=128)
                            [:, :, colbase + oc * 128:colbase + (oc + 1) * 128],
                            in_=swt[:].rearrange("pp (kt o) -> pp kt o", o=128)), 16)

        barrier(p, sems)
        with ExitStack() as sc1:
            sx = p.sb("sx", [128, H], F32, sc1)
            sxn = p.sb("sxn", [128, H], BF16, sc1)
            xnfm = p.sb("xnfm", [128, 32 * 1024], BF16, sc1)
            wb0 = p.sb("wb0", [128, 32 * 128], BF16, sc1)
            wb1 = p.sb("wb1", [128, 32 * 128], BF16, sc1)
            sev = p.sb("sev", [128, 1024], BF16, sc1)
            sev2 = p.sb("sev2", [128, 512], BF16, sc1)
            sro1 = p.sb("sro1", [128, 512], F32, sc1)
            sro2 = p.sb("sro2", [128, 512], F32, sc1)
            with nc.psum_tensor(p.u("pt8a"), [128, 1024], BF16) as pt8a, \
                 nc.psum_tensor(p.u("pt8b"), [128, 1024], BF16) as pt8b, \
                 nc.psum_tensor(p.u("pq0"), [128, 512], F32) as pq0, \
                 nc.psum_tensor(p.u("pq1"), [128, 512], F32) as pq1, \
                 nc.psum_tensor(p.u("prot"), [128, 512], F32) as prot:
                pq = [pq0, pq1]
                tks = [(0, 512), (512, 512)]

                def rope_store(dst, row0, col0):
                    def f(tag, tki, sev_sl, tw):
                        nc.tensor.wait_ge(a.h, a.n)
                        nc.tensor.wait_ge(v.h, v.n)
                        t.inc(nc.tensor.matmul(prot[:], srotm[:], sev_sl,
                                               start=True, stop=True))
                        nc.vector.wait_ge(t.h, t.n)
                        v.inc(nc.vector.tensor_tensor(
                            out=sro1[:], in0=sev_sl, in1=scos[:, tki * 512:(tki + 1) * 512],
                            op=ALU.mult))
                        v.inc(nc.vector.tensor_tensor(
                            out=sro2[:], in0=prot[:], in1=ssin[:, tki * 512:(tki + 1) * 512],
                            op=ALU.mult))
                        nc.vector.wait_ge(v.h, v.n)
                        nc.vector.wait_ge(sems["st"].h, sems["st"].n)
                        v.inc(nc.vector.tensor_add(out=sev2[:], in0=sro1[:],
                                                   in1=sro2[:]))
                        nc.sync.wait_ge(v.h, v.n)
                        sems["st"].inc(nc.sync.dma_start(
                            out=dst[row0 + tag * 128:row0 + (tag + 1) * 128,
                                    col0 + tki * 512:col0 + (tki + 1) * 512],
                            in_=sev2[:]), 16)
                    return f

                def plain_store(dst, row0, col0):
                    def f(tag, tki, sev_sl, tw):
                        sems["st"].inc(nc.sync.dma_start(
                            out=dst[row0 + tag * 128:row0 + (tag + 1) * 128,
                                    col0 + tki * 512:col0 + (tki + 1) * 512],
                            in_=sev_sl), 16)
                    return f

                wt_ap = wtbuf

                nc.gpsimd.wait_ge(qs.h, qs.n)
                nc.gpsimd.wait_ge(sems["st"].h, sems["st"].n)
                emit_ln_span(p, sems, xloc, 0, 8, sx, sxn, stats, xnfm,
                             [pt8a, pt8b], 1024)
                rope_tables(soff[:, 0:1], 0.0)
                if debug:
                    nc.sync.wait_ge(a.h, a.n)
                    qs.inc(nc.sync.dma_start(out=tabd[:, 0:1024], in_=scos[:]), 16)
                    qs.inc(nc.sync.dma_start(out=tabd[:, 1024:2048], in_=ssin[:]), 16)
                nc.tensor.wait_ge(v.h, v.n)
                emit_proj(p, sems, wt_ap, [(oc * 128, oc) for oc in range(32)],
                          xnfm, 1024, tks, [wb0, wb1], pq, sev,
                          rope_store(qbuf, 0, 0), None)

                for sp in range(2):
                    nc.sync.wait_ge(t.h, t.n)
                    emit_ln_span(p, sems, xfull, sp * 1024, 8, sx, sxn, stats,
                                 xnfm, [pt8a, pt8b], 1024)
                    if span_off is None:
                        rope_tables(None, float(sp * 1024))
                    else:
                        rope_tables(span_off(sp), 0.0)
                    nc.tensor.wait_ge(v.h, v.n)
                    emit_proj(p, sems, wt_ap,
                              [(4096 + oc * 128, oc) for oc in range(8)],
                              xnfm, 1024, tks, [wb0, wb1], pq, sev,
                              rope_store(kbuf, 0, sp * 1024), None)
                    emit_proj(p, sems, wt_ap,
                              [(5120 + oc * 128, oc) for oc in range(8)],
                              xnfm, 1024, tks, [wb0, wb1], pq, sev,
                              plain_store(vbuf, 0, sp * 1024), None)

        barrier(p, sems)
        with ExitStack() as sc2:
            emit_attention(p, sems, 8, 8, 4, qbuf, kbuf, vbuf, sctx, sc2, S)
        if debug:
            nc.sync.wait_ge(sems["a"].h, sems["a"].n)
            for hh in range(32):
                qs.inc(nc.sync.dma_start(
                    out=ctxd[hh * 128:(hh + 1) * 128, :],
                    in_=sctx[:, hh * 1024:(hh + 1) * 1024]), 16)

        barrier(p, sems)
        with ExitStack() as sc3:
            emit_outproj(p, sems, wtbuf[:, 6144:10240], sctx, 32, xloc, out,
                         None, None, sc3)

        nc.sync.wait_ge(qs.h, qs.n)
        nc.sync.wait_ge(qg.h, qg.n)
        nc.sync.wait_ge(qc.h, qc.n)
        nc.sync.wait_ge(sems["st"].h, sems["st"].n)


def build_l1_attn(debug=False):
    p = Prog("l1attn")
    nc = p.nc
    xloc = p.inp("xloc", [TLOC, H], F32)
    xfull = p.inp("xfull", [S, H], F32)
    wq = p.inp("wq", [H, H], F32)
    wk = p.inp("wk", [1024, H], F32)
    wv = p.inp("wv", [1024, H], F32)
    wo = p.inp("wo", [H, H], F32)
    eye = p.inp("eye", [128, 128], F32)
    invf = p.inp("invf", [128, 1], F32)
    rotm = p.inp("rotm", [128, 128], F32)
    posoff = p.inp("posoff", [1, 1], F32)
    out = p.outp("out", [TLOC, H], F32)
    dbg = None
    if debug:
        dbg = dict(qbuf=p.outp("qbuf", [H, TLOC], BF16),
                   kbuf=p.outp("kbuf", [1024, S], BF16),
                   vbuf=p.outp("vbuf", [1024, S], BF16),
                   wtbuf=p.outp("wtbuf", [H, 10240], BF16),
                   ctxd=p.outp("ctxd", [H, TLOC], BF16),
                   tabd=p.outp("tabd", [128, 2048], F32))
    with p.es:
        block = p.es.enter_context(nc.Block())
        sems = mk_sems(p, nc, "c")
        emit_block_l1attn(p, sems, xloc, xfull, wq, wk, wv, wo, eye, invf, rotm,
                          posoff, out, None, debug, dbg)
    return p


_INVF = (1.0 / (500000.0 ** (np.arange(128, dtype=np.float32) % 64 * 2.0 / 128.0))
         ).reshape(128, 1).astype(np.float32)
_ROTM = np.zeros((128, 128), np.float32)
_ROTM[np.arange(64) + 64, np.arange(64)] = -1.0
_ROTM[np.arange(64), np.arange(64) + 64] = 1.0


def _run_l1_attn(x2, wq, wk, wv, wo, debug=False):
    p = build_l1_attn(debug)
    in_maps = []
    for c in range(NCORE):
        b, h = c // 2, c % 2
        in_maps.append(dict(
            xloc=np.ascontiguousarray(x2[b, h * TLOC:(h + 1) * TLOC]),
            xfull=np.ascontiguousarray(x2[b]),
            wq=wq, wk=wk, wv=wv, wo=wo, eye=_EYE, invf=_INVF, rotm=_ROTM,
            posoff=np.full((1, 1), h * TLOC, np.float32)))
    res = run_bass_kernel_spmd(p.nc, in_maps, core_ids=list(range(NCORE)))
    out = np.empty((B, S, H), np.float32)
    for c in range(NCORE):
        b, h = c // 2, c % 2
        out[b, h * TLOC:(h + 1) * TLOC] = res.results[c]["out"]
    if debug:
        return out, res
    return out



def emit_block_l1ffn(p, sems, xin, qw_gu, qz_gu, sc_gu, gix_gu, qw_dn, qz_dn,
                     sc_dn, gix_dn, shifts, eye, out):
    nc = p.nc
    hgbuf = p.scratch(p.u("hgbuf"), [INTER, TLOC], BF16)
    hbuf = p.scratch(p.u("hbuf"), [INTER, TLOC], BF16)
    scsl_gu = p.scratch(p.u("scsl_gu"), [86, 32, 256], F32)
    scsl_dn = p.scratch(p.u("scsl_dn"), [16, 86, 256], F32)

    with block_scope(p):
        qs, qg, qc, a, v, t = (sems[k] for k in ["qs", "qg", "qc", "a", "v", "t"])

        p.seye = p.sb("seye", [128, 128], BF16)
        st1 = p.sb("st1", [128, 1], F32)
        st2 = p.sb("st2", [128, 1], F32)
        stm = p.sb("stm", [128, 1], F32)
        stv = p.sb("stv", [128, 1], F32)
        stb = p.sb("stb", [128, 1], F32)
        stats = (st1, st2, stm, stv, stb)
        xnfm = p.sb("xnfm", [128, 32 * 1024], BF16)
        pk_gu_s = p.sb("pk_gu_s", [32, 2752], I32)
        pk_dn_s = p.sb("pk_dn_s", [86, 512], I32)
        snz = p.sb("snz", [128, 512], U8)
        ssc2 = p.sb("ssc2", [128, 512], F32)
        sm2 = p.sb("sm2", [128, 512], BF16)
        sxg = p.sb("sxg", [32, TLOC], BF16)
        sxh = p.sb("sxh", [86, TLOC], BF16)
        sioc = p.sb("sioc", [128, 86], F32)
        sonehot = p.sb("sonehot", [128, 86], BF16)
        sshift = p.sb("sshift", [128, 1], I32)
        sgix = p.sb("sgix", [128, 1], I32)
        sgixf = p.sb("sgixf", [128, 1], F32)

        qg.inc(nc.gpsimd.dma_start(out=p.seye[:], in_=eye), 16)
        qs.inc(nc.sync.dma_start(out=sshift[:], in_=shifts), 16)
        sems["g"].inc(nc.gpsimd.iota(out=sioc[:], pattern=[[1, 86]], base=0,
                                     channel_multiplier=0,
                                     allow_small_or_imprecise_dtypes=True))
        qs.inc(nc.sync.dma_start(out=pk_gu_s[:], in_=qz_gu.bitcast(I32)), 16)
        qs.inc(nc.sync.dma_start(out=pk_dn_s[:], in_=qz_dn.bitcast(I32)), 16)
        qs.inc(nc.sync.dma_start(out=scsl_gu,
                                 in_=sc_gu.rearrange("g (s w) -> s g w", w=256)), 16)
        qs.inc(nc.sync.dma_start(out=scsl_dn,
                                 in_=sc_dn.rearrange("g (s w) -> s g w", w=256)), 16)
        sc_gu_flat = scsl_gu.rearrange("s g w -> (s g) w")
        sc_dn_flat = scsl_dn.rearrange("s g w -> (s g) w")
        nc.gpsimd.wait_ge(qs.h, qs.n)

        def emit_m2_slice_d(pk_sb, nrow, sc_src, c0, w):
            nc.scalar.wait_ge(v.h, v.n)
            qc.inc(nc.scalar.dma_start(out=ssc2[0:nrow, 0:w],
                                       in_=sc_src[:, c0:c0 + w]), 16)
            nc.vector.wait_ge(qc.h, qc.n)
            nc.vector.wait_ge(t.h, t.n)
            pk8 = pk_sb[0:nrow, c0 // 8:(c0 + w) // 8].bitcast(U8)
            nc.vector.tensor_scalar(out=snz[0:nrow, 0:w:2], in0=pk8, scalar1=15,
                                    scalar2=None, op0=ALU.bitwise_and)
            v.inc(nc.vector.tensor_scalar(out=snz[0:nrow, 1:w:2], in0=pk8,
                                          scalar1=4, scalar2=None,
                                          op0=ALU.logical_shift_right))
            nc.vector.wait_ge(v.h, v.n)
            v.inc(nc.vector.scalar_tensor_tensor(out=ssc2[0:nrow, 0:w],
                                                 in0=snz[0:nrow, 0:w],
                                                 scalar=7.0, in1=ssc2[0:nrow, 0:w],
                                                 op0=ALU.subtract, op1=ALU.mult))
            nc.vector.wait_ge(v.h, v.n)
            v.inc(nc.vector.tensor_scalar(out=sm2[0:nrow, 0:w],
                                          in0=ssc2[0:nrow, 0:w], scalar1=-1.0,
                                          scalar2=None, op0=ALU.mult))

        def onehot_gidx(gix_src, k0, ncols):
            nc.sync.wait_ge(v.h, v.n)
            qs.inc(nc.sync.dma_start(out=sgix[:], in_=gix_src[k0:k0 + 128, :]), 16)
            nc.vector.wait_ge(qs.h, qs.n)
            v.inc(nc.vector.tensor_copy(out=sgixf[:], in_=sgix[:]))
            nc.vector.wait_ge(v.h, v.n)
            nc.vector.wait_ge(t.h, t.n)
            nc.vector.wait_ge(sems["g"].h, sems["g"].n)
            v.inc(nc.vector.tensor_scalar(out=sonehot[:, 0:ncols],
                                          in0=sioc[:, 0:ncols],
                                          scalar1=sgixf[:, 0:1], scalar2=None,
                                          op0=ALU.is_equal))

        barrier(p, sems)
        with ExitStack() as sc1:
            sx = p.sb("sx", [128, H], F32, sc1)
            sxn = p.sb("sxn", [128, H], BF16, sc1)
            with nc.psum_tensor(p.u("pt8a"), [128, 1024], BF16) as pt8a, \
                 nc.psum_tensor(p.u("pt8b"), [128, 1024], BF16) as pt8b, \
                 nc.psum_tensor(p.u("pxg"), [32, 1024], F32) as pxg:
                emit_ln_span(p, sems, xin, 0, 8, sx, sxn, stats, xnfm,
                             [pt8a, pt8b], 1024)
                for kt in range(32):
                    onehot_gidx(gix_gu, kt * 128, 32)
                    nc.tensor.wait_ge(v.h, v.n)
                    for tk in range(2):
                        t.inc(nc.tensor.matmul(pxg[:, tk * 512:(tk + 1) * 512],
                                               sonehot[:, 0:32],
                                               xnfm[:, kt * 1024 + tk * 512:
                                                    kt * 1024 + (tk + 1) * 512],
                                               start=(kt == 0), stop=(kt == 31)))
                nc.scalar.wait_ge(t.h, t.n)
                a.inc(nc.scalar.copy(out=sxg[:], in_=pxg[:]))
                nc.tensor.wait_ge(a.h, a.n)

        barrier(p, sems)
        sgix2 = [p.sb("sgix2", [128, 1], I32) for _ in range(2)]
        with ExitStack() as sc2:
            spk = [p.sb("spk", [128, 512], I32, sc2) for _ in range(2)]
            spl = [p.sb("spl", [128, 512], F32, sc2) for _ in range(2)]
            snib = p.sb("snib", [128, 512], I32, sc2)
            sw = p.sb("sw", [128, 512], BF16, sc2)
            shg = p.sb("shg", [128, 512], BF16, sc2)
            lds = [Sem(p_sem_raw(p, "ldsA")), Sem(p_sem_raw(p, "ldsB"))]
            ldfree = [0, 0]

            def gu_slab_d(j0, w, is_gate):
                nj = w // 128
                with ExitStack() as pes:
                    pg = [[p.psum(f"pg{jc}{tk}", [128, 512], F32, pes)
                           for tk in range(2)] for jc in range(nj)]

                    def issue_load(kt, buf):
                        nc.gpsimd.wait_ge(v.h, ldfree[buf])
                        lds[buf].inc(nc.gpsimd.dma_start(
                            out=spk[buf][:, 0:w],
                            in_=qw_gu.bitcast(I32)[kt * 16:(kt + 1) * 16, j0:j0 + w]
                            .unsqueeze(1).broadcast_to([16, 8, w])), 16)
                        lds[buf].inc(nc.gpsimd.dma_start(
                            out=sgix2[buf][:],
                            in_=gix_gu[kt * 128:(kt + 1) * 128, :]), 16)
                        nc.gpsimd.wait_ge(lds[buf].h, lds[buf].n)
                        for gi in range(w // 256):
                            ci = (j0 + gi * 256) // 256
                            lds[buf].inc(nc.gpsimd.indirect_dma_start(
                                out=spl[buf][:, gi * 256:(gi + 1) * 256],
                                out_offset=None, in_=sc_gu_flat,
                                in_offset=bass.IndirectOffsetOnAxis(
                                    sgix2[buf][:, 0:1], 0),
                                element_offset=ci * 32 * 256), 16)

                    issue_load(0, 0)
                    for kt in range(32):
                        buf = kt % 2
                        if kt < 31:
                            issue_load(kt + 1, 1 - buf)
                        nc.vector.wait_ge(lds[buf].h, lds[buf].n)
                        nc.vector.wait_ge(t.h, t.n)
                        nc.vector.wait_ge(sems["st"].h, sems["st"].n)
                        v.inc(nc.vector.tensor_scalar(
                            out=snib[:, 0:w], in0=spk[buf][:, 0:w],
                            scalar1=sshift[:, 0:1], scalar2=15,
                            op0=ALU.logical_shift_right, op1=ALU.bitwise_and))
                        nc.vector.wait_ge(v.h, v.n)
                        v.inc(nc.vector.scalar_tensor_tensor(
                            out=sw[:, 0:w], in0=snib[:, 0:w], scalar=8.0,
                            in1=spl[buf][:, 0:w], op0=ALU.subtract, op1=ALU.mult))
                        ldfree[buf] = v.n
                        nc.tensor.wait_ge(v.h, v.n)
                        if kt == 0:
                            nc.tensor.wait_ge(a.h, a.n)
                        for jc in range(nj):
                            for tk in range(2):
                                t.inc(nc.tensor.matmul(
                                    pg[jc][tk][:],
                                    sw[:, jc * 128:(jc + 1) * 128],
                                    xnfm[:, kt * 1024 + tk * 512:
                                         kt * 1024 + (tk + 1) * 512],
                                    start=(kt == 0), stop=False))
                    emit_m2_slice_d(pk_gu_s, 32, sc_gu, j0, w)
                    nc.tensor.wait_ge(v.h, v.n)
                    for jc in range(nj):
                        for tk in range(2):
                            t.inc(nc.tensor.matmul(
                                pg[jc][tk][:],
                                sm2[0:32, jc * 128:(jc + 1) * 128],
                                sxg[:, tk * 512:(tk + 1) * 512],
                                start=False, stop=True))
                    for jc in range(nj):
                        jj = j0 + jc * 128
                        for tk in range(2):
                            if is_gate:
                                nc.scalar.wait_ge(t.h, t.n)
                                nc.scalar.wait_ge(sems["st"].h, sems["st"].n)
                                a.inc(nc.scalar.activation(out=shg[:],
                                                           in_=pg[jc][tk][:],
                                                           func=AF.Silu))
                                nc.sync.wait_ge(a.h, a.n)
                                sems["st"].inc(nc.sync.dma_start(
                                    out=hgbuf[jj:jj + 128,
                                              tk * 512:(tk + 1) * 512],
                                    in_=shg[:]), 16)
                            else:
                                jp = jj - INTER
                                nc.scalar.wait_ge(sems["st"].h, sems["st"].n)
                                qc.inc(nc.scalar.dma_start(
                                    out=shg[:],
                                    in_=hgbuf[jp:jp + 128,
                                              tk * 512:(tk + 1) * 512]), 16)
                                nc.vector.wait_ge(qc.h, qc.n)
                                nc.vector.wait_ge(t.h, t.n)
                                nc.vector.wait_ge(sems["st"].h, sems["st"].n)
                                v.inc(nc.vector.tensor_tensor(
                                    out=sw[:, 0:512], in0=pg[jc][tk][:],
                                    in1=shg[:], op=ALU.mult))
                                nc.sync.wait_ge(v.h, v.n)
                                sems["st"].inc(nc.sync.dma_start(
                                    out=hbuf[jp:jp + 128,
                                             tk * 512:(tk + 1) * 512],
                                    in_=sw[:, 0:512]), 16)
                    nc.tensor.wait_ge(a.h, a.n)
                    nc.tensor.wait_ge(v.h, v.n)

            slabs = [(i * 512, 512) for i in range(21)] + [(10752, 256)]
            for j0, w in slabs:
                gu_slab_d(j0, w, True)
            for j0, w in slabs:
                gu_slab_d(INTER + j0, w, False)

        barrier(p, sems)
        with ExitStack() as sc3:
            sht = p.sb("sht", [128, TLOC], BF16, sc3)
            with nc.psum_tensor(p.u("pxh"), [86, 1024], F32) as pxh:
                nc.sync.wait_ge(qs.h, qs.n)
                for g in range(86):
                    nc.sync.wait_ge(t.h, t.n)
                    qs.inc(nc.sync.dma_start(out=sht[:],
                                             in_=hbuf[g * 128:(g + 1) * 128, :]), 16)
                    onehot_gidx(gix_dn, g * 128, 86)
                    nc.tensor.wait_ge(v.h, v.n)
                    nc.tensor.wait_ge(qs.h, qs.n)
                    for tk in range(2):
                        t.inc(nc.tensor.matmul(pxh[:, tk * 512:(tk + 1) * 512],
                                               sonehot[:],
                                               sht[:, tk * 512:(tk + 1) * 512],
                                               start=(g == 0), stop=(g == 85)))
                nc.scalar.wait_ge(t.h, t.n)
                a.inc(nc.scalar.copy(out=sxh[:], in_=pxh[:]))
                nc.tensor.wait_ge(a.h, a.n)

        barrier(p, sems)
        with ExitStack() as sc4:
            spkd = p.sb("spkd", [128, 512], I32, sc4)
            spld = p.sb("spld", [128, 512], F32, sc4)
            snibd = p.sb("snibd", [128, 512], I32, sc4)
            swd0 = p.sb("swd0", [128, 512], BF16, sc4)
            swd1 = p.sb("swd1", [128, 512], BF16, sc4)
            swd = [swd0, swd1]
            sht0 = p.sb("sht0", [128, TLOC], BF16, sc4)
            sht1 = p.sb("sht1", [128, TLOC], BF16, sc4)
            shts = [sht0, sht1]
            sxr = p.sb("sxr", [128, 512], F32, sc4)
            sout = p.sb("sout", [128, 512], F32, sc4)
            hs = [Sem(p_sem_raw(p, "hsA")), Sem(p_sem_raw(p, "hsB"))]
            wds = [Sem(p_sem_raw(p, "wdsA")), Sem(p_sem_raw(p, "wdsB"))]
            spkd2 = [spkd, p.sb("spkd", [128, 512], I32, sc4)]
            spld2 = [spld, p.sb("spld", [128, 512], F32, sc4)]
            sgixd2 = [sgix, p.sb("sgixd", [128, 1], I32, sc4)]
            hfree = [0, 0]
            wdfree = [0, 0]
            with ExitStack() as pes:
                pd = [p.psum(f"pd{i}", [128, 512], F32, pes) for i in range(8)]
                for dm in range(8):
                    def issue_dn(kt, buf, dm=dm):
                        nc.sync.wait_ge(t.h, hfree[buf])
                        hs[buf].inc(nc.sync.dma_start(
                            out=shts[buf][:],
                            in_=hbuf[kt * 128:(kt + 1) * 128, :]), 16)
                        nc.gpsimd.wait_ge(v.h, wdfree[buf])
                        wds[buf].inc(nc.gpsimd.dma_start(
                            out=spkd2[buf][:],
                            in_=qw_dn.bitcast(I32)[kt * 16:(kt + 1) * 16,
                                                   dm * 512:(dm + 1) * 512]
                            .unsqueeze(1).broadcast_to([16, 8, 512])), 16)
                        wds[buf].inc(nc.gpsimd.dma_start(
                            out=sgixd2[buf][:],
                            in_=gix_dn[kt * 128:(kt + 1) * 128, :]), 16)
                        nc.gpsimd.wait_ge(wds[buf].h, wds[buf].n)
                        for gi in range(2):
                            ci = dm * 2 + gi
                            wds[buf].inc(nc.gpsimd.indirect_dma_start(
                                out=spld2[buf][:, gi * 256:(gi + 1) * 256],
                                out_offset=None, in_=sc_dn_flat,
                                in_offset=bass.IndirectOffsetOnAxis(
                                    sgixd2[buf][:, 0:1], 0),
                                element_offset=ci * 86 * 256), 16)

                    issue_dn(0, 0)
                    for kt in range(86):
                        buf = kt % 2
                        if kt < 85:
                            issue_dn(kt + 1, 1 - buf)
                        nc.vector.wait_ge(wds[buf].h, wds[buf].n)
                        nc.vector.wait_ge(t.h, t.n)
                        v.inc(nc.vector.tensor_scalar(
                            out=snibd[:], in0=spkd2[buf][:],
                            scalar1=sshift[:, 0:1], scalar2=15,
                            op0=ALU.logical_shift_right, op1=ALU.bitwise_and))
                        nc.vector.wait_ge(v.h, v.n)
                        v.inc(nc.vector.scalar_tensor_tensor(
                            out=swd[buf][:], in0=snibd[:], scalar=8.0,
                            in1=spld2[buf][:], op0=ALU.subtract, op1=ALU.mult))
                        wdfree[buf] = v.n
                        nc.tensor.wait_ge(v.h, v.n)
                        nc.tensor.wait_ge(hs[buf].h, hs[buf].n)
                        if kt == 0:
                            nc.tensor.wait_ge(a.h, a.n)
                            nc.tensor.wait_ge(qg.h, qg.n)
                        for tt in range(8):
                            t.inc(nc.tensor.matmul(
                                pd[tt][:],
                                shts[buf][:, tt * 128:(tt + 1) * 128],
                                swd[buf][:], start=(kt == 0), stop=False))
                        hfree[buf] = t.n
                    emit_m2_slice_d(pk_dn_s, 86, sc_dn, dm * 512, 512)
                    nc.tensor.wait_ge(v.h, v.n)
                    for tt in range(8):
                        t.inc(nc.tensor.matmul(
                            pd[tt][:], sxh[:, tt * 128:(tt + 1) * 128],
                            sm2[0:86, 0:512],
                            start=False, stop=True))
                    for tt in range(8):
                        nc.gpsimd.wait_ge(v.h, v.n)
                        qg.inc(nc.gpsimd.dma_start(
                            out=sxr[:],
                            in_=xin[tt * 128:(tt + 1) * 128,
                                    dm * 512:(dm + 1) * 512]), 16)
                        nc.vector.wait_ge(qg.h, qg.n)
                        nc.vector.wait_ge(t.h, t.n)
                        v.inc(nc.vector.tensor_add(out=sout[:], in0=pd[tt][:],
                                                   in1=sxr[:]))
                        nc.gpsimd.wait_ge(v.h, v.n)
                        qg.inc(nc.gpsimd.dma_start(
                            out=out[tt * 128:(tt + 1) * 128,
                                    dm * 512:(dm + 1) * 512], in_=sout[:]), 16)
                        nc.vector.wait_ge(qg.h, qg.n)

        nc.sync.wait_ge(qs.h, qs.n)
        nc.sync.wait_ge(qg.h, qg.n)
        nc.sync.wait_ge(qc.h, qc.n)
        nc.sync.wait_ge(sems["st"].h, sems["st"].n)


def build_l1_ffn():
    p = Prog("l1ffn")
    nc = p.nc
    U32_ = mybir.dt.uint32
    xin = p.inp("xin", [TLOC, H], F32)
    qw_gu = p.inp("qw_gu", [512, 2 * INTER], U32_)
    qz_gu = p.inp("qz_gu", [32, 2752], U32_)
    sc_gu = p.inp("sc_gu", [32, 2 * INTER], F32)
    gix_gu = p.inp("gix_gu", [H, 1], I32)
    qw_dn = p.inp("qw_dn", [1376, H], U32_)
    qz_dn = p.inp("qz_dn", [86, 512], U32_)
    sc_dn = p.inp("sc_dn", [86, H], F32)
    gix_dn = p.inp("gix_dn", [INTER, 1], I32)
    shifts = p.inp("shifts", [128, 1], I32)
    eye = p.inp("eye", [128, 128], F32)
    out = p.outp("out", [TLOC, H], F32)
    with p.es:
        block = p.es.enter_context(nc.Block())
        sems = mk_sems(p, nc, "d")
        emit_block_l1ffn(p, sems, xin, qw_gu, qz_gu, sc_gu, gix_gu, qw_dn, qz_dn,
                         sc_dn, gix_dn, shifts, eye, out)
    return p


_SHIFTS = ((np.arange(128) % 8) * 4).astype(np.int32).reshape(128, 1)


def _run_l1_ffn(x3, qw_gu, qz_gu, sc_gu, gix_gu, qw_dn, qz_dn, sc_dn, gix_dn):
    p = build_l1_ffn()
    in_maps = []
    for c in range(NCORE):
        b, h = c // 2, c % 2
        in_maps.append(dict(
            xin=np.ascontiguousarray(x3[b, h * TLOC:(h + 1) * TLOC]),
            qw_gu=qw_gu, qz_gu=qz_gu, sc_gu=sc_gu,
            gix_gu=np.ascontiguousarray(gix_gu.reshape(H, 1)),
            qw_dn=qw_dn, qz_dn=qz_dn, sc_dn=sc_dn,
            gix_dn=np.ascontiguousarray(gix_dn.reshape(INTER, 1)),
            shifts=_SHIFTS, eye=_EYE))
    res = run_bass_kernel_spmd(p.nc, in_maps, core_ids=list(range(NCORE)))
    out = np.empty((B, S, H), np.float32)
    for c in range(NCORE):
        b, h = c // 2, c % 2
        out[b, h * TLOC:(h + 1) * TLOC] = res.results[c]["out"]
    return out



def emit_exchange(p, xdone, x2loc, x2sh, x2meta, x2flag, nonce, hsel):
    nc = p.nc
    with block_scope(p):
        snon = p.sb("snon", [1, 1], I32)
        shsel = p.sb("shsel", [1, 1], I32)
        sflagin = p.sb("sflagin", [1, 1], I32)
        xq = Sem(p_sem_raw(p, "xq"))
        pollq = Sem(p_sem_raw(p, "pollq"))
        g = nc.gpsimd
        xq.inc(g.dma_start(out=snon[:], in_=nonce), 16)
        xq.inc(g.dma_start(out=shsel[:], in_=hsel), 16)
        g.wait_ge(xq.h, xq.n)
        with g.register("rf") as rf, g.register("rc") as rc, \
             g.register("rn") as rn, g.register("nd") as nd:
            g.reg_load(rn, snon[0:1, 0:1])
            g.reg_mov(rc, 0)
            xq_base = xq.n
            for arm in g.switch_core_id(n=128):
                xq.n = xq_base
                par = (int(arm) % 8) % 2
                xq.inc(g.dma_start(out=x2sh[par * 1024:(par + 1) * 1024, :],
                                   in_=x2loc), 16)
                g.wait_ge(xq.h, xq.n)
                xq.inc(g.dma_start(out=x2meta[par:par + 1, :], in_=shsel[:]), 16)
                g.wait_ge(xq.h, xq.n)
                xq.inc(g.dma_start(out=x2flag[par:par + 1, :], in_=snon[:]), 16)
                g.wait_ge(xq.h, xq.n)
                g.reg_add(rc, rc, 16)
                g.dma_start(out=sflagin[:],
                            in_=x2flag[1 - par:2 - par, :]).then_inc(pollq.h, 16)
                g.wait_ge(pollq.h, rc)
                g.reg_load(rf, sflagin[0:1, 0:1])
                g.reg_alu(nd, rf, rn, ALU.not_equal)
                with g.While(nd):
                    g.reg_add(rc, rc, 16)
                    g.dma_start(out=sflagin[:],
                                in_=x2flag[1 - par:2 - par, :]).then_inc(pollq.h, 16)
                    g.wait_ge(pollq.h, rc)
                    g.reg_load(rf, sflagin[0:1, 0:1])
                    g.reg_alu(nd, rf, rn, ALU.not_equal)
            xq.n = xq_base + 48
        g.sem_inc(xdone.h, 1)
        xdone.n += 1


def build_fused(detect_races=True):
    p = Prog("fused", detect_races)
    nc = p.nc
    U32_ = mybir.dt.uint32
    xloc = p.inp("xloc", [TLOC, H], F32)
    xfull = p.inp("xfull", [S, H], F32)
    qkvw = p.inp("qkvw", [H, 3 * H], F32)
    qkvb2 = p.inp("qkvb2", [96, 128], F32)
    ow = p.inp("ow", [H, H], F32)
    ob = p.inp("ob", [1, H], F32)
    b_qw_gu = p.inp("b_qw_gu", [H, 2752], I32)
    b_qz_gu = p.inp("b_qz_gu", [32, 2752], I32)
    b_sc_gu = p.inp("b_sc_gu", [32, 2 * INTER], F32)
    b_qw_dn = p.inp("b_qw_dn", [INTER, 512], I32)
    b_qz_dn = p.inp("b_qz_dn", [86, 512], I32)
    b_sc_dn = p.inp("b_sc_dn", [86, H], F32)
    wq = p.inp("wq", [H, H], F32)
    wk = p.inp("wk", [1024, H], F32)
    wv = p.inp("wv", [1024, H], F32)
    wo = p.inp("wo", [H, H], F32)
    d_qw_gu = p.inp("d_qw_gu", [512, 2 * INTER], U32_)
    d_qz_gu = p.inp("d_qz_gu", [32, 2752], U32_)
    d_sc_gu = p.inp("d_sc_gu", [32, 2 * INTER], F32)
    d_gix_gu = p.inp("d_gix_gu", [H, 1], I32)
    d_qw_dn = p.inp("d_qw_dn", [1376, H], U32_)
    d_qz_dn = p.inp("d_qz_dn", [86, 512], U32_)
    d_sc_dn = p.inp("d_sc_dn", [86, H], F32)
    d_gix_dn = p.inp("d_gix_dn", [INTER, 1], I32)
    ones = p.inp("ones", [1, 128], F32)
    eye = p.inp("eye", [128, 128], F32)
    invf = p.inp("invf", [128, 1], F32)
    rotm = p.inp("rotm", [128, 128], F32)
    posoff = p.inp("posoff", [1, 1], F32)
    hsel = p.inp("hsel", [1, 1], I32)
    nonce = p.inp("nonce", [1, 1], I32)
    shifts = p.inp("shifts", [128, 1], I32)
    out = p.outp("out", [TLOC, H], F32)
    x1loc = p.scratch("x1loc", [TLOC, H], F32)
    x2loc = p.scratch("x2loc", [TLOC, H], F32)
    x3loc = p.scratch("x3loc", [TLOC, H], F32)
    x2sh = p.scratch("x2sh", [S, H], F32, shared=True)
    x2meta = p.scratch("x2meta", [2, 1], I32, shared=True)
    x2flag = p.scratch("x2flag", [2, 1], I32, shared=True)

    with p.es:
        block = p.es.enter_context(nc.Block())
        sa = mk_sems(p, nc, "a")
        sb_ = mk_sems(p, nc, "b")
        sc_ = mk_sems(p, nc, "c")
        sd_ = mk_sems(p, nc, "d")
        xdone = Sem(p_sem_raw(p, "xdone"))

        emit_block_l0attn(p, sa, xloc, xfull, qkvw, qkvb2, ow, ob, ones, eye,
                          x1loc)
        barrier(p, sa)
        emit_block_l0ffn(p, sb_, x1loc, b_qw_gu, b_qz_gu, b_sc_gu, b_qw_dn,
                         b_qz_dn, b_sc_dn, eye, x2loc)
        barrier(p, sb_)
        emit_exchange(p, xdone, x2loc, x2sh, x2meta, x2flag, nonce, hsel)
        for eng in [nc.sync, nc.gpsimd, nc.scalar, nc.vector, nc.tensor]:
            eng.wait_ge(xdone.h, 1)
        soff_sp = [p.sb(f"soffsp{i}", [128, 1], F32) for i in range(2)]
        smeta_i = p.sb("smeta_i", [128, 1], I32)
        for sp in range(2):
            sc_["qs"].inc(nc.sync.dma_start(
                out=smeta_i[:], in_=x2meta[sp:sp + 1, :].partition_broadcast(128)),
                16)
            nc.vector.wait_ge(sc_["qs"].h, sc_["qs"].n)
            nc.vector.wait_ge(sc_["v"].h, sc_["v"].n)
            sc_["v"].inc(nc.vector.tensor_scalar(
                out=soff_sp[sp][:], in0=smeta_i[:], scalar1=1024.0, scalar2=None,
                op0=ALU.mult))
        nc.vector.wait_ge(sc_["v"].h, sc_["v"].n)
        emit_block_l1attn(p, sc_, x2loc, x2sh, wq, wk, wv, wo, eye, invf, rotm,
                          posoff, x3loc,
                          span_off=lambda sp: soff_sp[sp][:, 0:1])
        barrier(p, sc_)
        emit_block_l1ffn(p, sd_, x3loc, d_qw_gu, d_qz_gu, d_sc_gu, d_gix_gu,
                         d_qw_dn, d_qz_dn, d_sc_dn, d_gix_dn, shifts, eye, out)
    return p


def kernel_fused(inp):
    p = build_fused()
    x = np.asarray(inp["L0_attn_x"], np.float32)
    qkvb2 = np.ascontiguousarray(np.asarray(inp["L0_attn_attn_qkvb"]).reshape(96, 128))
    obr = np.ascontiguousarray(np.asarray(inp["L0_attn_attn_ob"]).reshape(1, H))
    nonce = np.array([[np.random.randint(1, 2**31 - 1)]], np.int32)
    in_maps = []
    for c in range(NCORE):
        b, h = c // 2, c % 2
        in_maps.append(dict(
            xloc=np.ascontiguousarray(x[b, h * TLOC:(h + 1) * TLOC]),
            xfull=np.ascontiguousarray(x[b]),
            qkvw=np.asarray(inp["L0_attn_attn_qkvw"]), qkvb2=qkvb2,
            ow=np.asarray(inp["L0_attn_attn_ow"]), ob=obr,
            b_qw_gu=np.asarray(inp["L0_ffn_qweight_gu"]),
            b_qz_gu=np.asarray(inp["L0_ffn_qzeros_gu"]),
            b_sc_gu=np.asarray(inp["L0_ffn_scales_gu"]),
            b_qw_dn=np.asarray(inp["L0_ffn_qweight_dn"]),
            b_qz_dn=np.asarray(inp["L0_ffn_qzeros_dn"]),
            b_sc_dn=np.asarray(inp["L0_ffn_scales_dn"]),
            wq=np.asarray(inp["L1_attn_Wq"]), wk=np.asarray(inp["L1_attn_Wk"]),
            wv=np.asarray(inp["L1_attn_Wv"]), wo=np.asarray(inp["L1_attn_Wo"]),
            d_qw_gu=np.asarray(inp["L1_ffn_qweight_gu"]),
            d_qz_gu=np.asarray(inp["L1_ffn_qzeros_gu"]),
            d_sc_gu=np.asarray(inp["L1_ffn_scales_gu"]),
            d_gix_gu=np.ascontiguousarray(
                np.asarray(inp["L1_ffn_g_idx_gu"]).reshape(H, 1)),
            d_qw_dn=np.asarray(inp["L1_ffn_qweight_down"]),
            d_qz_dn=np.asarray(inp["L1_ffn_qzeros_down"]),
            d_sc_dn=np.asarray(inp["L1_ffn_scales_down"]),
            d_gix_dn=np.ascontiguousarray(
                np.asarray(inp["L1_ffn_g_idx_down"]).reshape(INTER, 1)),
            ones=_ONES, eye=_EYE, invf=_INVF, rotm=_ROTM,
            posoff=np.full((1, 1), h * TLOC, np.float32),
            hsel=np.full((1, 1), h, np.int32),
            nonce=nonce, shifts=_SHIFTS))
    res = run_bass_kernel_spmd(p.nc, in_maps, core_ids=list(range(NCORE)))
    outv = np.empty((B, S, H), np.float32)
    for c in range(NCORE):
        b, h = c // 2, c % 2
        outv[b, h * TLOC:(h + 1) * TLOC] = res.results[c]["out"]
    return outv


# version 04
# speedup vs baseline: 1.5705x; 1.0924x over previous
import numpy as np
from contextlib import ExitStack
import concourse.bass as bass
import concourse.mybir as mybir
from concourse.bass_utils import run_bass_kernel_spmd

F32 = mybir.dt.float32
BF16 = mybir.dt.bfloat16
I32 = mybir.dt.int32
U8 = mybir.dt.uint8
AF = mybir.ActivationFunctionType
ALU = mybir.AluOpType
AX = mybir.AxisListType

B, S, H = 4, 2048, 4096
NCORE = 8
TLOC = 1024
INTER = 11008
EPS = 1e-5
RS = 1.0 / float(np.sqrt(128.0))


class Sem:
    def __init__(self, h):
        self.h = h
        self.n = 0
        self.eng = None

    def inc(self, instr, k=1):
        instr.then_inc(self.h, k)
        self.n += k
        if self.eng is not None and k == 16:
            self.eng.wait_ge(self.h, self.n)
        return self.n


class Prog:
    def __init__(self, name, detect_races=True):
        self.nc = bass.Bass("TRN2", target_bir_lowering=False,
                            detect_race_conditions=detect_races)
        self.es = ExitStack()
        self.root_es = self.es
        self.name = name
        self._uid = 0

    def u(self, base):
        self._uid += 1
        return f"{base}_{self._uid}"

    def sb(self, name, shape, dt, scope=None):
        return (scope or self.es).enter_context(
            self.nc.sbuf_tensor(self.u(name), shape, dt))

    def psum(self, name, shape, dt, scope):
        return scope.enter_context(self.nc.psum_tensor(self.u(name), shape, dt))

    def sem(self, name):
        return Sem(self.es.enter_context(self.nc.semaphore(name)))

    def inp(self, name, shape, dt):
        return self.nc.dram_tensor(name, shape, dt, kind="ExternalInput").ap()

    def outp(self, name, shape, dt):
        return self.nc.dram_tensor(name, shape, dt, kind="ExternalOutput").ap()

    def scratch(self, name, shape, dt, shared=False):
        if shared:
            return self.nc.dram_tensor(name, shape, dt, addr_space="Shared").ap()
        return self.nc.dram_tensor(name, shape, dt).ap()



from contextlib import contextmanager


@contextmanager
def block_scope(p):
    old = p.es
    p.es = ExitStack()
    try:
        with p.es:
            yield
    finally:
        p.es = old


def mk_sems(p, nc, tag):
    sems = {k: Sem(p_sem_raw(p, f"{tag}_{k}")) for k in
            ["qs", "qg", "qc", "a", "v", "t", "g", "st"]}
    sems["qs"].eng = nc.sync
    sems["qg"].eng = nc.gpsimd
    sems["qc"].eng = nc.scalar
    return sems


def p_sem_raw(p, name):
    return p.root_es.enter_context(p.nc.semaphore(p.u(name)))


def barrier(p, sems):
    nc = p.nc
    for eng in [nc.sync, nc.gpsimd, nc.scalar, nc.vector, nc.tensor]:
        for s in sems.values():
            eng.wait_ge(s.h, s.n)


def emit_ln_span(p, sems, src, tok0, ntile, sx, sxn, stats, xnfm, ptr8, W):
    nc = p.nc
    qs, a, v, t = sems["qs"], sems["a"], sems["v"], sems["t"]
    st1, st2, stm, stv, stb = stats
    xv = xnfm[:].rearrange("p (kt w) -> p kt w", w=W)
    for i in range(ntile):
        r0 = tok0 + i * 128
        nc.sync.wait_ge(a.h, a.n)
        qs.inc(nc.sync.dma_start(out=sx[:], in_=src[r0:r0 + 128, :]), 16)
        nc.scalar.wait_ge(qs.h, qs.n)
        nc.scalar.wait_ge(t.h, t.n)
        a.inc(nc.scalar.memzero(st2[:]))
        nc.scalar.wait_ge(a.h, a.n)
        a.inc(nc.scalar.activation(out=sxn[:], in_=sx[:], func=AF.Square,
                                   accum_out=st2[:, 0:1]))
        nc.vector.wait_ge(a.h, a.n)

        def vstep(instr):
            v.inc(instr)
            nc.vector.wait_ge(v.h, v.n)
        vstep(nc.vector.reduce_sum(out=st1[:, 0:1], in_=sx[:], axis=AX.X))
        vstep(nc.vector.tensor_scalar(out=stm[:, 0:1], in0=st1[:, 0:1],
                                      scalar1=1.0 / H, scalar2=None, op0=ALU.mult))
        vstep(nc.vector.tensor_tensor(out=stv[:, 0:1], in0=stm[:, 0:1],
                                      in1=stm[:, 0:1], op=ALU.mult))
        vstep(nc.vector.scalar_tensor_tensor(out=stv[:, 0:1], in0=st2[:, 0:1],
                                             scalar=1.0 / H, in1=stv[:, 0:1],
                                             op0=ALU.mult, op1=ALU.subtract))
        vstep(nc.vector.tensor_scalar(out=stv[:, 0:1], in0=stv[:, 0:1],
                                      scalar1=EPS, scalar2=None, op0=ALU.add))
        nc.scalar.wait_ge(v.h, v.n)
        a.inc(nc.scalar.sqrt(out=stv[:, 0:1], in_=stv[:, 0:1]))
        nc.vector.wait_ge(a.h, a.n)
        vstep(nc.vector.reciprocal(out=stv[:, 0:1], in_=stv[:, 0:1]))
        vstep(nc.vector.scalar_tensor_tensor(out=stb[:, 0:1], in0=stm[:, 0:1],
                                             scalar=-1.0, in1=stv[:, 0:1],
                                             op0=ALU.mult, op1=ALU.mult))
        nc.scalar.wait_ge(v.h, v.n)
        nc.scalar.wait_ge(t.h, t.n)
        a.inc(nc.scalar.activation(out=sxn[:], in_=sx[:], func=AF.Identity,
                                   scale=stv[:, 0:1], bias=stb[:, 0:1]))
        nc.tensor.wait_ge(a.h, a.n)
        evfree = [v.n, v.n]
        for g in range(4):
            bnk = g % 2
            nc.tensor.wait_ge(v.h, evfree[bnk])
            for j in range(8):
                kt = g * 8 + j
                t.inc(nc.tensor.matmul(ptr8[bnk][:, j * 128:(j + 1) * 128],
                                       sxn[:, kt * 128:(kt + 1) * 128],
                                       p.seye[:], is_transpose=True,
                                       start=(j == 0), stop=(j == 7)))
            nc.vector.wait_ge(t.h, t.n)
            v.inc(nc.vector.tensor_copy(
                out=xv[:, g * 8:(g + 1) * 8, i * 128:(i + 1) * 128],
                in_=ptr8[bnk][:]))
            evfree[bnk] = v.n


def emit_proj(p, sems, wsrc, chunks, xnfm, W, tks, wbufs, pq, sev, evac_fn,
              bias_fn=None, wrows=H):
    nc = p.nc
    qs, qg, a, t = sems["qs"], sems["qg"], sems["a"], sems["t"]
    st = sems["st"]
    nkt = wrows // 128
    ws = [Sem(p_sem_raw(p, "wsA")), Sem(p_sem_raw(p, "wsB"))]
    wfree = [t.n, t.n]
    for idx, (c0, tag) in enumerate(chunks):
        buf = idx % 2
        nc.gpsimd.wait_ge(t.h, wfree[buf])
        ws[buf].inc(nc.gpsimd.dma_start(
            out=wbufs[buf][:],
            in_=wsrc[:, c0:c0 + 128].rearrange("(kt p) o -> p kt o", p=128)), 16)
        nc.tensor.wait_ge(ws[buf].h, ws[buf].n)
        nc.tensor.wait_ge(a.h, a.n)
        for tki, (tk0, tw) in enumerate(tks):
            for kt in range(nkt):
                t.inc(nc.tensor.matmul(pq[tki][:, 0:tw],
                                       wbufs[buf][:, kt * 128:(kt + 1) * 128],
                                       xnfm[:, kt * W + tk0: kt * W + tk0 + tw],
                                       start=(kt == 0), stop=(kt == nkt - 1)))
        wfree[buf] = t.n
        nc.scalar.wait_ge(t.h, t.n)
        nc.scalar.wait_ge(st.h, st.n)
        for tki, (tk0, tw) in enumerate(tks):
            bias = bias_fn(tag) if bias_fn is not None else None
            if bias is not None:
                a.inc(nc.scalar.activation(out=sev[:, tki * 512: tki * 512 + tw],
                                           in_=pq[tki][:, 0:tw],
                                           func=AF.Identity, bias=bias, scale=1.0))
            else:
                a.inc(nc.scalar.copy(out=sev[:, tki * 512: tki * 512 + tw],
                                     in_=pq[tki][:, 0:tw]))
        nc.sync.wait_ge(a.h, a.n)
        for tki, (tk0, tw) in enumerate(tks):
            evac_fn(tag, tki, sev[:, tki * 512: tki * 512 + tw], tw)


def emit_attention(p, sems, nq_tiles, kv_heads, q_per_kv, qsrc, ksrc, vsrc, sctx,
                   sc2, SKV):
    nc = p.nc
    qs, a, v, t = sems["qs"], sems["a"], sems["v"], sems["t"]
    nkc = SKV // 512
    nkt = SKV // 128
    sq = p.sb("sq", [128, nq_tiles * 128], BF16, sc2)
    sk = p.sb("sk", [128, SKV], BF16, sc2)
    sv = p.sb("sv", [128, SKV], BF16, sc2)
    svt = p.sb("svt", [128, SKV], BF16, sc2)
    sprob2 = [p.sb("sprob", [128, SKV], BF16, sc2) for _ in range(2)]
    sprT = p.sb("sprT", [128, nkt * nq_tiles * 128], BF16, sc2)
    smax = p.sb("smax", [128, 1], F32, sc2)
    snegb = p.sb("snegb", [128, 1], F32, sc2)
    ssum = p.sb("ssum", [128, 1], F32, sc2)
    srec = p.sb("srec", [128, 1], F32, sc2)
    with nc.psum_tensor(p.u("psc"), [128, SKV], F32) as psc, \
         nc.psum_tensor(p.u("pt8a"), [128, 1024], BF16) as pt8a, \
         nc.psum_tensor(p.u("pt8b"), [128, 1024], BF16) as pt8b, \
         nc.psum_tensor(p.u("pc"), [128, 1024], F32) as pc:
        pt8 = [pt8a, pt8b]
        nc.sync.wait_ge(qs.h, qs.n)
        nc.sync.wait_ge(sems["qg"].h, sems["qg"].n)
        nc.sync.wait_ge(sems["qc"].h, sems["qc"].n)
        nc.sync.wait_ge(sems["st"].h, sems["st"].n)
        for kh in range(kv_heads):
            nc.sync.wait_ge(t.h, t.n)
            qs.inc(nc.sync.dma_start(out=sk[:], in_=ksrc[kh * 128:(kh + 1) * 128, :]), 16)
            qs.inc(nc.sync.dma_start(out=sv[:], in_=vsrc[kh * 128:(kh + 1) * 128, :]), 16)
            nc.tensor.wait_ge(qs.h, qs.n)
            evfree = [v.n, v.n]
            for g in range(nkt // 8):
                bnk = g % 2
                nc.tensor.wait_ge(v.h, evfree[bnk])
                for j in range(8):
                    kt = g * 8 + j
                    t.inc(nc.tensor.matmul(pt8[bnk][:, j * 128:(j + 1) * 128],
                                           sv[:, kt * 128:(kt + 1) * 128],
                                           p.seye[:], is_transpose=True,
                                           start=(j == 0), stop=(j == 7)))
                nc.vector.wait_ge(t.h, t.n)
                v.inc(nc.vector.tensor_copy(
                    out=svt[:, g * 1024:(g + 1) * 1024], in_=pt8[bnk][:]))
                evfree[bnk] = v.n
            for qq in range(q_per_kv):
                qh = kh * q_per_kv + qq
                nc.sync.wait_ge(t.h, t.n)
                qs.inc(nc.sync.dma_start(out=sq[:],
                                         in_=qsrc[qh * 128:(qh + 1) * 128, :]), 16)
                nc.tensor.wait_ge(qs.h, qs.n)
                prfree = [t.n, t.n]
                for qt in range(nq_tiles):
                    pb_ = qt % 2
                    sprob = sprob2[pb_]
                    nc.tensor.wait_ge(a.h, a.n)
                    for kc in range(nkc):
                        t.inc(nc.tensor.matmul(psc[:, kc * 512:(kc + 1) * 512],
                                               sq[:, qt * 128:(qt + 1) * 128],
                                               sk[:, kc * 512:(kc + 1) * 512],
                                               start=True, stop=True))
                    nc.vector.wait_ge(t.h, t.n)
                    v.inc(nc.vector.reduce_max(out=smax[:, 0:1], in_=psc[:],
                                               axis=AX.X))
                    nc.vector.wait_ge(v.h, v.n)
                    v.inc(nc.vector.tensor_scalar(out=snegb[:, 0:1], in0=smax[:, 0:1],
                                                  scalar1=-RS, scalar2=None,
                                                  op0=ALU.mult))
                    nc.scalar.wait_ge(v.h, v.n)
                    nc.scalar.wait_ge(t.h, prfree[pb_])
                    a.inc(nc.scalar.memzero(ssum[:]))
                    nc.scalar.wait_ge(a.h, a.n)
                    a.inc(nc.scalar.activation(out=sprob[:], in_=psc[:], func=AF.Exp,
                                               scale=RS, bias=snegb[:, 0:1],
                                               accum_out=ssum[:, 0:1]))
                    nc.vector.wait_ge(a.h, a.n)
                    v.inc(nc.vector.reciprocal(out=srec[:, 0:1], in_=ssum[:, 0:1]))
                    nc.scalar.wait_ge(v.h, v.n)
                    a.inc(nc.scalar.activation(out=sprob[:], in_=sprob[:],
                                               func=AF.Identity, scale=srec[:, 0:1]))
                    nc.tensor.wait_ge(a.h, a.n)
                    evfree = [v.n, v.n]
                    for g in range(nkt // 8):
                        bnk = g % 2
                        nc.tensor.wait_ge(v.h, evfree[bnk])
                        for j in range(8):
                            kt = g * 8 + j
                            t.inc(nc.tensor.matmul(
                                pt8[bnk][:, j * 128:(j + 1) * 128],
                                sprob[:, kt * 128:(kt + 1) * 128],
                                p.seye[:], is_transpose=True,
                                start=(j == 0), stop=(j == 7)))
                        nc.vector.wait_ge(t.h, t.n)
                        prv = sprT[:].rearrange("p (kt q) -> p kt q",
                                                q=nq_tiles * 128)
                        v.inc(nc.vector.tensor_copy(
                            out=prv[:, g * 8:(g + 1) * 8, qt * 128:(qt + 1) * 128],
                            in_=pt8[bnk][:]))
                        evfree[bnk] = v.n
                    prfree[pb_] = t.n
                nc.tensor.wait_ge(v.h, v.n)
                nc.tensor.wait_ge(a.h, a.n)
                for tk in range(nq_tiles * 128 // 512):
                    for kt in range(nkt):
                        t.inc(nc.tensor.matmul(
                            pc[:, tk * 512:(tk + 1) * 512],
                            svt[:, kt * 128:(kt + 1) * 128],
                            sprT[:, kt * nq_tiles * 128 + tk * 512:
                                 kt * nq_tiles * 128 + (tk + 1) * 512],
                            start=(kt == 0), stop=(kt == nkt - 1)))
                nc.scalar.wait_ge(t.h, t.n)
                a.inc(nc.scalar.copy(out=sctx[:, qh * 1024:(qh + 1) * 1024],
                                     in_=pc[:]))


def emit_outproj(p, sems, wsrc, ctx_sb, nft, res_src, dst, bias_row, sones, sc2,
                 wT=False):
    nc = p.nc
    qs, qg, a, v, t = sems["qs"], sems["qg"], sems["a"], sems["v"], sems["t"]
    st = sems["st"]
    ws = [Sem(p_sem_raw(p, "wsA")), Sem(p_sem_raw(p, "wsB"))]
    swo0 = p.sb("swo0", [128, 512], BF16, sc2)
    swo1 = p.sb("swo1", [128, 512], BF16, sc2)
    swo = [swo0, swo1]
    sxr = p.sb("sxr", [128, 512], F32, sc2)
    sout = p.sb("sout", [128, 512], F32, sc2)
    with ExitStack() as pes:
        po = [pes.enter_context(nc.psum_tensor(p.u(f"po{i}"), [128, 512], F32))
              for i in range(8)]
        nc.gpsimd.wait_ge(qs.h, qs.n)
        nc.gpsimd.wait_ge(st.h, st.n)
        for dm in range(8):
            wfree = [t.n, t.n]
            for hh in range(nft):
                buf = hh % 2
                nc.gpsimd.wait_ge(t.h, wfree[buf])
                ws[buf].inc(nc.gpsimd.dma_start(
                    out=swo[buf][:],
                    in_=wsrc[hh * 128:(hh + 1) * 128, dm * 512:(dm + 1) * 512]), 16)
                nc.tensor.wait_ge(ws[buf].h, ws[buf].n)
                if hh == 0:
                    nc.tensor.wait_ge(v.h, v.n)
                last = (hh == nft - 1) and (bias_row is None)
                for tt in range(8):
                    t.inc(nc.tensor.matmul(po[tt][:],
                                           ctx_sb[:, hh * 1024 + tt * 128:
                                                  hh * 1024 + (tt + 1) * 128],
                                           swo[buf][:],
                                           start=(hh == 0), stop=last))
                wfree[buf] = t.n
            if bias_row is not None:
                for tt in range(8):
                    t.inc(nc.tensor.matmul(po[tt][:], sones[:],
                                           bias_row[:, dm * 512:(dm + 1) * 512],
                                           start=False, stop=True))
            for tt in range(8):
                nc.sync.wait_ge(v.h, v.n)
                qs.inc(nc.sync.dma_start(
                    out=sxr[:],
                    in_=res_src[tt * 128:(tt + 1) * 128, dm * 512:(dm + 1) * 512]), 16)
                nc.vector.wait_ge(qs.h, qs.n)
                nc.vector.wait_ge(t.h, t.n)
                nc.vector.wait_ge(st.h, st.n)
                v.inc(nc.vector.tensor_add(out=sout[:], in0=po[tt][:], in1=sxr[:]))
                nc.sync.wait_ge(v.h, v.n)
                st.inc(nc.sync.dma_start(
                    out=dst[tt * 128:(tt + 1) * 128, dm * 512:(dm + 1) * 512],
                    in_=sout[:]), 16)



def emit_block_l0attn(p, sems, xloc, xfull, qkvw, qkvb2, ow, ob, ones, eye, out,
                      debug=False, dbg=None):
    nc = p.nc
    qbuf = dbg["qbuf"] if debug else p.scratch(p.u("qbuf"), [H, TLOC], BF16)
    kbuf = dbg["kbuf"] if debug else p.scratch(p.u("kbuf"), [H, S], BF16)
    vbuf = dbg["vbuf"] if debug else p.scratch(p.u("vbuf"), [H, S], BF16)
    ctxd = dbg["ctxd"] if debug else None

    with block_scope(p):
        qs, qg, qc, a, v, t = (sems[k] for k in ["qs", "qg", "qc", "a", "v", "t"])

        sev = p.sb("sev", [128, 1024], BF16)
        sbias = p.sb("sbias", [128, 96], F32)
        sones = p.sb("sones", [1, 128], F32)
        sob = p.sb("sob", [1, H], F32)
        st1 = p.sb("st1", [128, 1], F32)
        st2 = p.sb("st2", [128, 1], F32)
        stm = p.sb("stm", [128, 1], F32)
        stv = p.sb("stv", [128, 1], F32)
        stb = p.sb("stb", [128, 1], F32)
        stats = (st1, st2, stm, stv, stb)
        sctx = p.sb("sctx", [128, 32 * 1024], BF16)
        p.seye = p.sb("seye", [128, 128], BF16)
        p.seye32 = p.sb("seye32", [128, 128], F32)

        qg.inc(nc.gpsimd.dma_start(out=p.seye[:], in_=eye), 16)
        qs.inc(nc.sync.dma_start(out=p.seye32[:], in_=eye), 16)

        with nc.psum_tensor("pb", [128, 128], F32) as pb, \
             nc.sbuf_tensor("sbq", [128, 128], F32) as sbq:
            qs.inc(nc.sync.dma_start(out=sbq[0:96, :], in_=qkvb2), 16)
            qs.inc(nc.sync.dma_start(out=sones[:], in_=ones), 16)
            qs.inc(nc.sync.dma_start(out=sob[:], in_=ob), 16)
            nc.vector.wait_ge(qs.h, qs.n)
            v.inc(nc.vector.memset(sbq[96:128, :], 0.0))
            nc.tensor.wait_ge(qs.h, qs.n)
            nc.tensor.wait_ge(v.h, v.n)
            nc.tensor.wait_ge(qs.h, qs.n)
            nc.tensor.wait_ge(qg.h, qg.n)
            t.inc(nc.tensor.transpose(pb[:], sbq[:], p.seye32[:]))
            nc.vector.wait_ge(t.h, t.n)
            v.inc(nc.vector.tensor_copy(out=sbias[:, 0:96], in_=pb[:, 0:96]))
            nc.tensor.wait_ge(v.h, v.n)

        barrier(p, sems)
        with ExitStack() as sc1:
            sx = p.sb("sx", [128, H], F32, sc1)
            sxn = p.sb("sxn", [128, H], BF16, sc1)
            xnfm = p.sb("xnfm", [128, 32 * 1024], BF16, sc1)
            wb0 = p.sb("wb0", [128, 32 * 128], BF16, sc1)
            wb1 = p.sb("wb1", [128, 32 * 128], BF16, sc1)
            with nc.psum_tensor("pt8a", [128, 1024], BF16) as pt8a, \
                 nc.psum_tensor("pt8b", [128, 1024], BF16) as pt8b, \
                 nc.psum_tensor("pq0", [128, 512], F32) as pq0, \
                 nc.psum_tensor("pq1", [128, 512], F32) as pq1:
                pq = [pq0, pq1]
                tks = [(0, 512), (512, 512)]

                def bias_of(oc):
                    return sbias[:, oc:oc + 1]

                def ev_q(tag, tki, sv_, tw):
                    sems["st"].inc(nc.sync.dma_start(
                        out=qbuf[tag * 128:(tag + 1) * 128,
                                 tki * 512:(tki + 1) * 512], in_=sv_), 16)

                emit_ln_span(p, sems, xloc, 0, 8, sx, sxn, stats, xnfm,
                             [pt8a, pt8b], 1024)
                nc.tensor.wait_ge(v.h, v.n)
                emit_proj(p, sems, qkvw, [(oc * 128, oc) for oc in range(32)],
                          xnfm, 1024, tks, [wb0, wb1], pq, sev, ev_q, bias_of)

                for sp in range(2):
                    def ev_kv(tag, tki, sv_, tw, sp=sp):
                        dst = kbuf if tag < 64 else vbuf
                        r = (tag - 32) if tag < 64 else (tag - 64)
                        sems["st"].inc(nc.sync.dma_start(
                            out=dst[r * 128:(r + 1) * 128,
                                    sp * 1024 + tki * 512:
                                    sp * 1024 + (tki + 1) * 512], in_=sv_), 16)
                    nc.sync.wait_ge(t.h, t.n)
                    emit_ln_span(p, sems, xfull, sp * 1024, 8, sx, sxn, stats,
                                 xnfm, [pt8a, pt8b], 1024)
                    nc.tensor.wait_ge(v.h, v.n)
                    emit_proj(p, sems, qkvw,
                              [(oc * 128, oc) for oc in range(32, 96)],
                              xnfm, 1024, tks, [wb0, wb1], pq, sev, ev_kv, bias_of)

        barrier(p, sems)
        with ExitStack() as sc2:
            emit_attention(p, sems, 8, 32, 1, qbuf, kbuf, vbuf, sctx, sc2, S)
        if debug:
            nc.sync.wait_ge(sems["a"].h, sems["a"].n)
            for hh in range(32):
                sems["qs"].inc(nc.sync.dma_start(
                    out=ctxd[hh * 128:(hh + 1) * 128, :],
                    in_=sctx[:, hh * 1024:(hh + 1) * 1024]), 16)

        barrier(p, sems)
        with ExitStack() as sc3:
            emit_outproj(p, sems, ow, sctx, 32, xloc, out, sob, sones, sc3)

        nc.sync.wait_ge(qs.h, qs.n)
        nc.sync.wait_ge(qg.h, qg.n)
        nc.sync.wait_ge(sems["st"].h, sems["st"].n)


def build_l0_attn(debug=False):
    p = Prog("l0attn")
    nc = p.nc
    xloc = p.inp("xloc", [TLOC, H], F32)
    xfull = p.inp("xfull", [S, H], F32)
    qkvw = p.inp("qkvw", [H, 3 * H], F32)
    qkvb2 = p.inp("qkvb2", [96, 128], F32)
    ow = p.inp("ow", [H, H], F32)
    ob = p.inp("ob", [1, H], F32)
    ones = p.inp("ones", [1, 128], F32)
    eye = p.inp("eye", [128, 128], F32)
    out = p.outp("out", [TLOC, H], F32)
    dbg = None
    if debug:
        dbg = dict(qbuf=p.outp("qbuf", [H, TLOC], BF16),
                   kbuf=p.outp("kbuf", [H, S], BF16),
                   vbuf=p.outp("vbuf", [H, S], BF16),
                   ctxd=p.outp("ctxd", [H, TLOC], BF16))
    with p.es:
        block = p.es.enter_context(nc.Block())
        sems = mk_sems(p, nc, "a")
        emit_block_l0attn(p, sems, xloc, xfull, qkvw, qkvb2, ow, ob, ones, eye,
                          out, debug, dbg)
    return p



_ONES = np.ones((1, 128), np.float32)
_EYE = np.eye(128, dtype=np.float32)


def _run_l0_attn(x, qkvw, qkvb, ow, ob, debug=False):
    p = build_l0_attn(debug)
    qkvb2 = np.ascontiguousarray(qkvb.reshape(96, 128))
    obr = np.ascontiguousarray(ob.reshape(1, H))
    in_maps = []
    for c in range(NCORE):
        b, h = c // 2, c % 2
        in_maps.append(dict(
            xloc=np.ascontiguousarray(x[b, h * TLOC:(h + 1) * TLOC]),
            xfull=np.ascontiguousarray(x[b]),
            qkvw=qkvw, qkvb2=qkvb2, ow=ow, ob=obr, ones=_ONES, eye=_EYE))
    res = run_bass_kernel_spmd(p.nc, in_maps, core_ids=list(range(NCORE)))
    out = np.empty((B, S, H), np.float32)
    for c in range(NCORE):
        b, h = c // 2, c % 2
        out[b, h * TLOC:(h + 1) * TLOC] = res.results[c]["out"]
    if debug:
        return out, res
    return out


def kernel(L0_attn_x, L0_attn_attn_qkvw, L0_attn_attn_qkvb, L0_attn_attn_ow,
           L0_attn_attn_ob, L0_ffn_qweight_gu, L0_ffn_qzeros_gu, L0_ffn_scales_gu,
           L0_ffn_qweight_dn, L0_ffn_qzeros_dn, L0_ffn_scales_dn, L1_attn_Wq,
           L1_attn_Wk, L1_attn_Wv, L1_attn_Wo, L1_ffn_qweight_gu, L1_ffn_qzeros_gu,
           L1_ffn_scales_gu, L1_ffn_g_idx_gu, L1_ffn_qweight_down,
           L1_ffn_qzeros_down, L1_ffn_scales_down, L1_ffn_g_idx_down):
    return kernel_fused(dict(
        L0_attn_x=L0_attn_x, L0_attn_attn_qkvw=L0_attn_attn_qkvw,
        L0_attn_attn_qkvb=L0_attn_attn_qkvb, L0_attn_attn_ow=L0_attn_attn_ow,
        L0_attn_attn_ob=L0_attn_attn_ob, L0_ffn_qweight_gu=L0_ffn_qweight_gu,
        L0_ffn_qzeros_gu=L0_ffn_qzeros_gu, L0_ffn_scales_gu=L0_ffn_scales_gu,
        L0_ffn_qweight_dn=L0_ffn_qweight_dn, L0_ffn_qzeros_dn=L0_ffn_qzeros_dn,
        L0_ffn_scales_dn=L0_ffn_scales_dn, L1_attn_Wq=L1_attn_Wq,
        L1_attn_Wk=L1_attn_Wk, L1_attn_Wv=L1_attn_Wv, L1_attn_Wo=L1_attn_Wo,
        L1_ffn_qweight_gu=L1_ffn_qweight_gu, L1_ffn_qzeros_gu=L1_ffn_qzeros_gu,
        L1_ffn_scales_gu=L1_ffn_scales_gu, L1_ffn_g_idx_gu=L1_ffn_g_idx_gu,
        L1_ffn_qweight_down=L1_ffn_qweight_down,
        L1_ffn_qzeros_down=L1_ffn_qzeros_down,
        L1_ffn_scales_down=L1_ffn_scales_down,
        L1_ffn_g_idx_down=L1_ffn_g_idx_down))



def emit_unpack_n(p, sems, packed_words, nib_u8, w):
    nc = p.nc
    v = sems["v"]
    pk8 = packed_words.bitcast(U8)
    nc.vector.tensor_scalar(out=nib_u8[:, 0:w:2], in0=pk8, scalar1=15,
                            scalar2=None, op0=ALU.bitwise_and)
    v.inc(nc.vector.tensor_scalar(out=nib_u8[:, 1:w:2], in0=pk8, scalar1=4,
                                  scalar2=None, op0=ALU.logical_shift_right))


def emit_block_l0ffn(p, sems, xin, qw_gu, qz_gu, sc_gu, qw_dn, qz_dn, sc_dn,
                     eye, out):
    nc = p.nc
    hgbuf = p.scratch(p.u("hgbuf"), [INTER, TLOC], BF16)
    hbuf = p.scratch(p.u("hbuf"), [INTER, TLOC], BF16)

    with block_scope(p):
        qs, qg, qc, a, v, t = (sems[k] for k in ["qs", "qg", "qc", "a", "v", "t"])

        p.seye = p.sb("seye", [128, 128], BF16)
        st1 = p.sb("st1", [128, 1], F32)
        st2 = p.sb("st2", [128, 1], F32)
        stm = p.sb("stm", [128, 1], F32)
        stv = p.sb("stv", [128, 1], F32)
        stb = p.sb("stb", [128, 1], F32)
        stats = (st1, st2, stm, stv, stb)
        xnfm = p.sb("xnfm", [128, 32 * 1024], BF16)
        pk_gu_s = p.sb("pk_gu_s", [32, 2752], I32)
        pk_dn_s = p.sb("pk_dn_s", [86, 512], I32)
        snz = p.sb("snz", [128, 512], U8)
        ssc2 = p.sb("ssc2", [128, 512], F32)
        sm2 = p.sb("sm2", [128, 512], BF16)
        sxg = p.sb("sxg", [32, TLOC], BF16)
        sxh = p.sb("sxh", [86, TLOC], BF16)
        sioc = p.sb("sioc", [128, 86], F32)
        sonehot = p.sb("sonehot", [128, 86], BF16)

        qg.inc(nc.gpsimd.dma_start(out=p.seye[:], in_=eye), 16)
        sems["g"].inc(nc.gpsimd.iota(out=sioc[:], pattern=[[1, 86]], base=0,
                                     channel_multiplier=0,
                                     allow_small_or_imprecise_dtypes=True))

        qs.inc(nc.sync.dma_start(out=pk_gu_s[:], in_=qz_gu), 16)
        qs.inc(nc.sync.dma_start(out=pk_dn_s[:], in_=qz_dn), 16)

        def emit_m2_slice(pk_sb, nrow, sc_src, c0, w):
            nc.scalar.wait_ge(v.h, v.n)
            qc.inc(nc.scalar.dma_start(out=ssc2[0:nrow, 0:w],
                                       in_=sc_src[:, c0:c0 + w]), 16)
            nc.vector.wait_ge(qc.h, qc.n)
            nc.vector.wait_ge(t.h, t.n)
            pk8 = pk_sb[0:nrow, c0 // 8:(c0 + w) // 8].bitcast(U8)
            nc.vector.tensor_scalar(out=snz[0:nrow, 0:w:2], in0=pk8, scalar1=15,
                                    scalar2=None, op0=ALU.bitwise_and)
            v.inc(nc.vector.tensor_scalar(out=snz[0:nrow, 1:w:2], in0=pk8,
                                          scalar1=4, scalar2=None,
                                          op0=ALU.logical_shift_right))
            nc.vector.wait_ge(v.h, v.n)
            v.inc(nc.vector.scalar_tensor_tensor(out=ssc2[0:nrow, 0:w],
                                                 in0=snz[0:nrow, 0:w],
                                                 scalar=8.0, in1=ssc2[0:nrow, 0:w],
                                                 op0=ALU.subtract, op1=ALU.mult))
            nc.vector.wait_ge(v.h, v.n)
            v.inc(nc.vector.tensor_scalar(out=sm2[0:nrow, 0:w],
                                          in0=ssc2[0:nrow, 0:w], scalar1=-1.0,
                                          scalar2=None, op0=ALU.mult))

        barrier(p, sems)
        with ExitStack() as sc1:
            sx = p.sb("sx", [128, H], F32, sc1)
            sxn = p.sb("sxn", [128, H], BF16, sc1)
            soh32 = p.sb("soh32", [128, 32], BF16, sc1)
            with nc.psum_tensor(p.u("pt8a"), [128, 1024], BF16) as pt8a, \
                 nc.psum_tensor(p.u("pt8b"), [128, 1024], BF16) as pt8b, \
                 nc.psum_tensor(p.u("pxg"), [32, 1024], F32) as pxg:
                emit_ln_span(p, sems, xin, 0, 8, sx, sxn, stats, xnfm,
                             [pt8a, pt8b], 1024)
                nc.vector.wait_ge(sems["g"].h, sems["g"].n)
                for kt in range(32):
                    nc.vector.wait_ge(t.h, t.n)
                    v.inc(nc.vector.tensor_scalar(out=soh32[:, 0:32],
                                                  in0=sioc[:, 0:32],
                                                  scalar1=float(kt), scalar2=None,
                                                  op0=ALU.is_equal))
                    nc.tensor.wait_ge(v.h, v.n)
                    for tk in range(2):
                        t.inc(nc.tensor.matmul(pxg[:, tk * 512:(tk + 1) * 512],
                                               soh32[:, 0:32],
                                               xnfm[:, kt * 1024 + tk * 512:
                                                    kt * 1024 + (tk + 1) * 512],
                                               start=(kt == 0), stop=(kt == 31)))
                nc.scalar.wait_ge(t.h, t.n)
                a.inc(nc.scalar.copy(out=sxg[:], in_=pxg[:]))
                nc.tensor.wait_ge(a.h, a.n)

        barrier(p, sems)
        with ExitStack() as sc2:
            spk = p.sb("spk", [128, 32 * 64], I32, sc2)
            spl0 = p.sb("spl0", [128, 8 * 512], F32, sc2)
            spl1 = p.sb("spl1", [128, 8 * 512], F32, sc2)
            snib = p.sb("snib", [128, 512], U8, sc2)
            sw = p.sb("sw", [128, 512], BF16, sc2)
            shg = p.sb("shg", [128, 512], BF16, sc2)
            shh = p.sb("shh", [128, 512], BF16, sc2)
            spl = [spl0, spl1]
            sw2 = [sw, p.sb("sw", [128, 512], BF16, sc2)]
            pls = [Sem(p_sem_raw(p, "plsA")), Sem(p_sem_raw(p, "plsB"))]
            plfree = [0, 0]
            swready = [0, 0]
            swfree = [0, 0]

            def gu_slab(j0, w, is_gate):
                nj = w // 128
                with ExitStack() as pes:
                    pg = [[p.psum(f"pg{jc}{tk}", [128, 512], F32, pes)
                           for tk in range(2)] for jc in range(nj)]
                    nc.sync.wait_ge(v.h, v.n)
                    qs.inc(nc.sync.dma_start(
                        out=spk[:, 0:32 * (w // 8)],
                        in_=qw_gu[:, j0 // 8:(j0 + w) // 8]
                        .rearrange("(kt p) c -> p kt c", p=128)), 16)
                    for ktg in range(4):
                        bnk = ktg % 2
                        nc.scalar.wait_ge(v.h, plfree[bnk])
                        pls[bnk].inc(nc.scalar.dma_start(
                            out=spl[bnk][:, 0:8 * w],
                            in_=sc_gu[ktg * 8:(ktg + 1) * 8, j0:j0 + w]
                            .unsqueeze(0).broadcast_to([128, 8, w])), 16)
                        for kj in range(8):
                            kt = ktg * 8 + kj
                            wbuf = kt % 2
                            nc.vector.wait_ge(qs.h, qs.n)
                            nc.vector.wait_ge(pls[bnk].h, pls[bnk].n)
                            nc.vector.wait_ge(t.h, swfree[wbuf])
                            nc.vector.wait_ge(v.h, v.n)
                            emit_unpack_n(p, sems,
                                          spk[:, kt * (w // 8):(kt + 1) * (w // 8)],
                                          snib, w)
                            nc.vector.wait_ge(v.h, v.n)
                            v.inc(nc.vector.scalar_tensor_tensor(
                                out=sw2[wbuf][:, 0:w], in0=snib[:, 0:w], scalar=8.0,
                                in1=spl[bnk][:, kj * w:(kj + 1) * w],
                                op0=ALU.subtract, op1=ALU.mult))
                            swready[wbuf] = v.n
                            nc.tensor.wait_ge(v.h, swready[wbuf])
                            for jc in range(nj):
                                for tk in range(2):
                                    t.inc(nc.tensor.matmul(
                                        pg[jc][tk][:],
                                        sw2[wbuf][:, jc * 128:(jc + 1) * 128],
                                        xnfm[:, kt * 1024 + tk * 512:
                                             kt * 1024 + (tk + 1) * 512],
                                        start=(kt == 0), stop=False))
                            swfree[wbuf] = t.n
                            if kj == 7:
                                plfree[bnk] = v.n
                    emit_m2_slice(pk_gu_s, 32, sc_gu, j0, w)
                    nc.tensor.wait_ge(v.h, v.n)
                    for jc in range(nj):
                        for tk in range(2):
                            t.inc(nc.tensor.matmul(
                                pg[jc][tk][:],
                                sm2[0:32, jc * 128:(jc + 1) * 128],
                                sxg[:, tk * 512:(tk + 1) * 512],
                                start=False, stop=True))
                    for jc in range(nj):
                        jj = j0 + jc * 128
                        for tk in range(2):
                            if is_gate:
                                nc.scalar.wait_ge(t.h, t.n)
                                nc.scalar.wait_ge(qs.h, qs.n)
                                a.inc(nc.scalar.activation(out=shg[:],
                                                           in_=pg[jc][tk][:],
                                                           func=AF.Silu))
                                nc.sync.wait_ge(a.h, a.n)
                                qs.inc(nc.sync.dma_start(
                                    out=hgbuf[jj:jj + 128,
                                              tk * 512:(tk + 1) * 512],
                                    in_=shg[:]), 16)
                            else:
                                jp = jj - INTER
                                nc.scalar.wait_ge(qs.h, qs.n)
                                qc.inc(nc.scalar.dma_start(
                                    out=shg[:],
                                    in_=hgbuf[jp:jp + 128,
                                              tk * 512:(tk + 1) * 512]), 16)
                                nc.vector.wait_ge(qc.h, qc.n)
                                nc.vector.wait_ge(t.h, t.n)
                                nc.vector.wait_ge(qs.h, qs.n)
                                v.inc(nc.vector.tensor_tensor(
                                    out=sw[:, 0:512], in0=pg[jc][tk][:],
                                    in1=shg[:], op=ALU.mult))
                                nc.sync.wait_ge(v.h, v.n)
                                qs.inc(nc.sync.dma_start(
                                    out=hbuf[jp:jp + 128,
                                             tk * 512:(tk + 1) * 512],
                                    in_=sw[:, 0:512]), 16)
                    nc.tensor.wait_ge(a.h, a.n)
                    nc.tensor.wait_ge(v.h, v.n)

            slabs = [(i * 512, 512) for i in range(21)] + [(10752, 256)]
            for j0, w in slabs:
                gu_slab(j0, w, True)
            for j0, w in slabs:
                gu_slab(INTER + j0, w, False)

        barrier(p, sems)
        with ExitStack() as sc3:
            sht = p.sb("sht", [128, TLOC], BF16, sc3)
            with nc.psum_tensor(p.u("pxh"), [86, 1024], F32) as pxh:
                nc.sync.wait_ge(qs.h, qs.n)
                nc.sync.wait_ge(sems["st"].h, sems["st"].n)
                for g in range(86):
                    nc.sync.wait_ge(t.h, t.n)
                    qs.inc(nc.sync.dma_start(out=sht[:],
                                             in_=hbuf[g * 128:(g + 1) * 128, :]), 16)
                    nc.vector.wait_ge(t.h, t.n)
                    nc.vector.wait_ge(sems["g"].h, sems["g"].n)
                    v.inc(nc.vector.tensor_scalar(out=sonehot[:], in0=sioc[:],
                                                  scalar1=float(g), scalar2=None,
                                                  op0=ALU.is_equal))
                    nc.tensor.wait_ge(v.h, v.n)
                    nc.tensor.wait_ge(qs.h, qs.n)
                    for tk in range(2):
                        t.inc(nc.tensor.matmul(pxh[:, tk * 512:(tk + 1) * 512],
                                               sonehot[:],
                                               sht[:, tk * 512:(tk + 1) * 512],
                                               start=(g == 0), stop=(g == 85)))
                nc.scalar.wait_ge(t.h, t.n)
                a.inc(nc.scalar.copy(out=sxh[:], in_=pxh[:]))
                nc.tensor.wait_ge(a.h, a.n)

        barrier(p, sems)
        with ExitStack() as sc4:
            spkd = p.sb("spkd", [128, 64], I32, sc4)
            spld = p.sb("spld", [128, 512], F32, sc4)
            snibd = p.sb("snibd", [128, 512], U8, sc4)
            swd0 = p.sb("swd0", [128, 512], BF16, sc4)
            swd1 = p.sb("swd1", [128, 512], BF16, sc4)
            swd = [swd0, swd1]
            sht0 = p.sb("sht0", [128, TLOC], BF16, sc4)
            sht1 = p.sb("sht1", [128, TLOC], BF16, sc4)
            shts = [sht0, sht1]
            sxr = p.sb("sxr", [128, 512], F32, sc4)
            sout = p.sb("sout", [128, 512], F32, sc4)
            hs = [Sem(p_sem_raw(p, "hsA")), Sem(p_sem_raw(p, "hsB"))]
            pks = Sem(p_sem_raw(p, "pks"))
            pls2 = Sem(p_sem_raw(p, "pls2"))
            with ExitStack() as pes:
                pd = [p.psum(f"pd{i}", [128, 512], F32, pes) for i in range(8)]
                wdready = [0, 0]
                wdfree = [0, 0]
                for dm in range(8):
                    hfree = [t.n, t.n]
                    for kt in range(86):
                        buf = kt % 2
                        nc.sync.wait_ge(t.h, hfree[buf])
                        hs[buf].inc(nc.sync.dma_start(
                            out=shts[buf][:],
                            in_=hbuf[kt * 128:(kt + 1) * 128, :]), 16)
                        nc.scalar.wait_ge(v.h, v.n)
                        pks.inc(nc.scalar.dma_start(
                            out=spkd[:],
                            in_=qw_dn[kt * 128:(kt + 1) * 128,
                                      dm * 64:(dm + 1) * 64]), 16)
                        pls2.inc(nc.scalar.dma_start(
                            out=spld[:],
                            in_=sc_dn[kt:kt + 1, dm * 512:(dm + 1) * 512]
                            .partition_broadcast(128)), 16)
                        nc.vector.wait_ge(pks.h, pks.n)
                        nc.vector.wait_ge(pls2.h, pls2.n)
                        nc.vector.wait_ge(t.h, wdfree[buf])
                        nc.vector.wait_ge(v.h, v.n)
                        emit_unpack_n(p, sems, spkd[:], snibd, 512)
                        nc.vector.wait_ge(v.h, v.n)
                        v.inc(nc.vector.scalar_tensor_tensor(
                            out=swd[buf][:], in0=snibd[:], scalar=8.0,
                            in1=spld[:], op0=ALU.subtract, op1=ALU.mult))
                        wdready[buf] = v.n
                        nc.tensor.wait_ge(v.h, wdready[buf])
                        nc.tensor.wait_ge(hs[buf].h, hs[buf].n)
                        if kt == 0:
                            nc.tensor.wait_ge(a.h, a.n)
                            nc.tensor.wait_ge(qg.h, qg.n)
                        for tt in range(8):
                            t.inc(nc.tensor.matmul(
                                pd[tt][:],
                                shts[buf][:, tt * 128:(tt + 1) * 128],
                                swd[buf][:], start=(kt == 0), stop=False))
                        hfree[buf] = t.n
                        wdfree[buf] = t.n
                    emit_m2_slice(pk_dn_s, 86, sc_dn, dm * 512, 512)
                    nc.tensor.wait_ge(v.h, v.n)
                    for tt in range(8):
                        t.inc(nc.tensor.matmul(
                            pd[tt][:], sxh[:, tt * 128:(tt + 1) * 128],
                            sm2[0:86, 0:512],
                            start=False, stop=True))
                    for tt in range(8):
                        nc.gpsimd.wait_ge(v.h, v.n)
                        qg.inc(nc.gpsimd.dma_start(
                            out=sxr[:],
                            in_=xin[tt * 128:(tt + 1) * 128,
                                    dm * 512:(dm + 1) * 512]), 16)
                        nc.vector.wait_ge(qg.h, qg.n)
                        nc.vector.wait_ge(t.h, t.n)
                        v.inc(nc.vector.tensor_add(out=sout[:], in0=pd[tt][:],
                                                   in1=sxr[:]))
                        nc.gpsimd.wait_ge(v.h, v.n)
                        qg.inc(nc.gpsimd.dma_start(
                            out=out[tt * 128:(tt + 1) * 128,
                                    dm * 512:(dm + 1) * 512], in_=sout[:]), 16)
                        nc.vector.wait_ge(qg.h, qg.n)

        nc.sync.wait_ge(qs.h, qs.n)
        nc.sync.wait_ge(qg.h, qg.n)
        nc.sync.wait_ge(qc.h, qc.n)
        nc.sync.wait_ge(sems["st"].h, sems["st"].n)


def build_l0_ffn():
    p = Prog("l0ffn")
    nc = p.nc
    xin = p.inp("xin", [TLOC, H], F32)
    qw_gu = p.inp("qw_gu", [H, 2752], I32)
    qz_gu = p.inp("qz_gu", [32, 2752], I32)
    sc_gu = p.inp("sc_gu", [32, 2 * INTER], F32)
    qw_dn = p.inp("qw_dn", [INTER, 512], I32)
    qz_dn = p.inp("qz_dn", [86, 512], I32)
    sc_dn = p.inp("sc_dn", [86, H], F32)
    eye = p.inp("eye", [128, 128], F32)
    out = p.outp("out", [TLOC, H], F32)
    with p.es:
        block = p.es.enter_context(nc.Block())
        sems = mk_sems(p, nc, "b")
        emit_block_l0ffn(p, sems, xin, qw_gu, qz_gu, sc_gu, qw_dn, qz_dn, sc_dn,
                         eye, out)
    return p


def _run_l0_ffn(x1, qw_gu, qz_gu, sc_gu, qw_dn, qz_dn, sc_dn):
    p = build_l0_ffn()
    in_maps = []
    for c in range(NCORE):
        b, h = c // 2, c % 2
        in_maps.append(dict(
            xin=np.ascontiguousarray(x1[b, h * TLOC:(h + 1) * TLOC]),
            qw_gu=qw_gu, qz_gu=qz_gu, sc_gu=sc_gu,
            qw_dn=qw_dn, qz_dn=qz_dn, sc_dn=sc_dn, eye=_EYE))
    res = run_bass_kernel_spmd(p.nc, in_maps, core_ids=list(range(NCORE)))
    out = np.empty((B, S, H), np.float32)
    for c in range(NCORE):
        b, h = c // 2, c % 2
        out[b, h * TLOC:(h + 1) * TLOC] = res.results[c]["out"]
    return out



ROPE_C = 12582912.0


def emit_block_l1attn(p, sems, xloc, xfull, wq, wk, wv, wo, eye, invf, rotm,
                      posoff, out, span_off=None, debug=False, dbg=None):
    nc = p.nc
    qbuf = dbg["qbuf"] if debug else p.scratch(p.u("qbuf"), [H, TLOC], BF16)
    kbuf = dbg["kbuf"] if debug else p.scratch(p.u("kbuf"), [1024, S], BF16)
    vbuf = dbg["vbuf"] if debug else p.scratch(p.u("vbuf"), [1024, S], BF16)
    wtbuf = dbg["wtbuf"] if debug else p.scratch(p.u("wtbuf"), [H, 10240], BF16)
    ctxd = dbg["ctxd"] if debug else None
    tabd = dbg["tabd"] if debug else None

    with block_scope(p):
        qs, qg, qc, a, v, t = (sems[k] for k in ["qs", "qg", "qc", "a", "v", "t"])

        p.seye = p.sb("seye", [128, 128], BF16)
        srotm = p.sb("srotm", [128, 128], BF16)
        sinvf = p.sb("sinvf", [128, 1], F32)
        soff = p.sb("soff", [128, 1], F32)
        st1 = p.sb("st1", [128, 1], F32)
        st2 = p.sb("st2", [128, 1], F32)
        stm = p.sb("stm", [128, 1], F32)
        stv = p.sb("stv", [128, 1], F32)
        stb = p.sb("stb", [128, 1], F32)
        stats = (st1, st2, stm, stv, stb)
        scos = p.sb("scos", [128, 1024], F32)
        ssin = p.sb("ssin", [128, 1024], F32)
        sang = p.sb("sang", [128, 1024], F32)
        srnd = p.sb("srnd", [128, 1024], F32)
        sctx = p.sb("sctx", [128, 32 * 1024], BF16)

        qg.inc(nc.gpsimd.dma_start(out=p.seye[:], in_=eye), 16)
        qg.inc(nc.gpsimd.dma_start(out=srotm[:], in_=rotm), 16)
        qs.inc(nc.sync.dma_start(out=sinvf[:], in_=invf), 16)
        qs.inc(nc.sync.dma_start(out=soff[:], in_=posoff.partition_broadcast(128)), 16)

        def rope_tables(off_ap, off_imm):
            def vstep(i):
                v.inc(i)
                nc.vector.wait_ge(v.h, v.n)
            nc.gpsimd.wait_ge(v.h, v.n)
            sems["g"].inc(nc.gpsimd.iota(out=sang[:], pattern=[[1, 1024]], base=0,
                                         channel_multiplier=0,
                                         allow_small_or_imprecise_dtypes=True))
            nc.vector.wait_ge(sems["g"].h, sems["g"].n)
            nc.vector.wait_ge(a.h, a.n)
            if off_ap is not None:
                vstep(nc.vector.tensor_scalar(out=sang[:], in0=sang[:],
                                              scalar1=off_ap, scalar2=None,
                                              op0=ALU.add))
            elif off_imm != 0.0:
                vstep(nc.vector.tensor_scalar(out=sang[:], in0=sang[:],
                                              scalar1=float(off_imm), scalar2=None,
                                              op0=ALU.add))
            vstep(nc.vector.tensor_scalar(out=sang[:], in0=sang[:],
                                          scalar1=sinvf[:, 0:1],
                                          scalar2=1.0 / (2 * np.pi),
                                          op0=ALU.mult, op1=ALU.mult))
            vstep(nc.vector.tensor_scalar(out=srnd[:], in0=sang[:], scalar1=ROPE_C,
                                          scalar2=ROPE_C, op0=ALU.add,
                                          op1=ALU.subtract))
            vstep(nc.vector.tensor_tensor(out=srnd[:], in0=sang[:], in1=srnd[:],
                                          op=ALU.subtract))
            vstep(nc.vector.tensor_scalar(out=srnd[:], in0=srnd[:],
                                          scalar1=2 * np.pi, scalar2=None,
                                          op0=ALU.mult))
            nc.scalar.wait_ge(v.h, v.n)
            a.inc(nc.scalar.activation(out=ssin[:], in_=srnd[:], func=AF.Sin))
            nc.vector.wait_ge(a.h, a.n)
            vstep(nc.vector.tensor_scalar(out=sang[:], in0=sang[:], scalar1=0.25,
                                          scalar2=None, op0=ALU.add))
            vstep(nc.vector.tensor_scalar(out=srnd[:], in0=sang[:], scalar1=ROPE_C,
                                          scalar2=ROPE_C, op0=ALU.add,
                                          op1=ALU.subtract))
            vstep(nc.vector.tensor_tensor(out=srnd[:], in0=sang[:], in1=srnd[:],
                                          op=ALU.subtract))
            vstep(nc.vector.tensor_scalar(out=srnd[:], in0=srnd[:],
                                          scalar1=2 * np.pi, scalar2=None,
                                          op0=ALU.mult))
            nc.scalar.wait_ge(v.h, v.n)
            a.inc(nc.scalar.activation(out=scos[:], in_=srnd[:], func=AF.Sin))
            nc.vector.wait_ge(a.h, a.n)

        barrier(p, sems)
        with ExitStack() as sc0:
            swr = p.sb("swr", [128, H], BF16, sc0)
            swt = p.sb("swt", [128, 32 * 128], BF16, sc0)
            with nc.psum_tensor(p.u("pt8a"), [128, 1024], BF16) as pt8a, \
                 nc.psum_tensor(p.u("pt8b"), [128, 1024], BF16) as pt8b:
                pt8 = [pt8a, pt8b]
                srcs = [(wq, 32, 0), (wk, 8, 4096), (wv, 8, 5120), (wo, 32, 6144)]
                for wsrc, noc, colbase in srcs:
                    for oc in range(noc):
                        nc.gpsimd.wait_ge(t.h, t.n)
                        qg.inc(nc.gpsimd.dma_start(
                            out=swr[:], in_=wsrc[oc * 128:(oc + 1) * 128, :]), 16)
                        nc.tensor.wait_ge(qg.h, qg.n)
                        evfree = [v.n, v.n]
                        stored = sems["st"].n
                        for g in range(4):
                            bnk = g % 2
                            nc.tensor.wait_ge(v.h, evfree[bnk])
                            for j in range(8):
                                kt = g * 8 + j
                                t.inc(nc.tensor.matmul(
                                    pt8[bnk][:, j * 128:(j + 1) * 128],
                                    swr[:, kt * 128:(kt + 1) * 128],
                                    p.seye[:], is_transpose=True,
                                    start=(j == 0), stop=(j == 7)))
                            nc.vector.wait_ge(t.h, t.n)
                            nc.vector.wait_ge(sems["st"].h, stored)
                            v.inc(nc.vector.tensor_copy(
                                out=swt[:, g * 1024:(g + 1) * 1024],
                                in_=pt8[bnk][:]))
                            evfree[bnk] = v.n
                        nc.sync.wait_ge(v.h, v.n)
                        sems["st"].inc(nc.sync.dma_start(
                            out=wtbuf.rearrange("(kt pp) o -> pp kt o", pp=128)
                            [:, :, colbase + oc * 128:colbase + (oc + 1) * 128],
                            in_=swt[:].rearrange("pp (kt o) -> pp kt o", o=128)), 16)

        barrier(p, sems)
        with ExitStack() as sc1:
            sx = p.sb("sx", [128, H], F32, sc1)
            sxn = p.sb("sxn", [128, H], BF16, sc1)
            xnfm = p.sb("xnfm", [128, 32 * 1024], BF16, sc1)
            wb0 = p.sb("wb0", [128, 32 * 128], BF16, sc1)
            wb1 = p.sb("wb1", [128, 32 * 128], BF16, sc1)
            sev = p.sb("sev", [128, 1024], BF16, sc1)
            sev2 = p.sb("sev2", [128, 512], BF16, sc1)
            sro1 = p.sb("sro1", [128, 512], F32, sc1)
            sro2 = p.sb("sro2", [128, 512], F32, sc1)
            with nc.psum_tensor(p.u("pt8a"), [128, 1024], BF16) as pt8a, \
                 nc.psum_tensor(p.u("pt8b"), [128, 1024], BF16) as pt8b, \
                 nc.psum_tensor(p.u("pq0"), [128, 512], F32) as pq0, \
                 nc.psum_tensor(p.u("pq1"), [128, 512], F32) as pq1, \
                 nc.psum_tensor(p.u("prot"), [128, 512], F32) as prot:
                pq = [pq0, pq1]
                tks = [(0, 512), (512, 512)]

                def rope_store(dst, row0, col0):
                    def f(tag, tki, sev_sl, tw):
                        nc.tensor.wait_ge(a.h, a.n)
                        nc.tensor.wait_ge(v.h, v.n)
                        t.inc(nc.tensor.matmul(prot[:], srotm[:], sev_sl,
                                               start=True, stop=True))
                        nc.vector.wait_ge(t.h, t.n)
                        v.inc(nc.vector.tensor_tensor(
                            out=sro1[:], in0=sev_sl, in1=scos[:, tki * 512:(tki + 1) * 512],
                            op=ALU.mult))
                        v.inc(nc.vector.tensor_tensor(
                            out=sro2[:], in0=prot[:], in1=ssin[:, tki * 512:(tki + 1) * 512],
                            op=ALU.mult))
                        nc.vector.wait_ge(v.h, v.n)
                        nc.vector.wait_ge(sems["st"].h, sems["st"].n)
                        v.inc(nc.vector.tensor_add(out=sev2[:], in0=sro1[:],
                                                   in1=sro2[:]))
                        nc.sync.wait_ge(v.h, v.n)
                        sems["st"].inc(nc.sync.dma_start(
                            out=dst[row0 + tag * 128:row0 + (tag + 1) * 128,
                                    col0 + tki * 512:col0 + (tki + 1) * 512],
                            in_=sev2[:]), 16)
                    return f

                def plain_store(dst, row0, col0):
                    def f(tag, tki, sev_sl, tw):
                        sems["st"].inc(nc.sync.dma_start(
                            out=dst[row0 + tag * 128:row0 + (tag + 1) * 128,
                                    col0 + tki * 512:col0 + (tki + 1) * 512],
                            in_=sev_sl), 16)
                    return f

                wt_ap = wtbuf

                nc.gpsimd.wait_ge(qs.h, qs.n)
                nc.gpsimd.wait_ge(sems["st"].h, sems["st"].n)
                emit_ln_span(p, sems, xloc, 0, 8, sx, sxn, stats, xnfm,
                             [pt8a, pt8b], 1024)
                rope_tables(soff[:, 0:1], 0.0)
                if debug:
                    nc.sync.wait_ge(a.h, a.n)
                    qs.inc(nc.sync.dma_start(out=tabd[:, 0:1024], in_=scos[:]), 16)
                    qs.inc(nc.sync.dma_start(out=tabd[:, 1024:2048], in_=ssin[:]), 16)
                nc.tensor.wait_ge(v.h, v.n)
                emit_proj(p, sems, wt_ap, [(oc * 128, oc) for oc in range(32)],
                          xnfm, 1024, tks, [wb0, wb1], pq, sev,
                          rope_store(qbuf, 0, 0), None)

                for sp in range(2):
                    nc.sync.wait_ge(t.h, t.n)
                    emit_ln_span(p, sems, xfull, sp * 1024, 8, sx, sxn, stats,
                                 xnfm, [pt8a, pt8b], 1024)
                    if span_off is None:
                        rope_tables(None, float(sp * 1024))
                    else:
                        rope_tables(span_off(sp), 0.0)
                    nc.tensor.wait_ge(v.h, v.n)
                    emit_proj(p, sems, wt_ap,
                              [(4096 + oc * 128, oc) for oc in range(8)],
                              xnfm, 1024, tks, [wb0, wb1], pq, sev,
                              rope_store(kbuf, 0, sp * 1024), None)
                    emit_proj(p, sems, wt_ap,
                              [(5120 + oc * 128, oc) for oc in range(8)],
                              xnfm, 1024, tks, [wb0, wb1], pq, sev,
                              plain_store(vbuf, 0, sp * 1024), None)

        barrier(p, sems)
        with ExitStack() as sc2:
            emit_attention(p, sems, 8, 8, 4, qbuf, kbuf, vbuf, sctx, sc2, S)
        if debug:
            nc.sync.wait_ge(sems["a"].h, sems["a"].n)
            for hh in range(32):
                qs.inc(nc.sync.dma_start(
                    out=ctxd[hh * 128:(hh + 1) * 128, :],
                    in_=sctx[:, hh * 1024:(hh + 1) * 1024]), 16)

        barrier(p, sems)
        with ExitStack() as sc3:
            emit_outproj(p, sems, wtbuf[:, 6144:10240], sctx, 32, xloc, out,
                         None, None, sc3)

        nc.sync.wait_ge(qs.h, qs.n)
        nc.sync.wait_ge(qg.h, qg.n)
        nc.sync.wait_ge(qc.h, qc.n)
        nc.sync.wait_ge(sems["st"].h, sems["st"].n)


def build_l1_attn(debug=False):
    p = Prog("l1attn")
    nc = p.nc
    xloc = p.inp("xloc", [TLOC, H], F32)
    xfull = p.inp("xfull", [S, H], F32)
    wq = p.inp("wq", [H, H], F32)
    wk = p.inp("wk", [1024, H], F32)
    wv = p.inp("wv", [1024, H], F32)
    wo = p.inp("wo", [H, H], F32)
    eye = p.inp("eye", [128, 128], F32)
    invf = p.inp("invf", [128, 1], F32)
    rotm = p.inp("rotm", [128, 128], F32)
    posoff = p.inp("posoff", [1, 1], F32)
    out = p.outp("out", [TLOC, H], F32)
    dbg = None
    if debug:
        dbg = dict(qbuf=p.outp("qbuf", [H, TLOC], BF16),
                   kbuf=p.outp("kbuf", [1024, S], BF16),
                   vbuf=p.outp("vbuf", [1024, S], BF16),
                   wtbuf=p.outp("wtbuf", [H, 10240], BF16),
                   ctxd=p.outp("ctxd", [H, TLOC], BF16),
                   tabd=p.outp("tabd", [128, 2048], F32))
    with p.es:
        block = p.es.enter_context(nc.Block())
        sems = mk_sems(p, nc, "c")
        emit_block_l1attn(p, sems, xloc, xfull, wq, wk, wv, wo, eye, invf, rotm,
                          posoff, out, None, debug, dbg)
    return p


_INVF = (1.0 / (500000.0 ** (np.arange(128, dtype=np.float32) % 64 * 2.0 / 128.0))
         ).reshape(128, 1).astype(np.float32)
_ROTM = np.zeros((128, 128), np.float32)
_ROTM[np.arange(64) + 64, np.arange(64)] = -1.0
_ROTM[np.arange(64), np.arange(64) + 64] = 1.0


def _run_l1_attn(x2, wq, wk, wv, wo, debug=False):
    p = build_l1_attn(debug)
    in_maps = []
    for c in range(NCORE):
        b, h = c // 2, c % 2
        in_maps.append(dict(
            xloc=np.ascontiguousarray(x2[b, h * TLOC:(h + 1) * TLOC]),
            xfull=np.ascontiguousarray(x2[b]),
            wq=wq, wk=wk, wv=wv, wo=wo, eye=_EYE, invf=_INVF, rotm=_ROTM,
            posoff=np.full((1, 1), h * TLOC, np.float32)))
    res = run_bass_kernel_spmd(p.nc, in_maps, core_ids=list(range(NCORE)))
    out = np.empty((B, S, H), np.float32)
    for c in range(NCORE):
        b, h = c // 2, c % 2
        out[b, h * TLOC:(h + 1) * TLOC] = res.results[c]["out"]
    if debug:
        return out, res
    return out



def emit_block_l1ffn(p, sems, xin, qw_gu, qz_gu, sc_gu, gix_gu, qw_dn, qz_dn,
                     sc_dn, gix_dn, shifts, eye, out):
    nc = p.nc
    hgbuf = p.scratch(p.u("hgbuf"), [INTER, TLOC], BF16)
    hbuf = p.scratch(p.u("hbuf"), [INTER, TLOC], BF16)
    scsl_gu = p.scratch(p.u("scsl_gu"), [86, 32, 256], F32)
    scsl_dn = p.scratch(p.u("scsl_dn"), [16, 86, 256], F32)

    with block_scope(p):
        qs, qg, qc, a, v, t = (sems[k] for k in ["qs", "qg", "qc", "a", "v", "t"])

        p.seye = p.sb("seye", [128, 128], BF16)
        st1 = p.sb("st1", [128, 1], F32)
        st2 = p.sb("st2", [128, 1], F32)
        stm = p.sb("stm", [128, 1], F32)
        stv = p.sb("stv", [128, 1], F32)
        stb = p.sb("stb", [128, 1], F32)
        stats = (st1, st2, stm, stv, stb)
        xnfm = p.sb("xnfm", [128, 32 * 1024], BF16)
        pk_gu_s = p.sb("pk_gu_s", [32, 2752], I32)
        pk_dn_s = p.sb("pk_dn_s", [86, 512], I32)
        snz = p.sb("snz", [128, 512], U8)
        ssc2 = p.sb("ssc2", [128, 512], F32)
        sm2 = p.sb("sm2", [128, 512], BF16)
        sxg = p.sb("sxg", [32, TLOC], BF16)
        sxh = p.sb("sxh", [86, TLOC], BF16)
        sioc = p.sb("sioc", [128, 86], F32)
        sonehot = p.sb("sonehot", [128, 86], BF16)
        sshift = p.sb("sshift", [128, 1], I32)
        sgix = p.sb("sgix", [128, 1], I32)
        sgixf = p.sb("sgixf", [128, 1], F32)

        qg.inc(nc.gpsimd.dma_start(out=p.seye[:], in_=eye), 16)
        qs.inc(nc.sync.dma_start(out=sshift[:], in_=shifts), 16)
        sems["g"].inc(nc.gpsimd.iota(out=sioc[:], pattern=[[1, 86]], base=0,
                                     channel_multiplier=0,
                                     allow_small_or_imprecise_dtypes=True))
        qs.inc(nc.sync.dma_start(out=pk_gu_s[:], in_=qz_gu.bitcast(I32)), 16)
        qs.inc(nc.sync.dma_start(out=pk_dn_s[:], in_=qz_dn.bitcast(I32)), 16)
        qs.inc(nc.sync.dma_start(out=scsl_gu,
                                 in_=sc_gu.rearrange("g (s w) -> s g w", w=256)), 16)
        qs.inc(nc.sync.dma_start(out=scsl_dn,
                                 in_=sc_dn.rearrange("g (s w) -> s g w", w=256)), 16)
        sc_gu_flat = scsl_gu.rearrange("s g w -> (s g) w")
        sc_dn_flat = scsl_dn.rearrange("s g w -> (s g) w")
        nc.gpsimd.wait_ge(qs.h, qs.n)

        def emit_m2_slice_d(pk_sb, nrow, sc_src, c0, w):
            nc.scalar.wait_ge(v.h, v.n)
            qc.inc(nc.scalar.dma_start(out=ssc2[0:nrow, 0:w],
                                       in_=sc_src[:, c0:c0 + w]), 16)
            nc.vector.wait_ge(qc.h, qc.n)
            nc.vector.wait_ge(t.h, t.n)
            pk8 = pk_sb[0:nrow, c0 // 8:(c0 + w) // 8].bitcast(U8)
            nc.vector.tensor_scalar(out=snz[0:nrow, 0:w:2], in0=pk8, scalar1=15,
                                    scalar2=None, op0=ALU.bitwise_and)
            v.inc(nc.vector.tensor_scalar(out=snz[0:nrow, 1:w:2], in0=pk8,
                                          scalar1=4, scalar2=None,
                                          op0=ALU.logical_shift_right))
            nc.vector.wait_ge(v.h, v.n)
            v.inc(nc.vector.scalar_tensor_tensor(out=ssc2[0:nrow, 0:w],
                                                 in0=snz[0:nrow, 0:w],
                                                 scalar=7.0, in1=ssc2[0:nrow, 0:w],
                                                 op0=ALU.subtract, op1=ALU.mult))
            nc.vector.wait_ge(v.h, v.n)
            v.inc(nc.vector.tensor_scalar(out=sm2[0:nrow, 0:w],
                                          in0=ssc2[0:nrow, 0:w], scalar1=-1.0,
                                          scalar2=None, op0=ALU.mult))

        def onehot_gidx(gix_src, k0, ncols):
            nc.sync.wait_ge(v.h, v.n)
            qs.inc(nc.sync.dma_start(out=sgix[:], in_=gix_src[k0:k0 + 128, :]), 16)
            nc.vector.wait_ge(qs.h, qs.n)
            v.inc(nc.vector.tensor_copy(out=sgixf[:], in_=sgix[:]))
            nc.vector.wait_ge(v.h, v.n)
            nc.vector.wait_ge(t.h, t.n)
            nc.vector.wait_ge(sems["g"].h, sems["g"].n)
            v.inc(nc.vector.tensor_scalar(out=sonehot[:, 0:ncols],
                                          in0=sioc[:, 0:ncols],
                                          scalar1=sgixf[:, 0:1], scalar2=None,
                                          op0=ALU.is_equal))

        barrier(p, sems)
        with ExitStack() as sc1:
            sx = p.sb("sx", [128, H], F32, sc1)
            sxn = p.sb("sxn", [128, H], BF16, sc1)
            with nc.psum_tensor(p.u("pt8a"), [128, 1024], BF16) as pt8a, \
                 nc.psum_tensor(p.u("pt8b"), [128, 1024], BF16) as pt8b, \
                 nc.psum_tensor(p.u("pxg"), [32, 1024], F32) as pxg:
                emit_ln_span(p, sems, xin, 0, 8, sx, sxn, stats, xnfm,
                             [pt8a, pt8b], 1024)
                for kt in range(32):
                    onehot_gidx(gix_gu, kt * 128, 32)
                    nc.tensor.wait_ge(v.h, v.n)
                    for tk in range(2):
                        t.inc(nc.tensor.matmul(pxg[:, tk * 512:(tk + 1) * 512],
                                               sonehot[:, 0:32],
                                               xnfm[:, kt * 1024 + tk * 512:
                                                    kt * 1024 + (tk + 1) * 512],
                                               start=(kt == 0), stop=(kt == 31)))
                nc.scalar.wait_ge(t.h, t.n)
                a.inc(nc.scalar.copy(out=sxg[:], in_=pxg[:]))
                nc.tensor.wait_ge(a.h, a.n)

        barrier(p, sems)
        sgix2 = [p.sb("sgix2", [128, 1], I32) for _ in range(2)]
        with ExitStack() as sc2:
            spk = [p.sb("spk", [128, 512], I32, sc2) for _ in range(2)]
            spl = [p.sb("spl", [128, 512], F32, sc2) for _ in range(2)]
            snib = p.sb("snib", [128, 512], I32, sc2)
            sw = p.sb("sw", [128, 512], BF16, sc2)
            shg = p.sb("shg", [128, 512], BF16, sc2)
            shh = p.sb("shh", [128, 512], BF16, sc2)
            sw2 = [sw, p.sb("sw", [128, 512], BF16, sc2)]
            lds = [Sem(p_sem_raw(p, "ldsA")), Sem(p_sem_raw(p, "ldsB"))]
            ldfree = [0, 0]
            swready = [0, 0]
            swfree = [0, 0]

            def gu_slab_d(j0, w, is_gate):
                nj = w // 128
                with ExitStack() as pes:
                    pg = [[p.psum(f"pg{jc}{tk}", [128, 512], F32, pes)
                           for tk in range(2)] for jc in range(nj)]

                    def issue_load(kt, buf):
                        nc.gpsimd.wait_ge(v.h, ldfree[buf])
                        lds[buf].inc(nc.gpsimd.dma_start(
                            out=spk[buf][:, 0:w],
                            in_=qw_gu.bitcast(I32)[kt * 16:(kt + 1) * 16, j0:j0 + w]
                            .unsqueeze(1).broadcast_to([16, 8, w])), 16)
                        lds[buf].inc(nc.gpsimd.dma_start(
                            out=sgix2[buf][:],
                            in_=gix_gu[kt * 128:(kt + 1) * 128, :]), 16)
                        nc.gpsimd.wait_ge(lds[buf].h, lds[buf].n)
                        for gi in range(w // 256):
                            ci = (j0 + gi * 256) // 256
                            lds[buf].inc(nc.gpsimd.indirect_dma_start(
                                out=spl[buf][:, gi * 256:(gi + 1) * 256],
                                out_offset=None, in_=sc_gu_flat,
                                in_offset=bass.IndirectOffsetOnAxis(
                                    sgix2[buf][:, 0:1], 0),
                                element_offset=ci * 32 * 256), 16)

                    issue_load(0, 0)
                    for kt in range(32):
                        buf = kt % 2
                        if kt < 31:
                            issue_load(kt + 1, 1 - buf)
                        nc.vector.wait_ge(lds[buf].h, lds[buf].n)
                        nc.vector.wait_ge(t.h, swfree[buf])
                        nc.vector.wait_ge(sems["st"].h, sems["st"].n)
                        nc.vector.wait_ge(v.h, v.n)
                        v.inc(nc.vector.tensor_scalar(
                            out=snib[:, 0:w], in0=spk[buf][:, 0:w],
                            scalar1=sshift[:, 0:1], scalar2=15,
                            op0=ALU.logical_shift_right, op1=ALU.bitwise_and))
                        nc.vector.wait_ge(v.h, v.n)
                        v.inc(nc.vector.scalar_tensor_tensor(
                            out=sw2[buf][:, 0:w], in0=snib[:, 0:w], scalar=8.0,
                            in1=spl[buf][:, 0:w], op0=ALU.subtract, op1=ALU.mult))
                        ldfree[buf] = v.n
                        swready[buf] = v.n
                        nc.tensor.wait_ge(v.h, swready[buf])
                        if kt == 0:
                            nc.tensor.wait_ge(a.h, a.n)
                        for jc in range(nj):
                            for tk in range(2):
                                t.inc(nc.tensor.matmul(
                                    pg[jc][tk][:],
                                    sw2[buf][:, jc * 128:(jc + 1) * 128],
                                    xnfm[:, kt * 1024 + tk * 512:
                                         kt * 1024 + (tk + 1) * 512],
                                    start=(kt == 0), stop=False))
                        swfree[buf] = t.n
                    emit_m2_slice_d(pk_gu_s, 32, sc_gu, j0, w)
                    nc.tensor.wait_ge(v.h, v.n)
                    for jc in range(nj):
                        for tk in range(2):
                            t.inc(nc.tensor.matmul(
                                pg[jc][tk][:],
                                sm2[0:32, jc * 128:(jc + 1) * 128],
                                sxg[:, tk * 512:(tk + 1) * 512],
                                start=False, stop=True))
                    for jc in range(nj):
                        jj = j0 + jc * 128
                        for tk in range(2):
                            if is_gate:
                                nc.scalar.wait_ge(t.h, t.n)
                                nc.scalar.wait_ge(sems["st"].h, sems["st"].n)
                                a.inc(nc.scalar.activation(out=shg[:],
                                                           in_=pg[jc][tk][:],
                                                           func=AF.Silu))
                                nc.sync.wait_ge(a.h, a.n)
                                sems["st"].inc(nc.sync.dma_start(
                                    out=hgbuf[jj:jj + 128,
                                              tk * 512:(tk + 1) * 512],
                                    in_=shg[:]), 16)
                            else:
                                jp = jj - INTER
                                nc.scalar.wait_ge(sems["st"].h, sems["st"].n)
                                qc.inc(nc.scalar.dma_start(
                                    out=shg[:],
                                    in_=hgbuf[jp:jp + 128,
                                              tk * 512:(tk + 1) * 512]), 16)
                                nc.vector.wait_ge(qc.h, qc.n)
                                nc.vector.wait_ge(t.h, t.n)
                                nc.vector.wait_ge(sems["st"].h, sems["st"].n)
                                v.inc(nc.vector.tensor_tensor(
                                    out=shh[:], in0=pg[jc][tk][:],
                                    in1=shg[:], op=ALU.mult))
                                nc.sync.wait_ge(v.h, v.n)
                                sems["st"].inc(nc.sync.dma_start(
                                    out=hbuf[jp:jp + 128,
                                             tk * 512:(tk + 1) * 512],
                                    in_=shh[:]), 16)
                    nc.tensor.wait_ge(a.h, a.n)
                    nc.tensor.wait_ge(v.h, v.n)

            slabs = [(i * 512, 512) for i in range(21)] + [(10752, 256)]
            for j0, w in slabs:
                gu_slab_d(j0, w, True)
            for j0, w in slabs:
                gu_slab_d(INTER + j0, w, False)

        barrier(p, sems)
        with ExitStack() as sc3:
            sht = p.sb("sht", [128, TLOC], BF16, sc3)
            with nc.psum_tensor(p.u("pxh"), [86, 1024], F32) as pxh:
                nc.sync.wait_ge(qs.h, qs.n)
                for g in range(86):
                    nc.sync.wait_ge(t.h, t.n)
                    qs.inc(nc.sync.dma_start(out=sht[:],
                                             in_=hbuf[g * 128:(g + 1) * 128, :]), 16)
                    onehot_gidx(gix_dn, g * 128, 86)
                    nc.tensor.wait_ge(v.h, v.n)
                    nc.tensor.wait_ge(qs.h, qs.n)
                    for tk in range(2):
                        t.inc(nc.tensor.matmul(pxh[:, tk * 512:(tk + 1) * 512],
                                               sonehot[:],
                                               sht[:, tk * 512:(tk + 1) * 512],
                                               start=(g == 0), stop=(g == 85)))
                nc.scalar.wait_ge(t.h, t.n)
                a.inc(nc.scalar.copy(out=sxh[:], in_=pxh[:]))
                nc.tensor.wait_ge(a.h, a.n)

        barrier(p, sems)
        with ExitStack() as sc4:
            spkd = p.sb("spkd", [128, 512], I32, sc4)
            spld = p.sb("spld", [128, 512], F32, sc4)
            snibd = p.sb("snibd", [128, 512], I32, sc4)
            swd0 = p.sb("swd0", [128, 512], BF16, sc4)
            swd1 = p.sb("swd1", [128, 512], BF16, sc4)
            swd = [swd0, swd1]
            sht0 = p.sb("sht0", [128, TLOC], BF16, sc4)
            sht1 = p.sb("sht1", [128, TLOC], BF16, sc4)
            shts = [sht0, sht1]
            sxr = p.sb("sxr", [128, 512], F32, sc4)
            sout = p.sb("sout", [128, 512], F32, sc4)
            hs = [Sem(p_sem_raw(p, "hsA")), Sem(p_sem_raw(p, "hsB"))]
            wds = [Sem(p_sem_raw(p, "wdsA")), Sem(p_sem_raw(p, "wdsB"))]
            spkd2 = [spkd, p.sb("spkd", [128, 512], I32, sc4)]
            spld2 = [spld, p.sb("spld", [128, 512], F32, sc4)]
            sgixd2 = [sgix, p.sb("sgixd", [128, 1], I32, sc4)]
            hfree = [0, 0]
            wdfree = [0, 0]
            with ExitStack() as pes:
                pd = [p.psum(f"pd{i}", [128, 512], F32, pes) for i in range(8)]
                for dm in range(8):
                    def issue_dn(kt, buf, dm=dm):
                        nc.sync.wait_ge(t.h, hfree[buf])
                        hs[buf].inc(nc.sync.dma_start(
                            out=shts[buf][:],
                            in_=hbuf[kt * 128:(kt + 1) * 128, :]), 16)
                        nc.gpsimd.wait_ge(v.h, wdfree[buf])
                        wds[buf].inc(nc.gpsimd.dma_start(
                            out=spkd2[buf][:],
                            in_=qw_dn.bitcast(I32)[kt * 16:(kt + 1) * 16,
                                                   dm * 512:(dm + 1) * 512]
                            .unsqueeze(1).broadcast_to([16, 8, 512])), 16)
                        wds[buf].inc(nc.gpsimd.dma_start(
                            out=sgixd2[buf][:],
                            in_=gix_dn[kt * 128:(kt + 1) * 128, :]), 16)
                        nc.gpsimd.wait_ge(wds[buf].h, wds[buf].n)
                        for gi in range(2):
                            ci = dm * 2 + gi
                            wds[buf].inc(nc.gpsimd.indirect_dma_start(
                                out=spld2[buf][:, gi * 256:(gi + 1) * 256],
                                out_offset=None, in_=sc_dn_flat,
                                in_offset=bass.IndirectOffsetOnAxis(
                                    sgixd2[buf][:, 0:1], 0),
                                element_offset=ci * 86 * 256), 16)

                    issue_dn(0, 0)
                    for kt in range(86):
                        buf = kt % 2
                        if kt < 85:
                            issue_dn(kt + 1, 1 - buf)
                        nc.vector.wait_ge(wds[buf].h, wds[buf].n)
                        nc.vector.wait_ge(t.h, t.n)
                        nc.vector.wait_ge(v.h, v.n)
                        v.inc(nc.vector.tensor_scalar(
                            out=snibd[:], in0=spkd2[buf][:],
                            scalar1=sshift[:, 0:1], scalar2=15,
                            op0=ALU.logical_shift_right, op1=ALU.bitwise_and))
                        nc.vector.wait_ge(v.h, v.n)
                        v.inc(nc.vector.scalar_tensor_tensor(
                            out=swd[buf][:], in0=snibd[:], scalar=8.0,
                            in1=spld2[buf][:], op0=ALU.subtract, op1=ALU.mult))
                        wdfree[buf] = v.n
                        nc.tensor.wait_ge(v.h, v.n)
                        nc.tensor.wait_ge(hs[buf].h, hs[buf].n)
                        if kt == 0:
                            nc.tensor.wait_ge(a.h, a.n)
                            nc.tensor.wait_ge(qg.h, qg.n)
                        for tt in range(8):
                            t.inc(nc.tensor.matmul(
                                pd[tt][:],
                                shts[buf][:, tt * 128:(tt + 1) * 128],
                                swd[buf][:], start=(kt == 0), stop=False))
                        hfree[buf] = t.n
                    emit_m2_slice_d(pk_dn_s, 86, sc_dn, dm * 512, 512)
                    nc.tensor.wait_ge(v.h, v.n)
                    for tt in range(8):
                        t.inc(nc.tensor.matmul(
                            pd[tt][:], sxh[:, tt * 128:(tt + 1) * 128],
                            sm2[0:86, 0:512],
                            start=False, stop=True))
                    for tt in range(8):
                        nc.gpsimd.wait_ge(v.h, v.n)
                        qg.inc(nc.gpsimd.dma_start(
                            out=sxr[:],
                            in_=xin[tt * 128:(tt + 1) * 128,
                                    dm * 512:(dm + 1) * 512]), 16)
                        nc.vector.wait_ge(qg.h, qg.n)
                        nc.vector.wait_ge(t.h, t.n)
                        v.inc(nc.vector.tensor_add(out=sout[:], in0=pd[tt][:],
                                                   in1=sxr[:]))
                        nc.gpsimd.wait_ge(v.h, v.n)
                        qg.inc(nc.gpsimd.dma_start(
                            out=out[tt * 128:(tt + 1) * 128,
                                    dm * 512:(dm + 1) * 512], in_=sout[:]), 16)
                        nc.vector.wait_ge(qg.h, qg.n)

        nc.sync.wait_ge(qs.h, qs.n)
        nc.sync.wait_ge(qg.h, qg.n)
        nc.sync.wait_ge(qc.h, qc.n)
        nc.sync.wait_ge(sems["st"].h, sems["st"].n)


def build_l1_ffn():
    p = Prog("l1ffn")
    nc = p.nc
    U32_ = mybir.dt.uint32
    xin = p.inp("xin", [TLOC, H], F32)
    qw_gu = p.inp("qw_gu", [512, 2 * INTER], U32_)
    qz_gu = p.inp("qz_gu", [32, 2752], U32_)
    sc_gu = p.inp("sc_gu", [32, 2 * INTER], F32)
    gix_gu = p.inp("gix_gu", [H, 1], I32)
    qw_dn = p.inp("qw_dn", [1376, H], U32_)
    qz_dn = p.inp("qz_dn", [86, 512], U32_)
    sc_dn = p.inp("sc_dn", [86, H], F32)
    gix_dn = p.inp("gix_dn", [INTER, 1], I32)
    shifts = p.inp("shifts", [128, 1], I32)
    eye = p.inp("eye", [128, 128], F32)
    out = p.outp("out", [TLOC, H], F32)
    with p.es:
        block = p.es.enter_context(nc.Block())
        sems = mk_sems(p, nc, "d")
        emit_block_l1ffn(p, sems, xin, qw_gu, qz_gu, sc_gu, gix_gu, qw_dn, qz_dn,
                         sc_dn, gix_dn, shifts, eye, out)
    return p


_SHIFTS = ((np.arange(128) % 8) * 4).astype(np.int32).reshape(128, 1)


def _run_l1_ffn(x3, qw_gu, qz_gu, sc_gu, gix_gu, qw_dn, qz_dn, sc_dn, gix_dn):
    p = build_l1_ffn()
    in_maps = []
    for c in range(NCORE):
        b, h = c // 2, c % 2
        in_maps.append(dict(
            xin=np.ascontiguousarray(x3[b, h * TLOC:(h + 1) * TLOC]),
            qw_gu=qw_gu, qz_gu=qz_gu, sc_gu=sc_gu,
            gix_gu=np.ascontiguousarray(gix_gu.reshape(H, 1)),
            qw_dn=qw_dn, qz_dn=qz_dn, sc_dn=sc_dn,
            gix_dn=np.ascontiguousarray(gix_dn.reshape(INTER, 1)),
            shifts=_SHIFTS, eye=_EYE))
    res = run_bass_kernel_spmd(p.nc, in_maps, core_ids=list(range(NCORE)))
    out = np.empty((B, S, H), np.float32)
    for c in range(NCORE):
        b, h = c // 2, c % 2
        out[b, h * TLOC:(h + 1) * TLOC] = res.results[c]["out"]
    return out



def emit_exchange(p, xdone, x2loc, x2sh, x2meta, x2flag, nonce, hsel):
    nc = p.nc
    with block_scope(p):
        snon = p.sb("snon", [1, 1], I32)
        shsel = p.sb("shsel", [1, 1], I32)
        sflagin = p.sb("sflagin", [1, 1], I32)
        xq = Sem(p_sem_raw(p, "xq"))
        pollq = Sem(p_sem_raw(p, "pollq"))
        g = nc.gpsimd
        xq.inc(g.dma_start(out=snon[:], in_=nonce), 16)
        xq.inc(g.dma_start(out=shsel[:], in_=hsel), 16)
        g.wait_ge(xq.h, xq.n)
        with g.register("rf") as rf, g.register("rc") as rc, \
             g.register("rn") as rn, g.register("nd") as nd:
            g.reg_load(rn, snon[0:1, 0:1])
            g.reg_mov(rc, 0)
            xq_base = xq.n
            for arm in g.switch_core_id(n=128):
                xq.n = xq_base
                par = (int(arm) % 8) % 2
                xq.inc(g.dma_start(out=x2sh[par * 1024:(par + 1) * 1024, :],
                                   in_=x2loc), 16)
                g.wait_ge(xq.h, xq.n)
                xq.inc(g.dma_start(out=x2meta[par:par + 1, :], in_=shsel[:]), 16)
                g.wait_ge(xq.h, xq.n)
                xq.inc(g.dma_start(out=x2flag[par:par + 1, :], in_=snon[:]), 16)
                g.wait_ge(xq.h, xq.n)
                g.reg_add(rc, rc, 16)
                g.dma_start(out=sflagin[:],
                            in_=x2flag[1 - par:2 - par, :]).then_inc(pollq.h, 16)
                g.wait_ge(pollq.h, rc)
                g.reg_load(rf, sflagin[0:1, 0:1])
                g.reg_alu(nd, rf, rn, ALU.not_equal)
                with g.While(nd):
                    g.reg_add(rc, rc, 16)
                    g.dma_start(out=sflagin[:],
                                in_=x2flag[1 - par:2 - par, :]).then_inc(pollq.h, 16)
                    g.wait_ge(pollq.h, rc)
                    g.reg_load(rf, sflagin[0:1, 0:1])
                    g.reg_alu(nd, rf, rn, ALU.not_equal)
            xq.n = xq_base + 48
        g.sem_inc(xdone.h, 1)
        xdone.n += 1


def build_fused(detect_races=True):
    p = Prog("fused", detect_races)
    nc = p.nc
    U32_ = mybir.dt.uint32
    xloc = p.inp("xloc", [TLOC, H], F32)
    xfull = p.inp("xfull", [S, H], F32)
    qkvw = p.inp("qkvw", [H, 3 * H], F32)
    qkvb2 = p.inp("qkvb2", [96, 128], F32)
    ow = p.inp("ow", [H, H], F32)
    ob = p.inp("ob", [1, H], F32)
    b_qw_gu = p.inp("b_qw_gu", [H, 2752], I32)
    b_qz_gu = p.inp("b_qz_gu", [32, 2752], I32)
    b_sc_gu = p.inp("b_sc_gu", [32, 2 * INTER], F32)
    b_qw_dn = p.inp("b_qw_dn", [INTER, 512], I32)
    b_qz_dn = p.inp("b_qz_dn", [86, 512], I32)
    b_sc_dn = p.inp("b_sc_dn", [86, H], F32)
    wq = p.inp("wq", [H, H], F32)
    wk = p.inp("wk", [1024, H], F32)
    wv = p.inp("wv", [1024, H], F32)
    wo = p.inp("wo", [H, H], F32)
    d_qw_gu = p.inp("d_qw_gu", [512, 2 * INTER], U32_)
    d_qz_gu = p.inp("d_qz_gu", [32, 2752], U32_)
    d_sc_gu = p.inp("d_sc_gu", [32, 2 * INTER], F32)
    d_gix_gu = p.inp("d_gix_gu", [H, 1], I32)
    d_qw_dn = p.inp("d_qw_dn", [1376, H], U32_)
    d_qz_dn = p.inp("d_qz_dn", [86, 512], U32_)
    d_sc_dn = p.inp("d_sc_dn", [86, H], F32)
    d_gix_dn = p.inp("d_gix_dn", [INTER, 1], I32)
    ones = p.inp("ones", [1, 128], F32)
    eye = p.inp("eye", [128, 128], F32)
    invf = p.inp("invf", [128, 1], F32)
    rotm = p.inp("rotm", [128, 128], F32)
    posoff = p.inp("posoff", [1, 1], F32)
    hsel = p.inp("hsel", [1, 1], I32)
    nonce = p.inp("nonce", [1, 1], I32)
    shifts = p.inp("shifts", [128, 1], I32)
    out = p.outp("out", [TLOC, H], F32)
    x1loc = p.scratch("x1loc", [TLOC, H], F32)
    x2loc = p.scratch("x2loc", [TLOC, H], F32)
    x3loc = p.scratch("x3loc", [TLOC, H], F32)
    x2sh = p.scratch("x2sh", [S, H], F32, shared=True)
    x2meta = p.scratch("x2meta", [2, 1], I32, shared=True)
    x2flag = p.scratch("x2flag", [2, 1], I32, shared=True)

    with p.es:
        block = p.es.enter_context(nc.Block())
        sa = mk_sems(p, nc, "a")
        sb_ = mk_sems(p, nc, "b")
        sc_ = mk_sems(p, nc, "c")
        sd_ = mk_sems(p, nc, "d")
        xdone = Sem(p_sem_raw(p, "xdone"))

        emit_block_l0attn(p, sa, xloc, xfull, qkvw, qkvb2, ow, ob, ones, eye,
                          x1loc)
        barrier(p, sa)
        emit_block_l0ffn(p, sb_, x1loc, b_qw_gu, b_qz_gu, b_sc_gu, b_qw_dn,
                         b_qz_dn, b_sc_dn, eye, x2loc)
        barrier(p, sb_)
        emit_exchange(p, xdone, x2loc, x2sh, x2meta, x2flag, nonce, hsel)
        for eng in [nc.sync, nc.gpsimd, nc.scalar, nc.vector, nc.tensor]:
            eng.wait_ge(xdone.h, 1)
        soff_sp = [p.sb(f"soffsp{i}", [128, 1], F32) for i in range(2)]
        smeta_i = p.sb("smeta_i", [128, 1], I32)
        for sp in range(2):
            sc_["qs"].inc(nc.sync.dma_start(
                out=smeta_i[:], in_=x2meta[sp:sp + 1, :].partition_broadcast(128)),
                16)
            nc.vector.wait_ge(sc_["qs"].h, sc_["qs"].n)
            nc.vector.wait_ge(sc_["v"].h, sc_["v"].n)
            sc_["v"].inc(nc.vector.tensor_scalar(
                out=soff_sp[sp][:], in0=smeta_i[:], scalar1=1024.0, scalar2=None,
                op0=ALU.mult))
        nc.vector.wait_ge(sc_["v"].h, sc_["v"].n)
        emit_block_l1attn(p, sc_, x2loc, x2sh, wq, wk, wv, wo, eye, invf, rotm,
                          posoff, x3loc,
                          span_off=lambda sp: soff_sp[sp][:, 0:1])
        barrier(p, sc_)
        emit_block_l1ffn(p, sd_, x3loc, d_qw_gu, d_qz_gu, d_sc_gu, d_gix_gu,
                         d_qw_dn, d_qz_dn, d_sc_dn, d_gix_dn, shifts, eye, out)
    return p


def kernel_fused(inp):
    p = build_fused()
    x = np.asarray(inp["L0_attn_x"], np.float32)
    qkvb2 = np.ascontiguousarray(np.asarray(inp["L0_attn_attn_qkvb"]).reshape(96, 128))
    obr = np.ascontiguousarray(np.asarray(inp["L0_attn_attn_ob"]).reshape(1, H))
    nonce = np.array([[np.random.randint(1, 2**31 - 1)]], np.int32)
    in_maps = []
    for c in range(NCORE):
        b, h = c // 2, c % 2
        in_maps.append(dict(
            xloc=np.ascontiguousarray(x[b, h * TLOC:(h + 1) * TLOC]),
            xfull=np.ascontiguousarray(x[b]),
            qkvw=np.asarray(inp["L0_attn_attn_qkvw"]), qkvb2=qkvb2,
            ow=np.asarray(inp["L0_attn_attn_ow"]), ob=obr,
            b_qw_gu=np.asarray(inp["L0_ffn_qweight_gu"]),
            b_qz_gu=np.asarray(inp["L0_ffn_qzeros_gu"]),
            b_sc_gu=np.asarray(inp["L0_ffn_scales_gu"]),
            b_qw_dn=np.asarray(inp["L0_ffn_qweight_dn"]),
            b_qz_dn=np.asarray(inp["L0_ffn_qzeros_dn"]),
            b_sc_dn=np.asarray(inp["L0_ffn_scales_dn"]),
            wq=np.asarray(inp["L1_attn_Wq"]), wk=np.asarray(inp["L1_attn_Wk"]),
            wv=np.asarray(inp["L1_attn_Wv"]), wo=np.asarray(inp["L1_attn_Wo"]),
            d_qw_gu=np.asarray(inp["L1_ffn_qweight_gu"]),
            d_qz_gu=np.asarray(inp["L1_ffn_qzeros_gu"]),
            d_sc_gu=np.asarray(inp["L1_ffn_scales_gu"]),
            d_gix_gu=np.ascontiguousarray(
                np.asarray(inp["L1_ffn_g_idx_gu"]).reshape(H, 1)),
            d_qw_dn=np.asarray(inp["L1_ffn_qweight_down"]),
            d_qz_dn=np.asarray(inp["L1_ffn_qzeros_down"]),
            d_sc_dn=np.asarray(inp["L1_ffn_scales_down"]),
            d_gix_dn=np.ascontiguousarray(
                np.asarray(inp["L1_ffn_g_idx_down"]).reshape(INTER, 1)),
            ones=_ONES, eye=_EYE, invf=_INVF, rotm=_ROTM,
            posoff=np.full((1, 1), h * TLOC, np.float32),
            hsel=np.full((1, 1), h, np.int32),
            nonce=nonce, shifts=_SHIFTS))
    res = run_bass_kernel_spmd(p.nc, in_maps, core_ids=list(range(NCORE)))
    outv = np.empty((B, S, H), np.float32)
    for c in range(NCORE):
        b, h = c // 2, c % 2
        outv[b, h * TLOC:(h + 1) * TLOC] = res.results[c]["out"]
    return outv


# version 05 (final)
# speedup vs baseline: 1.7962x; 1.1437x over previous
import numpy as np
from contextlib import ExitStack
import concourse.bass as bass
import concourse.mybir as mybir
from concourse.bass_utils import run_bass_kernel_spmd

F32 = mybir.dt.float32
BF16 = mybir.dt.bfloat16
I32 = mybir.dt.int32
U8 = mybir.dt.uint8
AF = mybir.ActivationFunctionType
ALU = mybir.AluOpType
AX = mybir.AxisListType

B, S, H = 4, 2048, 4096
NCORE = 8
TLOC = 1024
INTER = 11008
EPS = 1e-5
RS = 1.0 / float(np.sqrt(128.0))


class Sem:
    def __init__(self, h):
        self.h = h
        self.n = 0
        self.eng = None

    def inc(self, instr, k=1):
        instr.then_inc(self.h, k)
        self.n += k
        if self.eng is not None and k == 16:
            self.eng.wait_ge(self.h, self.n)
        return self.n


class Prog:
    def __init__(self, name, detect_races=True):
        self.nc = bass.Bass("TRN2", target_bir_lowering=False,
                            detect_race_conditions=detect_races)
        self.es = ExitStack()
        self.root_es = self.es
        self.name = name
        self._uid = 0

    def u(self, base):
        self._uid += 1
        return f"{base}_{self._uid}"

    def sb(self, name, shape, dt, scope=None):
        return (scope or self.es).enter_context(
            self.nc.sbuf_tensor(self.u(name), shape, dt))

    def psum(self, name, shape, dt, scope):
        return scope.enter_context(self.nc.psum_tensor(self.u(name), shape, dt))

    def sem(self, name):
        return Sem(self.es.enter_context(self.nc.semaphore(name)))

    def inp(self, name, shape, dt):
        return self.nc.dram_tensor(name, shape, dt, kind="ExternalInput").ap()

    def outp(self, name, shape, dt):
        return self.nc.dram_tensor(name, shape, dt, kind="ExternalOutput").ap()

    def scratch(self, name, shape, dt, shared=False):
        if shared:
            return self.nc.dram_tensor(name, shape, dt, addr_space="Shared").ap()
        return self.nc.dram_tensor(name, shape, dt).ap()



from contextlib import contextmanager


@contextmanager
def block_scope(p):
    old = p.es
    p.es = ExitStack()
    try:
        with p.es:
            yield
    finally:
        p.es = old


def mk_sems(p, nc, tag):
    sems = {k: Sem(p_sem_raw(p, f"{tag}_{k}")) for k in
            ["qs", "qg", "qc", "a", "v", "t", "g", "st"]}
    sems["qs"].eng = nc.sync
    sems["qg"].eng = nc.gpsimd
    sems["qc"].eng = nc.scalar
    return sems


def p_sem_raw(p, name):
    return p.root_es.enter_context(p.nc.semaphore(p.u(name)))


def barrier(p, sems):
    nc = p.nc
    for eng in [nc.sync, nc.gpsimd, nc.scalar, nc.vector, nc.tensor]:
        for s in sems.values():
            eng.wait_ge(s.h, s.n)


def emit_ln_span(p, sems, src, tok0, ntile, sx, sxn, stats, xnfm, ptr8, W):
    nc = p.nc
    qs, a, v, t = sems["qs"], sems["a"], sems["v"], sems["t"]
    st1, st2, stm, stv, stb = stats
    xv = xnfm[:].rearrange("p (kt w) -> p kt w", w=W)
    for i in range(ntile):
        r0 = tok0 + i * 128
        nc.sync.wait_ge(a.h, a.n)
        qs.inc(nc.sync.dma_start(out=sx[:], in_=src[r0:r0 + 128, :]), 16)
        nc.scalar.wait_ge(qs.h, qs.n)
        nc.scalar.wait_ge(t.h, t.n)
        a.inc(nc.scalar.memzero(st2[:]))
        nc.scalar.wait_ge(a.h, a.n)
        a.inc(nc.scalar.activation(out=sxn[:], in_=sx[:], func=AF.Square,
                                   accum_out=st2[:, 0:1]))
        nc.vector.wait_ge(a.h, a.n)

        def vstep(instr):
            v.inc(instr)
            nc.vector.wait_ge(v.h, v.n)
        vstep(nc.vector.reduce_sum(out=st1[:, 0:1], in_=sx[:], axis=AX.X))
        vstep(nc.vector.tensor_scalar(out=stm[:, 0:1], in0=st1[:, 0:1],
                                      scalar1=1.0 / H, scalar2=None, op0=ALU.mult))
        vstep(nc.vector.tensor_tensor(out=stv[:, 0:1], in0=stm[:, 0:1],
                                      in1=stm[:, 0:1], op=ALU.mult))
        vstep(nc.vector.scalar_tensor_tensor(out=stv[:, 0:1], in0=st2[:, 0:1],
                                             scalar=1.0 / H, in1=stv[:, 0:1],
                                             op0=ALU.mult, op1=ALU.subtract))
        vstep(nc.vector.tensor_scalar(out=stv[:, 0:1], in0=stv[:, 0:1],
                                      scalar1=EPS, scalar2=None, op0=ALU.add))
        nc.scalar.wait_ge(v.h, v.n)
        a.inc(nc.scalar.sqrt(out=stv[:, 0:1], in_=stv[:, 0:1]))
        nc.vector.wait_ge(a.h, a.n)
        vstep(nc.vector.reciprocal(out=stv[:, 0:1], in_=stv[:, 0:1]))
        vstep(nc.vector.scalar_tensor_tensor(out=stb[:, 0:1], in0=stm[:, 0:1],
                                             scalar=-1.0, in1=stv[:, 0:1],
                                             op0=ALU.mult, op1=ALU.mult))
        nc.scalar.wait_ge(v.h, v.n)
        nc.scalar.wait_ge(t.h, t.n)
        a.inc(nc.scalar.activation(out=sxn[:], in_=sx[:], func=AF.Identity,
                                   scale=stv[:, 0:1], bias=stb[:, 0:1]))
        nc.tensor.wait_ge(a.h, a.n)
        evfree = [v.n, v.n]
        for g in range(4):
            bnk = g % 2
            nc.tensor.wait_ge(v.h, evfree[bnk])
            for j in range(8):
                kt = g * 8 + j
                t.inc(nc.tensor.matmul(ptr8[bnk][:, j * 128:(j + 1) * 128],
                                       sxn[:, kt * 128:(kt + 1) * 128],
                                       p.seye[:], is_transpose=True,
                                       start=(j == 0), stop=(j == 7)))
            nc.vector.wait_ge(t.h, t.n)
            v.inc(nc.vector.tensor_copy(
                out=xv[:, g * 8:(g + 1) * 8, i * 128:(i + 1) * 128],
                in_=ptr8[bnk][:]))
            evfree[bnk] = v.n


def emit_proj(p, sems, wsrc, chunks, xnfm, W, tks, wbufs, pq, sev, evac_fn,
              bias_fn=None, wrows=H):
    nc = p.nc
    qs, qg, a, t = sems["qs"], sems["qg"], sems["a"], sems["t"]
    st = sems["st"]
    nkt = wrows // 128
    ws = [Sem(p_sem_raw(p, "wsA")), Sem(p_sem_raw(p, "wsB"))]
    wfree = [t.n, t.n]
    for idx, (c0, tag) in enumerate(chunks):
        buf = idx % 2
        nc.gpsimd.wait_ge(t.h, wfree[buf])
        ws[buf].inc(nc.gpsimd.dma_start(
            out=wbufs[buf][:],
            in_=wsrc[:, c0:c0 + 128].rearrange("(kt p) o -> p kt o", p=128)), 16)
        nc.tensor.wait_ge(ws[buf].h, ws[buf].n)
        nc.tensor.wait_ge(a.h, a.n)
        for tki, (tk0, tw) in enumerate(tks):
            for kt in range(nkt):
                t.inc(nc.tensor.matmul(pq[tki][:, 0:tw],
                                       wbufs[buf][:, kt * 128:(kt + 1) * 128],
                                       xnfm[:, kt * W + tk0: kt * W + tk0 + tw],
                                       start=(kt == 0), stop=(kt == nkt - 1)))
        wfree[buf] = t.n
        nc.scalar.wait_ge(t.h, t.n)
        nc.scalar.wait_ge(st.h, st.n)
        for tki, (tk0, tw) in enumerate(tks):
            bias = bias_fn(tag) if bias_fn is not None else None
            if bias is not None:
                a.inc(nc.scalar.activation(out=sev[:, tki * 512: tki * 512 + tw],
                                           in_=pq[tki][:, 0:tw],
                                           func=AF.Identity, bias=bias, scale=1.0))
            else:
                a.inc(nc.scalar.copy(out=sev[:, tki * 512: tki * 512 + tw],
                                     in_=pq[tki][:, 0:tw]))
        nc.sync.wait_ge(a.h, a.n)
        for tki, (tk0, tw) in enumerate(tks):
            evac_fn(tag, tki, sev[:, tki * 512: tki * 512 + tw], tw)


def emit_attention(p, sems, nq_tiles, kv_heads, q_per_kv, qsrc, ksrc, vsrc, sctx,
                   sc2, SKV):
    nc = p.nc
    qs, a, v, t = sems["qs"], sems["a"], sems["v"], sems["t"]
    nkc = SKV // 512
    nkt = SKV // 128
    sq = p.sb("sq", [128, nq_tiles * 128], BF16, sc2)
    sk = p.sb("sk", [128, SKV], BF16, sc2)
    sv = p.sb("sv", [128, SKV], BF16, sc2)
    svt = p.sb("svt", [128, SKV], BF16, sc2)
    sprob2 = [p.sb("sprob", [128, SKV], BF16, sc2) for _ in range(2)]
    sprT = p.sb("sprT", [128, nkt * nq_tiles * 128], BF16, sc2)
    smax = p.sb("smax", [128, 1], F32, sc2)
    snegb = p.sb("snegb", [128, 1], F32, sc2)
    ssum = p.sb("ssum", [128, 1], F32, sc2)
    srec = p.sb("srec", [128, 1], F32, sc2)
    with nc.psum_tensor(p.u("psc"), [128, SKV], F32) as psc, \
         nc.psum_tensor(p.u("pt8a"), [128, 1024], BF16) as pt8a, \
         nc.psum_tensor(p.u("pt8b"), [128, 1024], BF16) as pt8b, \
         nc.psum_tensor(p.u("pc"), [128, 1024], F32) as pc:
        pt8 = [pt8a, pt8b]
        nc.sync.wait_ge(qs.h, qs.n)
        nc.sync.wait_ge(sems["qg"].h, sems["qg"].n)
        nc.sync.wait_ge(sems["qc"].h, sems["qc"].n)
        nc.sync.wait_ge(sems["st"].h, sems["st"].n)
        for kh in range(kv_heads):
            nc.sync.wait_ge(t.h, t.n)
            qs.inc(nc.sync.dma_start(out=sk[:], in_=ksrc[kh * 128:(kh + 1) * 128, :]), 16)
            qs.inc(nc.sync.dma_start(out=sv[:], in_=vsrc[kh * 128:(kh + 1) * 128, :]), 16)
            nc.tensor.wait_ge(qs.h, qs.n)
            evfree = [v.n, v.n]
            for g in range(nkt // 8):
                bnk = g % 2
                nc.tensor.wait_ge(v.h, evfree[bnk])
                for j in range(8):
                    kt = g * 8 + j
                    t.inc(nc.tensor.matmul(pt8[bnk][:, j * 128:(j + 1) * 128],
                                           sv[:, kt * 128:(kt + 1) * 128],
                                           p.seye[:], is_transpose=True,
                                           start=(j == 0), stop=(j == 7)))
                nc.vector.wait_ge(t.h, t.n)
                v.inc(nc.vector.tensor_copy(
                    out=svt[:, g * 1024:(g + 1) * 1024], in_=pt8[bnk][:]))
                evfree[bnk] = v.n
            for qq in range(q_per_kv):
                qh = kh * q_per_kv + qq
                nc.sync.wait_ge(t.h, t.n)
                qs.inc(nc.sync.dma_start(out=sq[:],
                                         in_=qsrc[qh * 128:(qh + 1) * 128, :]), 16)
                nc.tensor.wait_ge(qs.h, qs.n)
                prfree = [t.n, t.n]
                for qt in range(nq_tiles):
                    pb_ = qt % 2
                    sprob = sprob2[pb_]
                    nc.tensor.wait_ge(a.h, a.n)
                    for kc in range(nkc):
                        t.inc(nc.tensor.matmul(psc[:, kc * 512:(kc + 1) * 512],
                                               sq[:, qt * 128:(qt + 1) * 128],
                                               sk[:, kc * 512:(kc + 1) * 512],
                                               start=True, stop=True))
                    nc.vector.wait_ge(t.h, t.n)
                    v.inc(nc.vector.reduce_max(out=smax[:, 0:1], in_=psc[:],
                                               axis=AX.X))
                    nc.vector.wait_ge(v.h, v.n)
                    v.inc(nc.vector.tensor_scalar(out=snegb[:, 0:1], in0=smax[:, 0:1],
                                                  scalar1=-RS, scalar2=None,
                                                  op0=ALU.mult))
                    nc.scalar.wait_ge(v.h, v.n)
                    nc.scalar.wait_ge(t.h, prfree[pb_])
                    a.inc(nc.scalar.memzero(ssum[:]))
                    nc.scalar.wait_ge(a.h, a.n)
                    a.inc(nc.scalar.activation(out=sprob[:], in_=psc[:], func=AF.Exp,
                                               scale=RS, bias=snegb[:, 0:1],
                                               accum_out=ssum[:, 0:1]))
                    nc.vector.wait_ge(a.h, a.n)
                    v.inc(nc.vector.reciprocal(out=srec[:, 0:1], in_=ssum[:, 0:1]))
                    nc.scalar.wait_ge(v.h, v.n)
                    a.inc(nc.scalar.activation(out=sprob[:], in_=sprob[:],
                                               func=AF.Identity, scale=srec[:, 0:1]))
                    nc.tensor.wait_ge(a.h, a.n)
                    evfree = [v.n, v.n]
                    for g in range(nkt // 8):
                        bnk = g % 2
                        nc.tensor.wait_ge(v.h, evfree[bnk])
                        for j in range(8):
                            kt = g * 8 + j
                            t.inc(nc.tensor.matmul(
                                pt8[bnk][:, j * 128:(j + 1) * 128],
                                sprob[:, kt * 128:(kt + 1) * 128],
                                p.seye[:], is_transpose=True,
                                start=(j == 0), stop=(j == 7)))
                        nc.vector.wait_ge(t.h, t.n)
                        prv = sprT[:].rearrange("p (kt q) -> p kt q",
                                                q=nq_tiles * 128)
                        v.inc(nc.vector.tensor_copy(
                            out=prv[:, g * 8:(g + 1) * 8, qt * 128:(qt + 1) * 128],
                            in_=pt8[bnk][:]))
                        evfree[bnk] = v.n
                    prfree[pb_] = t.n
                nc.tensor.wait_ge(v.h, v.n)
                nc.tensor.wait_ge(a.h, a.n)
                for tk in range(nq_tiles * 128 // 512):
                    for kt in range(nkt):
                        t.inc(nc.tensor.matmul(
                            pc[:, tk * 512:(tk + 1) * 512],
                            svt[:, kt * 128:(kt + 1) * 128],
                            sprT[:, kt * nq_tiles * 128 + tk * 512:
                                 kt * nq_tiles * 128 + (tk + 1) * 512],
                            start=(kt == 0), stop=(kt == nkt - 1)))
                nc.scalar.wait_ge(t.h, t.n)
                a.inc(nc.scalar.copy(out=sctx[:, qh * 1024:(qh + 1) * 1024],
                                     in_=pc[:]))


def emit_outproj(p, sems, wsrc, ctx_sb, nft, res_src, dst, bias_row, sones, sc2,
                 wT=False):
    nc = p.nc
    qs, qg, a, v, t = sems["qs"], sems["qg"], sems["a"], sems["v"], sems["t"]
    st = sems["st"]
    ws = [Sem(p_sem_raw(p, "wsA")), Sem(p_sem_raw(p, "wsB"))]
    swo0 = p.sb("swo0", [128, 512], BF16, sc2)
    swo1 = p.sb("swo1", [128, 512], BF16, sc2)
    swo = [swo0, swo1]
    sxr = p.sb("sxr", [128, 512], F32, sc2)
    sout = p.sb("sout", [128, 512], F32, sc2)
    with ExitStack() as pes:
        po = [pes.enter_context(nc.psum_tensor(p.u(f"po{i}"), [128, 512], F32))
              for i in range(8)]
        nc.gpsimd.wait_ge(qs.h, qs.n)
        nc.gpsimd.wait_ge(st.h, st.n)
        for dm in range(8):
            wfree = [t.n, t.n]
            for hh in range(nft):
                buf = hh % 2
                nc.gpsimd.wait_ge(t.h, wfree[buf])
                ws[buf].inc(nc.gpsimd.dma_start(
                    out=swo[buf][:],
                    in_=wsrc[hh * 128:(hh + 1) * 128, dm * 512:(dm + 1) * 512]), 16)
                nc.tensor.wait_ge(ws[buf].h, ws[buf].n)
                if hh == 0:
                    nc.tensor.wait_ge(v.h, v.n)
                last = (hh == nft - 1) and (bias_row is None)
                for tt in range(8):
                    t.inc(nc.tensor.matmul(po[tt][:],
                                           ctx_sb[:, hh * 1024 + tt * 128:
                                                  hh * 1024 + (tt + 1) * 128],
                                           swo[buf][:],
                                           start=(hh == 0), stop=last))
                wfree[buf] = t.n
            if bias_row is not None:
                for tt in range(8):
                    t.inc(nc.tensor.matmul(po[tt][:], sones[:],
                                           bias_row[:, dm * 512:(dm + 1) * 512],
                                           start=False, stop=True))
            for tt in range(8):
                nc.sync.wait_ge(v.h, v.n)
                qs.inc(nc.sync.dma_start(
                    out=sxr[:],
                    in_=res_src[tt * 128:(tt + 1) * 128, dm * 512:(dm + 1) * 512]), 16)
                nc.vector.wait_ge(qs.h, qs.n)
                nc.vector.wait_ge(t.h, t.n)
                nc.vector.wait_ge(st.h, st.n)
                v.inc(nc.vector.tensor_add(out=sout[:], in0=po[tt][:], in1=sxr[:]))
                nc.sync.wait_ge(v.h, v.n)
                st.inc(nc.sync.dma_start(
                    out=dst[tt * 128:(tt + 1) * 128, dm * 512:(dm + 1) * 512],
                    in_=sout[:]), 16)



def emit_block_l0attn(p, sems, xloc, xfull, qkvw, qkvb2, ow, ob, ones, eye, out,
                      debug=False, dbg=None):
    nc = p.nc
    qbuf = dbg["qbuf"] if debug else p.scratch(p.u("qbuf"), [H, TLOC], BF16)
    kbuf = dbg["kbuf"] if debug else p.scratch(p.u("kbuf"), [H, S], BF16)
    vbuf = dbg["vbuf"] if debug else p.scratch(p.u("vbuf"), [H, S], BF16)
    ctxd = dbg["ctxd"] if debug else None

    with block_scope(p):
        qs, qg, qc, a, v, t = (sems[k] for k in ["qs", "qg", "qc", "a", "v", "t"])

        sev = p.sb("sev", [128, 1024], BF16)
        sbias = p.sb("sbias", [128, 96], F32)
        sones = p.sb("sones", [1, 128], F32)
        sob = p.sb("sob", [1, H], F32)
        st1 = p.sb("st1", [128, 1], F32)
        st2 = p.sb("st2", [128, 1], F32)
        stm = p.sb("stm", [128, 1], F32)
        stv = p.sb("stv", [128, 1], F32)
        stb = p.sb("stb", [128, 1], F32)
        stats = (st1, st2, stm, stv, stb)
        sctx = p.sb("sctx", [128, 32 * 1024], BF16)
        p.seye = p.sb("seye", [128, 128], BF16)
        p.seye32 = p.sb("seye32", [128, 128], F32)

        qg.inc(nc.gpsimd.dma_start(out=p.seye[:], in_=eye), 16)
        qs.inc(nc.sync.dma_start(out=p.seye32[:], in_=eye), 16)

        with nc.psum_tensor("pb", [128, 128], F32) as pb, \
             nc.sbuf_tensor("sbq", [128, 128], F32) as sbq:
            qs.inc(nc.sync.dma_start(out=sbq[0:96, :], in_=qkvb2), 16)
            qs.inc(nc.sync.dma_start(out=sones[:], in_=ones), 16)
            qs.inc(nc.sync.dma_start(out=sob[:], in_=ob), 16)
            nc.vector.wait_ge(qs.h, qs.n)
            v.inc(nc.vector.memset(sbq[96:128, :], 0.0))
            nc.tensor.wait_ge(qs.h, qs.n)
            nc.tensor.wait_ge(v.h, v.n)
            nc.tensor.wait_ge(qs.h, qs.n)
            nc.tensor.wait_ge(qg.h, qg.n)
            t.inc(nc.tensor.transpose(pb[:], sbq[:], p.seye32[:]))
            nc.vector.wait_ge(t.h, t.n)
            v.inc(nc.vector.tensor_copy(out=sbias[:, 0:96], in_=pb[:, 0:96]))
            nc.tensor.wait_ge(v.h, v.n)

        barrier(p, sems)
        with ExitStack() as sc1:
            sx = p.sb("sx", [128, H], F32, sc1)
            sxn = p.sb("sxn", [128, H], BF16, sc1)
            xnfm = p.sb("xnfm", [128, 32 * 1024], BF16, sc1)
            wb0 = p.sb("wb0", [128, 32 * 128], BF16, sc1)
            wb1 = p.sb("wb1", [128, 32 * 128], BF16, sc1)
            with nc.psum_tensor("pt8a", [128, 1024], BF16) as pt8a, \
                 nc.psum_tensor("pt8b", [128, 1024], BF16) as pt8b, \
                 nc.psum_tensor("pq0", [128, 512], F32) as pq0, \
                 nc.psum_tensor("pq1", [128, 512], F32) as pq1:
                pq = [pq0, pq1]
                tks = [(0, 512), (512, 512)]

                def bias_of(oc):
                    return sbias[:, oc:oc + 1]

                def ev_q(tag, tki, sv_, tw):
                    sems["st"].inc(nc.sync.dma_start(
                        out=qbuf[tag * 128:(tag + 1) * 128,
                                 tki * 512:(tki + 1) * 512], in_=sv_), 16)

                emit_ln_span(p, sems, xloc, 0, 8, sx, sxn, stats, xnfm,
                             [pt8a, pt8b], 1024)
                nc.tensor.wait_ge(v.h, v.n)
                emit_proj(p, sems, qkvw, [(oc * 128, oc) for oc in range(32)],
                          xnfm, 1024, tks, [wb0, wb1], pq, sev, ev_q, bias_of)

                for sp in range(2):
                    def ev_kv(tag, tki, sv_, tw, sp=sp):
                        dst = kbuf if tag < 64 else vbuf
                        r = (tag - 32) if tag < 64 else (tag - 64)
                        sems["st"].inc(nc.sync.dma_start(
                            out=dst[r * 128:(r + 1) * 128,
                                    sp * 1024 + tki * 512:
                                    sp * 1024 + (tki + 1) * 512], in_=sv_), 16)
                    nc.sync.wait_ge(t.h, t.n)
                    emit_ln_span(p, sems, xfull, sp * 1024, 8, sx, sxn, stats,
                                 xnfm, [pt8a, pt8b], 1024)
                    nc.tensor.wait_ge(v.h, v.n)
                    emit_proj(p, sems, qkvw,
                              [(oc * 128, oc) for oc in range(32, 96)],
                              xnfm, 1024, tks, [wb0, wb1], pq, sev, ev_kv, bias_of)

        barrier(p, sems)
        with ExitStack() as sc2:
            emit_attention(p, sems, 8, 32, 1, qbuf, kbuf, vbuf, sctx, sc2, S)
        if debug:
            nc.sync.wait_ge(sems["a"].h, sems["a"].n)
            for hh in range(32):
                sems["qs"].inc(nc.sync.dma_start(
                    out=ctxd[hh * 128:(hh + 1) * 128, :],
                    in_=sctx[:, hh * 1024:(hh + 1) * 1024]), 16)

        barrier(p, sems)
        with ExitStack() as sc3:
            emit_outproj(p, sems, ow, sctx, 32, xloc, out, sob, sones, sc3)

        nc.sync.wait_ge(qs.h, qs.n)
        nc.sync.wait_ge(qg.h, qg.n)
        nc.sync.wait_ge(sems["st"].h, sems["st"].n)


def build_l0_attn(debug=False):
    p = Prog("l0attn")
    nc = p.nc
    xloc = p.inp("xloc", [TLOC, H], F32)
    xfull = p.inp("xfull", [S, H], F32)
    qkvw = p.inp("qkvw", [H, 3 * H], F32)
    qkvb2 = p.inp("qkvb2", [96, 128], F32)
    ow = p.inp("ow", [H, H], F32)
    ob = p.inp("ob", [1, H], F32)
    ones = p.inp("ones", [1, 128], F32)
    eye = p.inp("eye", [128, 128], F32)
    out = p.outp("out", [TLOC, H], F32)
    dbg = None
    if debug:
        dbg = dict(qbuf=p.outp("qbuf", [H, TLOC], BF16),
                   kbuf=p.outp("kbuf", [H, S], BF16),
                   vbuf=p.outp("vbuf", [H, S], BF16),
                   ctxd=p.outp("ctxd", [H, TLOC], BF16))
    with p.es:
        block = p.es.enter_context(nc.Block())
        sems = mk_sems(p, nc, "a")
        emit_block_l0attn(p, sems, xloc, xfull, qkvw, qkvb2, ow, ob, ones, eye,
                          out, debug, dbg)
    return p



_ONES = np.ones((1, 128), np.float32)
_EYE = np.eye(128, dtype=np.float32)


def _run_l0_attn(x, qkvw, qkvb, ow, ob, debug=False):
    p = build_l0_attn(debug)
    qkvb2 = np.ascontiguousarray(qkvb.reshape(96, 128))
    obr = np.ascontiguousarray(ob.reshape(1, H))
    in_maps = []
    for c in range(NCORE):
        b, h = c // 2, c % 2
        in_maps.append(dict(
            xloc=np.ascontiguousarray(x[b, h * TLOC:(h + 1) * TLOC]),
            xfull=np.ascontiguousarray(x[b]),
            qkvw=qkvw, qkvb2=qkvb2, ow=ow, ob=obr, ones=_ONES, eye=_EYE))
    res = run_bass_kernel_spmd(p.nc, in_maps, core_ids=list(range(NCORE)))
    out = np.empty((B, S, H), np.float32)
    for c in range(NCORE):
        b, h = c // 2, c % 2
        out[b, h * TLOC:(h + 1) * TLOC] = res.results[c]["out"]
    if debug:
        return out, res
    return out


def kernel(L0_attn_x, L0_attn_attn_qkvw, L0_attn_attn_qkvb, L0_attn_attn_ow,
           L0_attn_attn_ob, L0_ffn_qweight_gu, L0_ffn_qzeros_gu, L0_ffn_scales_gu,
           L0_ffn_qweight_dn, L0_ffn_qzeros_dn, L0_ffn_scales_dn, L1_attn_Wq,
           L1_attn_Wk, L1_attn_Wv, L1_attn_Wo, L1_ffn_qweight_gu, L1_ffn_qzeros_gu,
           L1_ffn_scales_gu, L1_ffn_g_idx_gu, L1_ffn_qweight_down,
           L1_ffn_qzeros_down, L1_ffn_scales_down, L1_ffn_g_idx_down):
    return kernel_fused(dict(
        L0_attn_x=L0_attn_x, L0_attn_attn_qkvw=L0_attn_attn_qkvw,
        L0_attn_attn_qkvb=L0_attn_attn_qkvb, L0_attn_attn_ow=L0_attn_attn_ow,
        L0_attn_attn_ob=L0_attn_attn_ob, L0_ffn_qweight_gu=L0_ffn_qweight_gu,
        L0_ffn_qzeros_gu=L0_ffn_qzeros_gu, L0_ffn_scales_gu=L0_ffn_scales_gu,
        L0_ffn_qweight_dn=L0_ffn_qweight_dn, L0_ffn_qzeros_dn=L0_ffn_qzeros_dn,
        L0_ffn_scales_dn=L0_ffn_scales_dn, L1_attn_Wq=L1_attn_Wq,
        L1_attn_Wk=L1_attn_Wk, L1_attn_Wv=L1_attn_Wv, L1_attn_Wo=L1_attn_Wo,
        L1_ffn_qweight_gu=L1_ffn_qweight_gu, L1_ffn_qzeros_gu=L1_ffn_qzeros_gu,
        L1_ffn_scales_gu=L1_ffn_scales_gu, L1_ffn_g_idx_gu=L1_ffn_g_idx_gu,
        L1_ffn_qweight_down=L1_ffn_qweight_down,
        L1_ffn_qzeros_down=L1_ffn_qzeros_down,
        L1_ffn_scales_down=L1_ffn_scales_down,
        L1_ffn_g_idx_down=L1_ffn_g_idx_down))



def emit_unpack_n(p, sems, packed_words, nib_u8, w):
    nc = p.nc
    v = sems["v"]
    pk8 = packed_words.bitcast(U8)
    nc.vector.tensor_scalar(out=nib_u8[:, 0:w:2], in0=pk8, scalar1=15,
                            scalar2=None, op0=ALU.bitwise_and)
    v.inc(nc.vector.tensor_scalar(out=nib_u8[:, 1:w:2], in0=pk8, scalar1=4,
                                  scalar2=None, op0=ALU.logical_shift_right))


def emit_block_l0ffn(p, sems, xin, qw_gu, qz_gu, sc_gu, qw_dn, qz_dn, sc_dn,
                     eye, out):
    nc = p.nc
    hgbuf = p.scratch(p.u("hgbuf"), [INTER, TLOC], BF16)
    hbuf = p.scratch(p.u("hbuf"), [INTER, TLOC], BF16)

    with block_scope(p):
        qs, qg, qc, a, v, t = (sems[k] for k in ["qs", "qg", "qc", "a", "v", "t"])

        p.seye = p.sb("seye", [128, 128], BF16)
        st1 = p.sb("st1", [128, 1], F32)
        st2 = p.sb("st2", [128, 1], F32)
        stm = p.sb("stm", [128, 1], F32)
        stv = p.sb("stv", [128, 1], F32)
        stb = p.sb("stb", [128, 1], F32)
        stats = (st1, st2, stm, stv, stb)
        xnfm = p.sb("xnfm", [128, 32 * 1024], BF16)
        pk_gu_s = p.sb("pk_gu_s", [32, 2752], I32)
        pk_dn_s = p.sb("pk_dn_s", [86, 512], I32)
        snz = p.sb("snz", [128, 512], U8)
        ssc2 = p.sb("ssc2", [128, 512], F32)
        sm2 = p.sb("sm2", [128, 512], BF16)
        sxg = p.sb("sxg", [32, TLOC], BF16)
        sxh = p.sb("sxh", [86, TLOC], BF16)
        sioc = p.sb("sioc", [128, 86], F32)
        sonehot = p.sb("sonehot", [128, 86], BF16)

        qg.inc(nc.gpsimd.dma_start(out=p.seye[:], in_=eye), 16)
        sems["g"].inc(nc.gpsimd.iota(out=sioc[:], pattern=[[1, 86]], base=0,
                                     channel_multiplier=0,
                                     allow_small_or_imprecise_dtypes=True))

        qs.inc(nc.sync.dma_start(out=pk_gu_s[:], in_=qz_gu), 16)
        qs.inc(nc.sync.dma_start(out=pk_dn_s[:], in_=qz_dn), 16)

        def emit_m2_slice(pk_sb, nrow, sc_src, c0, w):
            nc.scalar.wait_ge(v.h, v.n)
            qc.inc(nc.scalar.dma_start(out=ssc2[0:nrow, 0:w],
                                       in_=sc_src[:, c0:c0 + w]), 16)
            nc.vector.wait_ge(qc.h, qc.n)
            nc.vector.wait_ge(t.h, t.n)
            pk8 = pk_sb[0:nrow, c0 // 8:(c0 + w) // 8].bitcast(U8)
            nc.vector.tensor_scalar(out=snz[0:nrow, 0:w:2], in0=pk8, scalar1=15,
                                    scalar2=None, op0=ALU.bitwise_and)
            v.inc(nc.vector.tensor_scalar(out=snz[0:nrow, 1:w:2], in0=pk8,
                                          scalar1=4, scalar2=None,
                                          op0=ALU.logical_shift_right))
            nc.vector.wait_ge(v.h, v.n)
            v.inc(nc.vector.scalar_tensor_tensor(out=ssc2[0:nrow, 0:w],
                                                 in0=snz[0:nrow, 0:w],
                                                 scalar=8.0, in1=ssc2[0:nrow, 0:w],
                                                 op0=ALU.subtract, op1=ALU.mult))
            nc.vector.wait_ge(v.h, v.n)
            v.inc(nc.vector.tensor_scalar(out=sm2[0:nrow, 0:w],
                                          in0=ssc2[0:nrow, 0:w], scalar1=-1.0,
                                          scalar2=None, op0=ALU.mult))

        barrier(p, sems)
        with ExitStack() as sc1:
            sx = p.sb("sx", [128, H], F32, sc1)
            sxn = p.sb("sxn", [128, H], BF16, sc1)
            soh32 = p.sb("soh32", [128, 32], BF16, sc1)
            with nc.psum_tensor(p.u("pt8a"), [128, 1024], BF16) as pt8a, \
                 nc.psum_tensor(p.u("pt8b"), [128, 1024], BF16) as pt8b, \
                 nc.psum_tensor(p.u("pxg"), [32, 1024], F32) as pxg:
                emit_ln_span(p, sems, xin, 0, 8, sx, sxn, stats, xnfm,
                             [pt8a, pt8b], 1024)
                nc.vector.wait_ge(sems["g"].h, sems["g"].n)
                for kt in range(32):
                    nc.vector.wait_ge(t.h, t.n)
                    v.inc(nc.vector.tensor_scalar(out=soh32[:, 0:32],
                                                  in0=sioc[:, 0:32],
                                                  scalar1=float(kt), scalar2=None,
                                                  op0=ALU.is_equal))
                    nc.tensor.wait_ge(v.h, v.n)
                    for tk in range(2):
                        t.inc(nc.tensor.matmul(pxg[:, tk * 512:(tk + 1) * 512],
                                               soh32[:, 0:32],
                                               xnfm[:, kt * 1024 + tk * 512:
                                                    kt * 1024 + (tk + 1) * 512],
                                               start=(kt == 0), stop=(kt == 31)))
                nc.scalar.wait_ge(t.h, t.n)
                a.inc(nc.scalar.copy(out=sxg[:], in_=pxg[:]))
                nc.tensor.wait_ge(a.h, a.n)

        barrier(p, sems)
        with ExitStack() as sc2:
            spk = p.sb("spk", [128, 32 * 64], I32, sc2)
            spl0 = p.sb("spl0", [128, 8 * 512], F32, sc2)
            spl1 = p.sb("spl1", [128, 8 * 512], F32, sc2)
            snib = p.sb("snib", [128, 512], U8, sc2)
            sw = p.sb("sw", [128, 512], BF16, sc2)
            shg = p.sb("shg", [128, 512], BF16, sc2)
            shh = p.sb("shh", [128, 512], BF16, sc2)
            spl = [spl0, spl1]
            sw2 = [sw, p.sb("sw", [128, 512], BF16, sc2)]
            pls = [Sem(p_sem_raw(p, "plsA")), Sem(p_sem_raw(p, "plsB"))]
            plfree = [0, 0]
            swready = [0, 0]
            swfree = [0, 0]

            def gu_slab(j0, w, is_gate):
                nj = w // 128
                with ExitStack() as pes:
                    pg = [[p.psum(f"pg{jc}{tk}", [128, 512], F32, pes)
                           for tk in range(2)] for jc in range(nj)]
                    nc.sync.wait_ge(v.h, v.n)
                    qs.inc(nc.sync.dma_start(
                        out=spk[:, 0:32 * (w // 8)],
                        in_=qw_gu[:, j0 // 8:(j0 + w) // 8]
                        .rearrange("(kt p) c -> p kt c", p=128)), 16)
                    for ktg in range(4):
                        bnk = ktg % 2
                        nc.scalar.wait_ge(v.h, plfree[bnk])
                        pls[bnk].inc(nc.scalar.dma_start(
                            out=spl[bnk][:, 0:8 * w],
                            in_=sc_gu[ktg * 8:(ktg + 1) * 8, j0:j0 + w]
                            .unsqueeze(0).broadcast_to([128, 8, w])), 16)
                        for kj in range(8):
                            kt = ktg * 8 + kj
                            wbuf = kt % 2
                            nc.vector.wait_ge(qs.h, qs.n)
                            nc.vector.wait_ge(pls[bnk].h, pls[bnk].n)
                            nc.vector.wait_ge(t.h, swfree[wbuf])
                            nc.vector.wait_ge(v.h, v.n)
                            emit_unpack_n(p, sems,
                                          spk[:, kt * (w // 8):(kt + 1) * (w // 8)],
                                          snib, w)
                            nc.vector.wait_ge(v.h, v.n)
                            v.inc(nc.vector.scalar_tensor_tensor(
                                out=sw2[wbuf][:, 0:w], in0=snib[:, 0:w], scalar=8.0,
                                in1=spl[bnk][:, kj * w:(kj + 1) * w],
                                op0=ALU.subtract, op1=ALU.mult))
                            swready[wbuf] = v.n
                            nc.tensor.wait_ge(v.h, swready[wbuf])
                            for jc in range(nj):
                                for tk in range(2):
                                    t.inc(nc.tensor.matmul(
                                        pg[jc][tk][:],
                                        sw2[wbuf][:, jc * 128:(jc + 1) * 128],
                                        xnfm[:, kt * 1024 + tk * 512:
                                             kt * 1024 + (tk + 1) * 512],
                                        start=(kt == 0), stop=False))
                            swfree[wbuf] = t.n
                            if kj == 7:
                                plfree[bnk] = v.n
                    emit_m2_slice(pk_gu_s, 32, sc_gu, j0, w)
                    nc.tensor.wait_ge(v.h, v.n)
                    for jc in range(nj):
                        for tk in range(2):
                            t.inc(nc.tensor.matmul(
                                pg[jc][tk][:],
                                sm2[0:32, jc * 128:(jc + 1) * 128],
                                sxg[:, tk * 512:(tk + 1) * 512],
                                start=False, stop=True))
                    for jc in range(nj):
                        jj = j0 + jc * 128
                        for tk in range(2):
                            if is_gate:
                                nc.scalar.wait_ge(t.h, t.n)
                                nc.scalar.wait_ge(qs.h, qs.n)
                                a.inc(nc.scalar.activation(out=shg[:],
                                                           in_=pg[jc][tk][:],
                                                           func=AF.Silu))
                                nc.sync.wait_ge(a.h, a.n)
                                qs.inc(nc.sync.dma_start(
                                    out=hgbuf[jj:jj + 128,
                                              tk * 512:(tk + 1) * 512],
                                    in_=shg[:]), 16)
                            else:
                                jp = jj - INTER
                                nc.scalar.wait_ge(qs.h, qs.n)
                                qc.inc(nc.scalar.dma_start(
                                    out=shg[:],
                                    in_=hgbuf[jp:jp + 128,
                                              tk * 512:(tk + 1) * 512]), 16)
                                nc.vector.wait_ge(qc.h, qc.n)
                                nc.vector.wait_ge(t.h, t.n)
                                nc.vector.wait_ge(qs.h, qs.n)
                                v.inc(nc.vector.tensor_tensor(
                                    out=sw[:, 0:512], in0=pg[jc][tk][:],
                                    in1=shg[:], op=ALU.mult))
                                nc.sync.wait_ge(v.h, v.n)
                                qs.inc(nc.sync.dma_start(
                                    out=hbuf[jp:jp + 128,
                                             tk * 512:(tk + 1) * 512],
                                    in_=sw[:, 0:512]), 16)
                    nc.tensor.wait_ge(a.h, a.n)
                    nc.tensor.wait_ge(v.h, v.n)

            slabs = [(i * 512, 512) for i in range(21)] + [(10752, 256)]
            for j0, w in slabs:
                gu_slab(j0, w, True)
            for j0, w in slabs:
                gu_slab(INTER + j0, w, False)

        barrier(p, sems)
        with ExitStack() as sc3:
            sht = p.sb("sht", [128, TLOC], BF16, sc3)
            with nc.psum_tensor(p.u("pxh"), [86, 1024], F32) as pxh:
                nc.sync.wait_ge(qs.h, qs.n)
                nc.sync.wait_ge(sems["st"].h, sems["st"].n)
                for g in range(86):
                    nc.sync.wait_ge(t.h, t.n)
                    qs.inc(nc.sync.dma_start(out=sht[:],
                                             in_=hbuf[g * 128:(g + 1) * 128, :]), 16)
                    nc.vector.wait_ge(t.h, t.n)
                    nc.vector.wait_ge(sems["g"].h, sems["g"].n)
                    v.inc(nc.vector.tensor_scalar(out=sonehot[:], in0=sioc[:],
                                                  scalar1=float(g), scalar2=None,
                                                  op0=ALU.is_equal))
                    nc.tensor.wait_ge(v.h, v.n)
                    nc.tensor.wait_ge(qs.h, qs.n)
                    for tk in range(2):
                        t.inc(nc.tensor.matmul(pxh[:, tk * 512:(tk + 1) * 512],
                                               sonehot[:],
                                               sht[:, tk * 512:(tk + 1) * 512],
                                               start=(g == 0), stop=(g == 85)))
                nc.scalar.wait_ge(t.h, t.n)
                a.inc(nc.scalar.copy(out=sxh[:], in_=pxh[:]))
                nc.tensor.wait_ge(a.h, a.n)

        barrier(p, sems)
        with ExitStack() as sc4:
            spkd = p.sb("spkd", [128, 64], I32, sc4)
            spld = p.sb("spld", [128, 512], F32, sc4)
            snibd = p.sb("snibd", [128, 512], U8, sc4)
            swd0 = p.sb("swd0", [128, 512], BF16, sc4)
            swd1 = p.sb("swd1", [128, 512], BF16, sc4)
            swd = [swd0, swd1]
            sht0 = p.sb("sht0", [128, TLOC], BF16, sc4)
            sht1 = p.sb("sht1", [128, TLOC], BF16, sc4)
            shts = [sht0, sht1]
            sxr = p.sb("sxr", [128, 512], F32, sc4)
            sout = p.sb("sout", [128, 512], F32, sc4)
            hs = [Sem(p_sem_raw(p, "hsA")), Sem(p_sem_raw(p, "hsB"))]
            pks = Sem(p_sem_raw(p, "pks"))
            pls2 = Sem(p_sem_raw(p, "pls2"))
            with ExitStack() as pes:
                pd = [p.psum(f"pd{i}", [128, 512], F32, pes) for i in range(8)]
                wdready = [0, 0]
                wdfree = [0, 0]
                for dm in range(8):
                    hfree = [t.n, t.n]
                    for kt in range(86):
                        buf = kt % 2
                        nc.sync.wait_ge(t.h, hfree[buf])
                        hs[buf].inc(nc.sync.dma_start(
                            out=shts[buf][:],
                            in_=hbuf[kt * 128:(kt + 1) * 128, :]), 16)
                        nc.scalar.wait_ge(v.h, v.n)
                        pks.inc(nc.scalar.dma_start(
                            out=spkd[:],
                            in_=qw_dn[kt * 128:(kt + 1) * 128,
                                      dm * 64:(dm + 1) * 64]), 16)
                        pls2.inc(nc.scalar.dma_start(
                            out=spld[:],
                            in_=sc_dn[kt:kt + 1, dm * 512:(dm + 1) * 512]
                            .partition_broadcast(128)), 16)
                        nc.vector.wait_ge(pks.h, pks.n)
                        nc.vector.wait_ge(pls2.h, pls2.n)
                        nc.vector.wait_ge(t.h, wdfree[buf])
                        nc.vector.wait_ge(v.h, v.n)
                        emit_unpack_n(p, sems, spkd[:], snibd, 512)
                        nc.vector.wait_ge(v.h, v.n)
                        v.inc(nc.vector.scalar_tensor_tensor(
                            out=swd[buf][:], in0=snibd[:], scalar=8.0,
                            in1=spld[:], op0=ALU.subtract, op1=ALU.mult))
                        wdready[buf] = v.n
                        nc.tensor.wait_ge(v.h, wdready[buf])
                        nc.tensor.wait_ge(hs[buf].h, hs[buf].n)
                        if kt == 0:
                            nc.tensor.wait_ge(a.h, a.n)
                            nc.tensor.wait_ge(qg.h, qg.n)
                        for tt in range(8):
                            t.inc(nc.tensor.matmul(
                                pd[tt][:],
                                shts[buf][:, tt * 128:(tt + 1) * 128],
                                swd[buf][:], start=(kt == 0), stop=False))
                        hfree[buf] = t.n
                        wdfree[buf] = t.n
                    emit_m2_slice(pk_dn_s, 86, sc_dn, dm * 512, 512)
                    nc.tensor.wait_ge(v.h, v.n)
                    for tt in range(8):
                        t.inc(nc.tensor.matmul(
                            pd[tt][:], sxh[:, tt * 128:(tt + 1) * 128],
                            sm2[0:86, 0:512],
                            start=False, stop=True))
                    for tt in range(8):
                        nc.gpsimd.wait_ge(v.h, v.n)
                        qg.inc(nc.gpsimd.dma_start(
                            out=sxr[:],
                            in_=xin[tt * 128:(tt + 1) * 128,
                                    dm * 512:(dm + 1) * 512]), 16)
                        nc.vector.wait_ge(qg.h, qg.n)
                        nc.vector.wait_ge(t.h, t.n)
                        v.inc(nc.vector.tensor_add(out=sout[:], in0=pd[tt][:],
                                                   in1=sxr[:]))
                        nc.gpsimd.wait_ge(v.h, v.n)
                        qg.inc(nc.gpsimd.dma_start(
                            out=out[tt * 128:(tt + 1) * 128,
                                    dm * 512:(dm + 1) * 512], in_=sout[:]), 16)
                        nc.vector.wait_ge(qg.h, qg.n)

        nc.sync.wait_ge(qs.h, qs.n)
        nc.sync.wait_ge(qg.h, qg.n)
        nc.sync.wait_ge(qc.h, qc.n)
        nc.sync.wait_ge(sems["st"].h, sems["st"].n)


def build_l0_ffn():
    p = Prog("l0ffn")
    nc = p.nc
    xin = p.inp("xin", [TLOC, H], F32)
    qw_gu = p.inp("qw_gu", [H, 2752], I32)
    qz_gu = p.inp("qz_gu", [32, 2752], I32)
    sc_gu = p.inp("sc_gu", [32, 2 * INTER], F32)
    qw_dn = p.inp("qw_dn", [INTER, 512], I32)
    qz_dn = p.inp("qz_dn", [86, 512], I32)
    sc_dn = p.inp("sc_dn", [86, H], F32)
    eye = p.inp("eye", [128, 128], F32)
    out = p.outp("out", [TLOC, H], F32)
    with p.es:
        block = p.es.enter_context(nc.Block())
        sems = mk_sems(p, nc, "b")
        emit_block_l0ffn(p, sems, xin, qw_gu, qz_gu, sc_gu, qw_dn, qz_dn, sc_dn,
                         eye, out)
    return p


def _run_l0_ffn(x1, qw_gu, qz_gu, sc_gu, qw_dn, qz_dn, sc_dn):
    p = build_l0_ffn()
    in_maps = []
    for c in range(NCORE):
        b, h = c // 2, c % 2
        in_maps.append(dict(
            xin=np.ascontiguousarray(x1[b, h * TLOC:(h + 1) * TLOC]),
            qw_gu=qw_gu, qz_gu=qz_gu, sc_gu=sc_gu,
            qw_dn=qw_dn, qz_dn=qz_dn, sc_dn=sc_dn, eye=_EYE))
    res = run_bass_kernel_spmd(p.nc, in_maps, core_ids=list(range(NCORE)))
    out = np.empty((B, S, H), np.float32)
    for c in range(NCORE):
        b, h = c // 2, c % 2
        out[b, h * TLOC:(h + 1) * TLOC] = res.results[c]["out"]
    return out



ROPE_C = 12582912.0


def emit_block_l1attn(p, sems, xloc, xfull, wq, wk, wv, wo, eye, invf, rotm,
                      posoff, out, span_off=None, debug=False, dbg=None):
    nc = p.nc
    qbuf = dbg["qbuf"] if debug else p.scratch(p.u("qbuf"), [H, TLOC], BF16)
    kbuf = dbg["kbuf"] if debug else p.scratch(p.u("kbuf"), [1024, S], BF16)
    vbuf = dbg["vbuf"] if debug else p.scratch(p.u("vbuf"), [1024, S], BF16)
    wtbuf = dbg["wtbuf"] if debug else p.scratch(p.u("wtbuf"), [H, 10240], BF16)
    ctxd = dbg["ctxd"] if debug else None
    tabd = dbg["tabd"] if debug else None

    with block_scope(p):
        qs, qg, qc, a, v, t = (sems[k] for k in ["qs", "qg", "qc", "a", "v", "t"])

        p.seye = p.sb("seye", [128, 128], BF16)
        srotm = p.sb("srotm", [128, 128], BF16)
        sinvf = p.sb("sinvf", [128, 1], F32)
        soff = p.sb("soff", [128, 1], F32)
        st1 = p.sb("st1", [128, 1], F32)
        st2 = p.sb("st2", [128, 1], F32)
        stm = p.sb("stm", [128, 1], F32)
        stv = p.sb("stv", [128, 1], F32)
        stb = p.sb("stb", [128, 1], F32)
        stats = (st1, st2, stm, stv, stb)
        scos = p.sb("scos", [128, 1024], F32)
        ssin = p.sb("ssin", [128, 1024], F32)
        sang = p.sb("sang", [128, 1024], F32)
        srnd = p.sb("srnd", [128, 1024], F32)
        sctx = p.sb("sctx", [128, 32 * 1024], BF16)

        qg.inc(nc.gpsimd.dma_start(out=p.seye[:], in_=eye), 16)
        qg.inc(nc.gpsimd.dma_start(out=srotm[:], in_=rotm), 16)
        qs.inc(nc.sync.dma_start(out=sinvf[:], in_=invf), 16)
        qs.inc(nc.sync.dma_start(out=soff[:], in_=posoff.partition_broadcast(128)), 16)

        def rope_tables(off_ap, off_imm):
            def vstep(i):
                v.inc(i)
                nc.vector.wait_ge(v.h, v.n)
            nc.gpsimd.wait_ge(v.h, v.n)
            sems["g"].inc(nc.gpsimd.iota(out=sang[:], pattern=[[1, 1024]], base=0,
                                         channel_multiplier=0,
                                         allow_small_or_imprecise_dtypes=True))
            nc.vector.wait_ge(sems["g"].h, sems["g"].n)
            nc.vector.wait_ge(a.h, a.n)
            if off_ap is not None:
                vstep(nc.vector.tensor_scalar(out=sang[:], in0=sang[:],
                                              scalar1=off_ap, scalar2=None,
                                              op0=ALU.add))
            elif off_imm != 0.0:
                vstep(nc.vector.tensor_scalar(out=sang[:], in0=sang[:],
                                              scalar1=float(off_imm), scalar2=None,
                                              op0=ALU.add))
            vstep(nc.vector.tensor_scalar(out=sang[:], in0=sang[:],
                                          scalar1=sinvf[:, 0:1],
                                          scalar2=1.0 / (2 * np.pi),
                                          op0=ALU.mult, op1=ALU.mult))
            vstep(nc.vector.tensor_scalar(out=srnd[:], in0=sang[:], scalar1=ROPE_C,
                                          scalar2=ROPE_C, op0=ALU.add,
                                          op1=ALU.subtract))
            vstep(nc.vector.tensor_tensor(out=srnd[:], in0=sang[:], in1=srnd[:],
                                          op=ALU.subtract))
            vstep(nc.vector.tensor_scalar(out=srnd[:], in0=srnd[:],
                                          scalar1=2 * np.pi, scalar2=None,
                                          op0=ALU.mult))
            nc.scalar.wait_ge(v.h, v.n)
            a.inc(nc.scalar.activation(out=ssin[:], in_=srnd[:], func=AF.Sin))
            nc.vector.wait_ge(a.h, a.n)
            vstep(nc.vector.tensor_scalar(out=sang[:], in0=sang[:], scalar1=0.25,
                                          scalar2=None, op0=ALU.add))
            vstep(nc.vector.tensor_scalar(out=srnd[:], in0=sang[:], scalar1=ROPE_C,
                                          scalar2=ROPE_C, op0=ALU.add,
                                          op1=ALU.subtract))
            vstep(nc.vector.tensor_tensor(out=srnd[:], in0=sang[:], in1=srnd[:],
                                          op=ALU.subtract))
            vstep(nc.vector.tensor_scalar(out=srnd[:], in0=srnd[:],
                                          scalar1=2 * np.pi, scalar2=None,
                                          op0=ALU.mult))
            nc.scalar.wait_ge(v.h, v.n)
            a.inc(nc.scalar.activation(out=scos[:], in_=srnd[:], func=AF.Sin))
            nc.vector.wait_ge(a.h, a.n)

        barrier(p, sems)
        with ExitStack() as sc0:
            swr = p.sb("swr", [128, H], BF16, sc0)
            swt = p.sb("swt", [128, 32 * 128], BF16, sc0)
            with nc.psum_tensor(p.u("pt8a"), [128, 1024], BF16) as pt8a, \
                 nc.psum_tensor(p.u("pt8b"), [128, 1024], BF16) as pt8b:
                pt8 = [pt8a, pt8b]
                srcs = [(wq, 32, 0), (wk, 8, 4096), (wv, 8, 5120), (wo, 32, 6144)]
                for wsrc, noc, colbase in srcs:
                    for oc in range(noc):
                        nc.gpsimd.wait_ge(t.h, t.n)
                        qg.inc(nc.gpsimd.dma_start(
                            out=swr[:], in_=wsrc[oc * 128:(oc + 1) * 128, :]), 16)
                        nc.tensor.wait_ge(qg.h, qg.n)
                        evfree = [v.n, v.n]
                        stored = sems["st"].n
                        for g in range(4):
                            bnk = g % 2
                            nc.tensor.wait_ge(v.h, evfree[bnk])
                            for j in range(8):
                                kt = g * 8 + j
                                t.inc(nc.tensor.matmul(
                                    pt8[bnk][:, j * 128:(j + 1) * 128],
                                    swr[:, kt * 128:(kt + 1) * 128],
                                    p.seye[:], is_transpose=True,
                                    start=(j == 0), stop=(j == 7)))
                            nc.vector.wait_ge(t.h, t.n)
                            nc.vector.wait_ge(sems["st"].h, stored)
                            v.inc(nc.vector.tensor_copy(
                                out=swt[:, g * 1024:(g + 1) * 1024],
                                in_=pt8[bnk][:]))
                            evfree[bnk] = v.n
                        nc.sync.wait_ge(v.h, v.n)
                        sems["st"].inc(nc.sync.dma_start(
                            out=wtbuf.rearrange("(kt pp) o -> pp kt o", pp=128)
                            [:, :, colbase + oc * 128:colbase + (oc + 1) * 128],
                            in_=swt[:].rearrange("pp (kt o) -> pp kt o", o=128)), 16)

        barrier(p, sems)
        with ExitStack() as sc1:
            sx = p.sb("sx", [128, H], F32, sc1)
            sxn = p.sb("sxn", [128, H], BF16, sc1)
            xnfm = p.sb("xnfm", [128, 32 * 1024], BF16, sc1)
            wb0 = p.sb("wb0", [128, 32 * 128], BF16, sc1)
            wb1 = p.sb("wb1", [128, 32 * 128], BF16, sc1)
            sev = p.sb("sev", [128, 1024], BF16, sc1)
            sev2 = p.sb("sev2", [128, 512], BF16, sc1)
            sro1 = p.sb("sro1", [128, 512], F32, sc1)
            sro2 = p.sb("sro2", [128, 512], F32, sc1)
            with nc.psum_tensor(p.u("pt8a"), [128, 1024], BF16) as pt8a, \
                 nc.psum_tensor(p.u("pt8b"), [128, 1024], BF16) as pt8b, \
                 nc.psum_tensor(p.u("pq0"), [128, 512], F32) as pq0, \
                 nc.psum_tensor(p.u("pq1"), [128, 512], F32) as pq1, \
                 nc.psum_tensor(p.u("prot"), [128, 512], F32) as prot:
                pq = [pq0, pq1]
                tks = [(0, 512), (512, 512)]

                def rope_store(dst, row0, col0):
                    def f(tag, tki, sev_sl, tw):
                        nc.tensor.wait_ge(a.h, a.n)
                        nc.tensor.wait_ge(v.h, v.n)
                        t.inc(nc.tensor.matmul(prot[:], srotm[:], sev_sl,
                                               start=True, stop=True))
                        nc.vector.wait_ge(t.h, t.n)
                        v.inc(nc.vector.tensor_tensor(
                            out=sro1[:], in0=sev_sl, in1=scos[:, tki * 512:(tki + 1) * 512],
                            op=ALU.mult))
                        v.inc(nc.vector.tensor_tensor(
                            out=sro2[:], in0=prot[:], in1=ssin[:, tki * 512:(tki + 1) * 512],
                            op=ALU.mult))
                        nc.vector.wait_ge(v.h, v.n)
                        nc.vector.wait_ge(sems["st"].h, sems["st"].n)
                        v.inc(nc.vector.tensor_add(out=sev2[:], in0=sro1[:],
                                                   in1=sro2[:]))
                        nc.sync.wait_ge(v.h, v.n)
                        sems["st"].inc(nc.sync.dma_start(
                            out=dst[row0 + tag * 128:row0 + (tag + 1) * 128,
                                    col0 + tki * 512:col0 + (tki + 1) * 512],
                            in_=sev2[:]), 16)
                    return f

                def plain_store(dst, row0, col0):
                    def f(tag, tki, sev_sl, tw):
                        sems["st"].inc(nc.sync.dma_start(
                            out=dst[row0 + tag * 128:row0 + (tag + 1) * 128,
                                    col0 + tki * 512:col0 + (tki + 1) * 512],
                            in_=sev_sl), 16)
                    return f

                wt_ap = wtbuf

                nc.gpsimd.wait_ge(qs.h, qs.n)
                nc.gpsimd.wait_ge(sems["st"].h, sems["st"].n)
                emit_ln_span(p, sems, xloc, 0, 8, sx, sxn, stats, xnfm,
                             [pt8a, pt8b], 1024)
                rope_tables(soff[:, 0:1], 0.0)
                if debug:
                    nc.sync.wait_ge(a.h, a.n)
                    qs.inc(nc.sync.dma_start(out=tabd[:, 0:1024], in_=scos[:]), 16)
                    qs.inc(nc.sync.dma_start(out=tabd[:, 1024:2048], in_=ssin[:]), 16)
                nc.tensor.wait_ge(v.h, v.n)
                emit_proj(p, sems, wt_ap, [(oc * 128, oc) for oc in range(32)],
                          xnfm, 1024, tks, [wb0, wb1], pq, sev,
                          rope_store(qbuf, 0, 0), None)

                for sp in range(2):
                    nc.sync.wait_ge(t.h, t.n)
                    emit_ln_span(p, sems, xfull, sp * 1024, 8, sx, sxn, stats,
                                 xnfm, [pt8a, pt8b], 1024)
                    if span_off is None:
                        rope_tables(None, float(sp * 1024))
                    else:
                        rope_tables(span_off(sp), 0.0)
                    nc.tensor.wait_ge(v.h, v.n)
                    emit_proj(p, sems, wt_ap,
                              [(4096 + oc * 128, oc) for oc in range(8)],
                              xnfm, 1024, tks, [wb0, wb1], pq, sev,
                              rope_store(kbuf, 0, sp * 1024), None)
                    emit_proj(p, sems, wt_ap,
                              [(5120 + oc * 128, oc) for oc in range(8)],
                              xnfm, 1024, tks, [wb0, wb1], pq, sev,
                              plain_store(vbuf, 0, sp * 1024), None)

        barrier(p, sems)
        with ExitStack() as sc2:
            emit_attention(p, sems, 8, 8, 4, qbuf, kbuf, vbuf, sctx, sc2, S)
        if debug:
            nc.sync.wait_ge(sems["a"].h, sems["a"].n)
            for hh in range(32):
                qs.inc(nc.sync.dma_start(
                    out=ctxd[hh * 128:(hh + 1) * 128, :],
                    in_=sctx[:, hh * 1024:(hh + 1) * 1024]), 16)

        barrier(p, sems)
        with ExitStack() as sc3:
            emit_outproj(p, sems, wtbuf[:, 6144:10240], sctx, 32, xloc, out,
                         None, None, sc3)

        nc.sync.wait_ge(qs.h, qs.n)
        nc.sync.wait_ge(qg.h, qg.n)
        nc.sync.wait_ge(qc.h, qc.n)
        nc.sync.wait_ge(sems["st"].h, sems["st"].n)


def build_l1_attn(debug=False):
    p = Prog("l1attn")
    nc = p.nc
    xloc = p.inp("xloc", [TLOC, H], F32)
    xfull = p.inp("xfull", [S, H], F32)
    wq = p.inp("wq", [H, H], F32)
    wk = p.inp("wk", [1024, H], F32)
    wv = p.inp("wv", [1024, H], F32)
    wo = p.inp("wo", [H, H], F32)
    eye = p.inp("eye", [128, 128], F32)
    invf = p.inp("invf", [128, 1], F32)
    rotm = p.inp("rotm", [128, 128], F32)
    posoff = p.inp("posoff", [1, 1], F32)
    out = p.outp("out", [TLOC, H], F32)
    dbg = None
    if debug:
        dbg = dict(qbuf=p.outp("qbuf", [H, TLOC], BF16),
                   kbuf=p.outp("kbuf", [1024, S], BF16),
                   vbuf=p.outp("vbuf", [1024, S], BF16),
                   wtbuf=p.outp("wtbuf", [H, 10240], BF16),
                   ctxd=p.outp("ctxd", [H, TLOC], BF16),
                   tabd=p.outp("tabd", [128, 2048], F32))
    with p.es:
        block = p.es.enter_context(nc.Block())
        sems = mk_sems(p, nc, "c")
        emit_block_l1attn(p, sems, xloc, xfull, wq, wk, wv, wo, eye, invf, rotm,
                          posoff, out, None, debug, dbg)
    return p


_INVF = (1.0 / (500000.0 ** (np.arange(128, dtype=np.float32) % 64 * 2.0 / 128.0))
         ).reshape(128, 1).astype(np.float32)
_ROTM = np.zeros((128, 128), np.float32)
_ROTM[np.arange(64) + 64, np.arange(64)] = -1.0
_ROTM[np.arange(64), np.arange(64) + 64] = 1.0


def _run_l1_attn(x2, wq, wk, wv, wo, debug=False):
    p = build_l1_attn(debug)
    in_maps = []
    for c in range(NCORE):
        b, h = c // 2, c % 2
        in_maps.append(dict(
            xloc=np.ascontiguousarray(x2[b, h * TLOC:(h + 1) * TLOC]),
            xfull=np.ascontiguousarray(x2[b]),
            wq=wq, wk=wk, wv=wv, wo=wo, eye=_EYE, invf=_INVF, rotm=_ROTM,
            posoff=np.full((1, 1), h * TLOC, np.float32)))
    res = run_bass_kernel_spmd(p.nc, in_maps, core_ids=list(range(NCORE)))
    out = np.empty((B, S, H), np.float32)
    for c in range(NCORE):
        b, h = c // 2, c % 2
        out[b, h * TLOC:(h + 1) * TLOC] = res.results[c]["out"]
    if debug:
        return out, res
    return out



def emit_block_l1ffn(p, sems, xin, qw_gu, qz_gu, sc_gu, gix_gu, qw_dn, qz_dn,
                     sc_dn, gix_dn, shifts, eye, out):
    nc = p.nc
    hgbuf = p.scratch(p.u("hgbuf"), [INTER, TLOC], BF16)
    hbuf = p.scratch(p.u("hbuf"), [INTER, TLOC], BF16)
    scsl_gu = p.scratch(p.u("scsl_gu"), [86, 32, 256], F32)
    scsl_dn = p.scratch(p.u("scsl_dn"), [16, 86, 256], F32)

    with block_scope(p):
        qs, qg, qc, a, v, t = (sems[k] for k in ["qs", "qg", "qc", "a", "v", "t"])

        p.seye = p.sb("seye", [128, 128], BF16)
        st1 = p.sb("st1", [128, 1], F32)
        st2 = p.sb("st2", [128, 1], F32)
        stm = p.sb("stm", [128, 1], F32)
        stv = p.sb("stv", [128, 1], F32)
        stb = p.sb("stb", [128, 1], F32)
        stats = (st1, st2, stm, stv, stb)
        xnfm = p.sb("xnfm", [128, 32 * 1024], BF16)
        pk_gu_s = p.sb("pk_gu_s", [32, 2752], I32)
        pk_dn_s = p.sb("pk_dn_s", [86, 512], I32)
        snz = p.sb("snz", [128, 512], U8)
        ssc2 = p.sb("ssc2", [128, 512], F32)
        sm2 = p.sb("sm2", [128, 512], BF16)
        sxg = p.sb("sxg", [32, TLOC], BF16)
        sxh = p.sb("sxh", [86, TLOC], BF16)
        sioc = p.sb("sioc", [128, 86], F32)
        sonehot = p.sb("sonehot", [128, 86], BF16)
        sshift = p.sb("sshift", [128, 1], I32)
        sgix = p.sb("sgix", [128, 1], I32)
        sgixf = p.sb("sgixf", [128, 1], F32)

        qg.inc(nc.gpsimd.dma_start(out=p.seye[:], in_=eye), 16)
        qs.inc(nc.sync.dma_start(out=sshift[:], in_=shifts), 16)
        sems["g"].inc(nc.gpsimd.iota(out=sioc[:], pattern=[[1, 86]], base=0,
                                     channel_multiplier=0,
                                     allow_small_or_imprecise_dtypes=True))
        qs.inc(nc.sync.dma_start(out=pk_gu_s[:], in_=qz_gu.bitcast(I32)), 16)
        qs.inc(nc.sync.dma_start(out=pk_dn_s[:], in_=qz_dn.bitcast(I32)), 16)
        qs.inc(nc.sync.dma_start(out=scsl_gu,
                                 in_=sc_gu.rearrange("g (s w) -> s g w", w=256)), 16)
        qs.inc(nc.sync.dma_start(out=scsl_dn,
                                 in_=sc_dn.rearrange("g (s w) -> s g w", w=256)), 16)
        sc_gu_flat = scsl_gu.rearrange("s g w -> (s g) w")
        sc_dn_flat = scsl_dn.rearrange("s g w -> (s g) w")
        nc.gpsimd.wait_ge(qs.h, qs.n)

        def emit_m2_slice_d(pk_sb, nrow, sc_src, c0, w):
            nc.scalar.wait_ge(v.h, v.n)
            qc.inc(nc.scalar.dma_start(out=ssc2[0:nrow, 0:w],
                                       in_=sc_src[:, c0:c0 + w]), 16)
            nc.vector.wait_ge(qc.h, qc.n)
            nc.vector.wait_ge(t.h, t.n)
            pk8 = pk_sb[0:nrow, c0 // 8:(c0 + w) // 8].bitcast(U8)
            nc.vector.tensor_scalar(out=snz[0:nrow, 0:w:2], in0=pk8, scalar1=15,
                                    scalar2=None, op0=ALU.bitwise_and)
            v.inc(nc.vector.tensor_scalar(out=snz[0:nrow, 1:w:2], in0=pk8,
                                          scalar1=4, scalar2=None,
                                          op0=ALU.logical_shift_right))
            nc.vector.wait_ge(v.h, v.n)
            v.inc(nc.vector.scalar_tensor_tensor(out=ssc2[0:nrow, 0:w],
                                                 in0=snz[0:nrow, 0:w],
                                                 scalar=7.0, in1=ssc2[0:nrow, 0:w],
                                                 op0=ALU.subtract, op1=ALU.mult))
            nc.vector.wait_ge(v.h, v.n)
            v.inc(nc.vector.tensor_scalar(out=sm2[0:nrow, 0:w],
                                          in0=ssc2[0:nrow, 0:w], scalar1=-1.0,
                                          scalar2=None, op0=ALU.mult))

        def onehot_gidx(gix_src, k0, ncols):
            nc.sync.wait_ge(v.h, v.n)
            qs.inc(nc.sync.dma_start(out=sgix[:], in_=gix_src[k0:k0 + 128, :]), 16)
            nc.vector.wait_ge(qs.h, qs.n)
            v.inc(nc.vector.tensor_copy(out=sgixf[:], in_=sgix[:]))
            nc.vector.wait_ge(v.h, v.n)
            nc.vector.wait_ge(t.h, t.n)
            nc.vector.wait_ge(sems["g"].h, sems["g"].n)
            v.inc(nc.vector.tensor_scalar(out=sonehot[:, 0:ncols],
                                          in0=sioc[:, 0:ncols],
                                          scalar1=sgixf[:, 0:1], scalar2=None,
                                          op0=ALU.is_equal))

        barrier(p, sems)
        with ExitStack() as sc1:
            sx = p.sb("sx", [128, H], F32, sc1)
            sxn = p.sb("sxn", [128, H], BF16, sc1)
            with nc.psum_tensor(p.u("pt8a"), [128, 1024], BF16) as pt8a, \
                 nc.psum_tensor(p.u("pt8b"), [128, 1024], BF16) as pt8b, \
                 nc.psum_tensor(p.u("pxg"), [32, 1024], F32) as pxg:
                emit_ln_span(p, sems, xin, 0, 8, sx, sxn, stats, xnfm,
                             [pt8a, pt8b], 1024)
                for kt in range(32):
                    onehot_gidx(gix_gu, kt * 128, 32)
                    nc.tensor.wait_ge(v.h, v.n)
                    for tk in range(2):
                        t.inc(nc.tensor.matmul(pxg[:, tk * 512:(tk + 1) * 512],
                                               sonehot[:, 0:32],
                                               xnfm[:, kt * 1024 + tk * 512:
                                                    kt * 1024 + (tk + 1) * 512],
                                               start=(kt == 0), stop=(kt == 31)))
                nc.scalar.wait_ge(t.h, t.n)
                a.inc(nc.scalar.copy(out=sxg[:], in_=pxg[:]))
                nc.tensor.wait_ge(a.h, a.n)

        barrier(p, sems)
        sgixa_gu = [p.sb("sgixg", [128, 1], I32) for _ in range(32)]
        sgixa_dn = [p.sb("sgixd", [128, 1], I32) for _ in range(86)]
        for kt in range(32):
            qs.inc(nc.sync.dma_start(out=sgixa_gu[kt][:],
                                     in_=gix_gu[kt * 128:(kt + 1) * 128, :]), 16)
        for kt in range(86):
            qs.inc(nc.sync.dma_start(out=sgixa_dn[kt][:],
                                     in_=gix_dn[kt * 128:(kt + 1) * 128, :]), 16)
        nc.gpsimd.wait_ge(qs.h, qs.n)
        with ExitStack() as sc2:
            spk = [p.sb("spk", [128, 512], I32, sc2) for _ in range(2)]
            spl = [p.sb("spl", [128, 512], F32, sc2) for _ in range(2)]
            snib = p.sb("snib", [128, 512], I32, sc2)
            sw = p.sb("sw", [128, 512], BF16, sc2)
            shg = p.sb("shg", [128, 512], BF16, sc2)
            shh = p.sb("shh", [128, 512], BF16, sc2)
            sw2 = [sw, p.sb("sw", [128, 512], BF16, sc2)]
            lds = [Sem(p_sem_raw(p, "ldsA")), Sem(p_sem_raw(p, "ldsB"))]
            sks = [Sem(p_sem_raw(p, "sksA")), Sem(p_sem_raw(p, "sksB"))]
            ldfree = [0, 0]
            swready = [0, 0]
            swfree = [0, 0]

            def gu_slab_d(j0, w, is_gate):
                nj = w // 128
                with ExitStack() as pes:
                    pg = [[p.psum(f"pg{jc}{tk}", [128, 512], F32, pes)
                           for tk in range(2)] for jc in range(nj)]

                    def issue_load(kt, buf):
                        nc.sync.wait_ge(v.h, ldfree[buf])
                        sks[buf].inc(nc.sync.dma_start(
                            out=spk[buf][:, 0:w],
                            in_=qw_gu.bitcast(I32)[kt * 16:(kt + 1) * 16, j0:j0 + w]
                            .unsqueeze(1).broadcast_to([16, 8, w])), 16)
                        nc.gpsimd.wait_ge(v.h, ldfree[buf])
                        for gi in range(w // 256):
                            ci = (j0 + gi * 256) // 256
                            lds[buf].inc(nc.gpsimd.indirect_dma_start(
                                out=spl[buf][:, gi * 256:(gi + 1) * 256],
                                out_offset=None, in_=sc_gu_flat,
                                in_offset=bass.IndirectOffsetOnAxis(
                                    sgixa_gu[kt][:, 0:1], 0),
                                element_offset=ci * 32 * 256), 16)

                    issue_load(0, 0)
                    for kt in range(32):
                        buf = kt % 2
                        if kt < 31:
                            issue_load(kt + 1, 1 - buf)
                        nc.vector.wait_ge(lds[buf].h, lds[buf].n)
                        nc.vector.wait_ge(sks[buf].h, sks[buf].n)
                        nc.vector.wait_ge(t.h, swfree[buf])
                        nc.vector.wait_ge(sems["st"].h, sems["st"].n)
                        nc.vector.wait_ge(v.h, v.n)
                        v.inc(nc.vector.tensor_scalar(
                            out=snib[:, 0:w], in0=spk[buf][:, 0:w],
                            scalar1=sshift[:, 0:1], scalar2=15,
                            op0=ALU.logical_shift_right, op1=ALU.bitwise_and))
                        nc.vector.wait_ge(v.h, v.n)
                        v.inc(nc.vector.scalar_tensor_tensor(
                            out=sw2[buf][:, 0:w], in0=snib[:, 0:w], scalar=8.0,
                            in1=spl[buf][:, 0:w], op0=ALU.subtract, op1=ALU.mult))
                        ldfree[buf] = v.n
                        swready[buf] = v.n
                        nc.tensor.wait_ge(v.h, swready[buf])
                        if kt == 0:
                            nc.tensor.wait_ge(a.h, a.n)
                        for jc in range(nj):
                            for tk in range(2):
                                t.inc(nc.tensor.matmul(
                                    pg[jc][tk][:],
                                    sw2[buf][:, jc * 128:(jc + 1) * 128],
                                    xnfm[:, kt * 1024 + tk * 512:
                                         kt * 1024 + (tk + 1) * 512],
                                    start=(kt == 0), stop=False))
                        swfree[buf] = t.n
                    emit_m2_slice_d(pk_gu_s, 32, sc_gu, j0, w)
                    nc.tensor.wait_ge(v.h, v.n)
                    for jc in range(nj):
                        for tk in range(2):
                            t.inc(nc.tensor.matmul(
                                pg[jc][tk][:],
                                sm2[0:32, jc * 128:(jc + 1) * 128],
                                sxg[:, tk * 512:(tk + 1) * 512],
                                start=False, stop=True))
                    for jc in range(nj):
                        jj = j0 + jc * 128
                        for tk in range(2):
                            if is_gate:
                                nc.scalar.wait_ge(t.h, t.n)
                                nc.scalar.wait_ge(sems["st"].h, sems["st"].n)
                                a.inc(nc.scalar.activation(out=shg[:],
                                                           in_=pg[jc][tk][:],
                                                           func=AF.Silu))
                                nc.sync.wait_ge(a.h, a.n)
                                sems["st"].inc(nc.sync.dma_start(
                                    out=hgbuf[jj:jj + 128,
                                              tk * 512:(tk + 1) * 512],
                                    in_=shg[:]), 16)
                            else:
                                jp = jj - INTER
                                nc.scalar.wait_ge(sems["st"].h, sems["st"].n)
                                qc.inc(nc.scalar.dma_start(
                                    out=shg[:],
                                    in_=hgbuf[jp:jp + 128,
                                              tk * 512:(tk + 1) * 512]), 16)
                                nc.vector.wait_ge(qc.h, qc.n)
                                nc.vector.wait_ge(t.h, t.n)
                                nc.vector.wait_ge(sems["st"].h, sems["st"].n)
                                v.inc(nc.vector.tensor_tensor(
                                    out=shh[:], in0=pg[jc][tk][:],
                                    in1=shg[:], op=ALU.mult))
                                nc.sync.wait_ge(v.h, v.n)
                                sems["st"].inc(nc.sync.dma_start(
                                    out=hbuf[jp:jp + 128,
                                             tk * 512:(tk + 1) * 512],
                                    in_=shh[:]), 16)
                    nc.tensor.wait_ge(a.h, a.n)
                    nc.tensor.wait_ge(v.h, v.n)

            slabs = [(i * 512, 512) for i in range(21)] + [(10752, 256)]
            for j0, w in slabs:
                gu_slab_d(j0, w, True)
            for j0, w in slabs:
                gu_slab_d(INTER + j0, w, False)

        barrier(p, sems)
        with ExitStack() as sc3:
            sht = p.sb("sht", [128, TLOC], BF16, sc3)
            with nc.psum_tensor(p.u("pxh"), [86, 1024], F32) as pxh:
                nc.sync.wait_ge(qs.h, qs.n)
                for g in range(86):
                    nc.sync.wait_ge(t.h, t.n)
                    qs.inc(nc.sync.dma_start(out=sht[:],
                                             in_=hbuf[g * 128:(g + 1) * 128, :]), 16)
                    onehot_gidx(gix_dn, g * 128, 86)
                    nc.tensor.wait_ge(v.h, v.n)
                    nc.tensor.wait_ge(qs.h, qs.n)
                    for tk in range(2):
                        t.inc(nc.tensor.matmul(pxh[:, tk * 512:(tk + 1) * 512],
                                               sonehot[:],
                                               sht[:, tk * 512:(tk + 1) * 512],
                                               start=(g == 0), stop=(g == 85)))
                nc.scalar.wait_ge(t.h, t.n)
                a.inc(nc.scalar.copy(out=sxh[:], in_=pxh[:]))
                nc.tensor.wait_ge(a.h, a.n)

        barrier(p, sems)
        with ExitStack() as sc4:
            spkd = p.sb("spkd", [128, 512], I32, sc4)
            spld = p.sb("spld", [128, 512], F32, sc4)
            snibd = p.sb("snibd", [128, 512], I32, sc4)
            swd0 = p.sb("swd0", [128, 512], BF16, sc4)
            swd1 = p.sb("swd1", [128, 512], BF16, sc4)
            swd = [swd0, swd1]
            sht0 = p.sb("sht0", [128, TLOC], BF16, sc4)
            sht1 = p.sb("sht1", [128, TLOC], BF16, sc4)
            shts = [sht0, sht1]
            sxr = p.sb("sxr", [128, 512], F32, sc4)
            sout = p.sb("sout", [128, 512], F32, sc4)
            hs = [Sem(p_sem_raw(p, "hsA")), Sem(p_sem_raw(p, "hsB"))]
            wds = [Sem(p_sem_raw(p, "wdsA")), Sem(p_sem_raw(p, "wdsB"))]
            pks2 = [Sem(p_sem_raw(p, "pks2A")), Sem(p_sem_raw(p, "pks2B"))]
            spkd2 = [spkd, p.sb("spkd", [128, 512], I32, sc4)]
            spld2 = [spld, p.sb("spld", [128, 512], F32, sc4)]
            hfree = [0, 0]
            wdfree = [0, 0]
            with ExitStack() as pes:
                pd = [p.psum(f"pd{i}", [128, 512], F32, pes) for i in range(8)]
                for dm in range(8):
                    def issue_dn(kt, buf, dm=dm):
                        nc.sync.wait_ge(t.h, hfree[buf])
                        hs[buf].inc(nc.sync.dma_start(
                            out=shts[buf][:],
                            in_=hbuf[kt * 128:(kt + 1) * 128, :]), 16)
                        nc.sync.wait_ge(v.h, wdfree[buf])
                        pks2[buf].inc(nc.sync.dma_start(
                            out=spkd2[buf][:],
                            in_=qw_dn.bitcast(I32)[kt * 16:(kt + 1) * 16,
                                                   dm * 512:(dm + 1) * 512]
                            .unsqueeze(1).broadcast_to([16, 8, 512])), 16)
                        nc.gpsimd.wait_ge(v.h, wdfree[buf])
                        for gi in range(2):
                            ci = dm * 2 + gi
                            wds[buf].inc(nc.gpsimd.indirect_dma_start(
                                out=spld2[buf][:, gi * 256:(gi + 1) * 256],
                                out_offset=None, in_=sc_dn_flat,
                                in_offset=bass.IndirectOffsetOnAxis(
                                    sgixa_dn[kt][:, 0:1], 0),
                                element_offset=ci * 86 * 256), 16)

                    issue_dn(0, 0)
                    for kt in range(86):
                        buf = kt % 2
                        if kt < 85:
                            issue_dn(kt + 1, 1 - buf)
                        nc.vector.wait_ge(wds[buf].h, wds[buf].n)
                        nc.vector.wait_ge(pks2[buf].h, pks2[buf].n)
                        nc.vector.wait_ge(t.h, t.n)
                        nc.vector.wait_ge(v.h, v.n)
                        v.inc(nc.vector.tensor_scalar(
                            out=snibd[:], in0=spkd2[buf][:],
                            scalar1=sshift[:, 0:1], scalar2=15,
                            op0=ALU.logical_shift_right, op1=ALU.bitwise_and))
                        nc.vector.wait_ge(v.h, v.n)
                        v.inc(nc.vector.scalar_tensor_tensor(
                            out=swd[buf][:], in0=snibd[:], scalar=8.0,
                            in1=spld2[buf][:], op0=ALU.subtract, op1=ALU.mult))
                        wdfree[buf] = v.n
                        nc.tensor.wait_ge(v.h, v.n)
                        nc.tensor.wait_ge(hs[buf].h, hs[buf].n)
                        if kt == 0:
                            nc.tensor.wait_ge(a.h, a.n)
                            nc.tensor.wait_ge(qg.h, qg.n)
                        for tt in range(8):
                            t.inc(nc.tensor.matmul(
                                pd[tt][:],
                                shts[buf][:, tt * 128:(tt + 1) * 128],
                                swd[buf][:], start=(kt == 0), stop=False))
                        hfree[buf] = t.n
                    emit_m2_slice_d(pk_dn_s, 86, sc_dn, dm * 512, 512)
                    nc.tensor.wait_ge(v.h, v.n)
                    for tt in range(8):
                        t.inc(nc.tensor.matmul(
                            pd[tt][:], sxh[:, tt * 128:(tt + 1) * 128],
                            sm2[0:86, 0:512],
                            start=False, stop=True))
                    for tt in range(8):
                        nc.gpsimd.wait_ge(v.h, v.n)
                        qg.inc(nc.gpsimd.dma_start(
                            out=sxr[:],
                            in_=xin[tt * 128:(tt + 1) * 128,
                                    dm * 512:(dm + 1) * 512]), 16)
                        nc.vector.wait_ge(qg.h, qg.n)
                        nc.vector.wait_ge(t.h, t.n)
                        v.inc(nc.vector.tensor_add(out=sout[:], in0=pd[tt][:],
                                                   in1=sxr[:]))
                        nc.gpsimd.wait_ge(v.h, v.n)
                        qg.inc(nc.gpsimd.dma_start(
                            out=out[tt * 128:(tt + 1) * 128,
                                    dm * 512:(dm + 1) * 512], in_=sout[:]), 16)
                        nc.vector.wait_ge(qg.h, qg.n)

        nc.sync.wait_ge(qs.h, qs.n)
        nc.sync.wait_ge(qg.h, qg.n)
        nc.sync.wait_ge(qc.h, qc.n)
        nc.sync.wait_ge(sems["st"].h, sems["st"].n)


def build_l1_ffn():
    p = Prog("l1ffn")
    nc = p.nc
    U32_ = mybir.dt.uint32
    xin = p.inp("xin", [TLOC, H], F32)
    qw_gu = p.inp("qw_gu", [512, 2 * INTER], U32_)
    qz_gu = p.inp("qz_gu", [32, 2752], U32_)
    sc_gu = p.inp("sc_gu", [32, 2 * INTER], F32)
    gix_gu = p.inp("gix_gu", [H, 1], I32)
    qw_dn = p.inp("qw_dn", [1376, H], U32_)
    qz_dn = p.inp("qz_dn", [86, 512], U32_)
    sc_dn = p.inp("sc_dn", [86, H], F32)
    gix_dn = p.inp("gix_dn", [INTER, 1], I32)
    shifts = p.inp("shifts", [128, 1], I32)
    eye = p.inp("eye", [128, 128], F32)
    out = p.outp("out", [TLOC, H], F32)
    with p.es:
        block = p.es.enter_context(nc.Block())
        sems = mk_sems(p, nc, "d")
        emit_block_l1ffn(p, sems, xin, qw_gu, qz_gu, sc_gu, gix_gu, qw_dn, qz_dn,
                         sc_dn, gix_dn, shifts, eye, out)
    return p


_SHIFTS = ((np.arange(128) % 8) * 4).astype(np.int32).reshape(128, 1)


def _run_l1_ffn(x3, qw_gu, qz_gu, sc_gu, gix_gu, qw_dn, qz_dn, sc_dn, gix_dn):
    p = build_l1_ffn()
    in_maps = []
    for c in range(NCORE):
        b, h = c // 2, c % 2
        in_maps.append(dict(
            xin=np.ascontiguousarray(x3[b, h * TLOC:(h + 1) * TLOC]),
            qw_gu=qw_gu, qz_gu=qz_gu, sc_gu=sc_gu,
            gix_gu=np.ascontiguousarray(gix_gu.reshape(H, 1)),
            qw_dn=qw_dn, qz_dn=qz_dn, sc_dn=sc_dn,
            gix_dn=np.ascontiguousarray(gix_dn.reshape(INTER, 1)),
            shifts=_SHIFTS, eye=_EYE))
    res = run_bass_kernel_spmd(p.nc, in_maps, core_ids=list(range(NCORE)))
    out = np.empty((B, S, H), np.float32)
    for c in range(NCORE):
        b, h = c // 2, c % 2
        out[b, h * TLOC:(h + 1) * TLOC] = res.results[c]["out"]
    return out



def emit_exchange(p, xdone, x2loc, x2sh, x2meta, x2flag, nonce, hsel):
    nc = p.nc
    with block_scope(p):
        snon = p.sb("snon", [1, 1], I32)
        shsel = p.sb("shsel", [1, 1], I32)
        sflagin = p.sb("sflagin", [1, 1], I32)
        xq = Sem(p_sem_raw(p, "xq"))
        pollq = Sem(p_sem_raw(p, "pollq"))
        g = nc.gpsimd
        xq.inc(g.dma_start(out=snon[:], in_=nonce), 16)
        xq.inc(g.dma_start(out=shsel[:], in_=hsel), 16)
        g.wait_ge(xq.h, xq.n)
        with g.register("rf") as rf, g.register("rc") as rc, \
             g.register("rn") as rn, g.register("nd") as nd:
            g.reg_load(rn, snon[0:1, 0:1])
            g.reg_mov(rc, 0)
            xq_base = xq.n
            for arm in g.switch_core_id(n=128):
                xq.n = xq_base
                par = (int(arm) % 8) % 2
                xq.inc(g.dma_start(out=x2sh[par * 1024:(par + 1) * 1024, :],
                                   in_=x2loc), 16)
                g.wait_ge(xq.h, xq.n)
                xq.inc(g.dma_start(out=x2meta[par:par + 1, :], in_=shsel[:]), 16)
                g.wait_ge(xq.h, xq.n)
                xq.inc(g.dma_start(out=x2flag[par:par + 1, :], in_=snon[:]), 16)
                g.wait_ge(xq.h, xq.n)
                g.reg_add(rc, rc, 16)
                g.dma_start(out=sflagin[:],
                            in_=x2flag[1 - par:2 - par, :]).then_inc(pollq.h, 16)
                g.wait_ge(pollq.h, rc)
                g.reg_load(rf, sflagin[0:1, 0:1])
                g.reg_alu(nd, rf, rn, ALU.not_equal)
                with g.While(nd):
                    g.reg_add(rc, rc, 16)
                    g.dma_start(out=sflagin[:],
                                in_=x2flag[1 - par:2 - par, :]).then_inc(pollq.h, 16)
                    g.wait_ge(pollq.h, rc)
                    g.reg_load(rf, sflagin[0:1, 0:1])
                    g.reg_alu(nd, rf, rn, ALU.not_equal)
            xq.n = xq_base + 48
        g.sem_inc(xdone.h, 1)
        xdone.n += 1


def build_fused(detect_races=True):
    p = Prog("fused", detect_races)
    nc = p.nc
    U32_ = mybir.dt.uint32
    xloc = p.inp("xloc", [TLOC, H], F32)
    xfull = p.inp("xfull", [S, H], F32)
    qkvw = p.inp("qkvw", [H, 3 * H], F32)
    qkvb2 = p.inp("qkvb2", [96, 128], F32)
    ow = p.inp("ow", [H, H], F32)
    ob = p.inp("ob", [1, H], F32)
    b_qw_gu = p.inp("b_qw_gu", [H, 2752], I32)
    b_qz_gu = p.inp("b_qz_gu", [32, 2752], I32)
    b_sc_gu = p.inp("b_sc_gu", [32, 2 * INTER], F32)
    b_qw_dn = p.inp("b_qw_dn", [INTER, 512], I32)
    b_qz_dn = p.inp("b_qz_dn", [86, 512], I32)
    b_sc_dn = p.inp("b_sc_dn", [86, H], F32)
    wq = p.inp("wq", [H, H], F32)
    wk = p.inp("wk", [1024, H], F32)
    wv = p.inp("wv", [1024, H], F32)
    wo = p.inp("wo", [H, H], F32)
    d_qw_gu = p.inp("d_qw_gu", [512, 2 * INTER], U32_)
    d_qz_gu = p.inp("d_qz_gu", [32, 2752], U32_)
    d_sc_gu = p.inp("d_sc_gu", [32, 2 * INTER], F32)
    d_gix_gu = p.inp("d_gix_gu", [H, 1], I32)
    d_qw_dn = p.inp("d_qw_dn", [1376, H], U32_)
    d_qz_dn = p.inp("d_qz_dn", [86, 512], U32_)
    d_sc_dn = p.inp("d_sc_dn", [86, H], F32)
    d_gix_dn = p.inp("d_gix_dn", [INTER, 1], I32)
    ones = p.inp("ones", [1, 128], F32)
    eye = p.inp("eye", [128, 128], F32)
    invf = p.inp("invf", [128, 1], F32)
    rotm = p.inp("rotm", [128, 128], F32)
    posoff = p.inp("posoff", [1, 1], F32)
    hsel = p.inp("hsel", [1, 1], I32)
    nonce = p.inp("nonce", [1, 1], I32)
    shifts = p.inp("shifts", [128, 1], I32)
    out = p.outp("out", [TLOC, H], F32)
    x1loc = p.scratch("x1loc", [TLOC, H], F32)
    x2loc = p.scratch("x2loc", [TLOC, H], F32)
    x3loc = p.scratch("x3loc", [TLOC, H], F32)
    x2sh = p.scratch("x2sh", [S, H], F32, shared=True)
    x2meta = p.scratch("x2meta", [2, 1], I32, shared=True)
    x2flag = p.scratch("x2flag", [2, 1], I32, shared=True)

    with p.es:
        block = p.es.enter_context(nc.Block())
        sa = mk_sems(p, nc, "a")
        sb_ = mk_sems(p, nc, "b")
        sc_ = mk_sems(p, nc, "c")
        sd_ = mk_sems(p, nc, "d")
        xdone = Sem(p_sem_raw(p, "xdone"))

        emit_block_l0attn(p, sa, xloc, xfull, qkvw, qkvb2, ow, ob, ones, eye,
                          x1loc)
        barrier(p, sa)
        emit_block_l0ffn(p, sb_, x1loc, b_qw_gu, b_qz_gu, b_sc_gu, b_qw_dn,
                         b_qz_dn, b_sc_dn, eye, x2loc)
        barrier(p, sb_)
        emit_exchange(p, xdone, x2loc, x2sh, x2meta, x2flag, nonce, hsel)
        for eng in [nc.sync, nc.gpsimd, nc.scalar, nc.vector, nc.tensor]:
            eng.wait_ge(xdone.h, 1)
        soff_sp = [p.sb(f"soffsp{i}", [128, 1], F32) for i in range(2)]
        smeta_i = p.sb("smeta_i", [128, 1], I32)
        for sp in range(2):
            sc_["qs"].inc(nc.sync.dma_start(
                out=smeta_i[:], in_=x2meta[sp:sp + 1, :].partition_broadcast(128)),
                16)
            nc.vector.wait_ge(sc_["qs"].h, sc_["qs"].n)
            nc.vector.wait_ge(sc_["v"].h, sc_["v"].n)
            sc_["v"].inc(nc.vector.tensor_scalar(
                out=soff_sp[sp][:], in0=smeta_i[:], scalar1=1024.0, scalar2=None,
                op0=ALU.mult))
        nc.vector.wait_ge(sc_["v"].h, sc_["v"].n)
        emit_block_l1attn(p, sc_, x2loc, x2sh, wq, wk, wv, wo, eye, invf, rotm,
                          posoff, x3loc,
                          span_off=lambda sp: soff_sp[sp][:, 0:1])
        barrier(p, sc_)
        emit_block_l1ffn(p, sd_, x3loc, d_qw_gu, d_qz_gu, d_sc_gu, d_gix_gu,
                         d_qw_dn, d_qz_dn, d_sc_dn, d_gix_dn, shifts, eye, out)
    return p


def kernel_fused(inp):
    p = build_fused()
    x = np.asarray(inp["L0_attn_x"], np.float32)
    qkvb2 = np.ascontiguousarray(np.asarray(inp["L0_attn_attn_qkvb"]).reshape(96, 128))
    obr = np.ascontiguousarray(np.asarray(inp["L0_attn_attn_ob"]).reshape(1, H))
    nonce = np.array([[np.random.randint(1, 2**31 - 1)]], np.int32)
    in_maps = []
    for c in range(NCORE):
        b, h = c // 2, c % 2
        in_maps.append(dict(
            xloc=np.ascontiguousarray(x[b, h * TLOC:(h + 1) * TLOC]),
            xfull=np.ascontiguousarray(x[b]),
            qkvw=np.asarray(inp["L0_attn_attn_qkvw"]), qkvb2=qkvb2,
            ow=np.asarray(inp["L0_attn_attn_ow"]), ob=obr,
            b_qw_gu=np.asarray(inp["L0_ffn_qweight_gu"]),
            b_qz_gu=np.asarray(inp["L0_ffn_qzeros_gu"]),
            b_sc_gu=np.asarray(inp["L0_ffn_scales_gu"]),
            b_qw_dn=np.asarray(inp["L0_ffn_qweight_dn"]),
            b_qz_dn=np.asarray(inp["L0_ffn_qzeros_dn"]),
            b_sc_dn=np.asarray(inp["L0_ffn_scales_dn"]),
            wq=np.asarray(inp["L1_attn_Wq"]), wk=np.asarray(inp["L1_attn_Wk"]),
            wv=np.asarray(inp["L1_attn_Wv"]), wo=np.asarray(inp["L1_attn_Wo"]),
            d_qw_gu=np.asarray(inp["L1_ffn_qweight_gu"]),
            d_qz_gu=np.asarray(inp["L1_ffn_qzeros_gu"]),
            d_sc_gu=np.asarray(inp["L1_ffn_scales_gu"]),
            d_gix_gu=np.ascontiguousarray(
                np.asarray(inp["L1_ffn_g_idx_gu"]).reshape(H, 1)),
            d_qw_dn=np.asarray(inp["L1_ffn_qweight_down"]),
            d_qz_dn=np.asarray(inp["L1_ffn_qzeros_down"]),
            d_sc_dn=np.asarray(inp["L1_ffn_scales_down"]),
            d_gix_dn=np.ascontiguousarray(
                np.asarray(inp["L1_ffn_g_idx_down"]).reshape(INTER, 1)),
            ones=_ONES, eye=_EYE, invf=_INVF, rotm=_ROTM,
            posoff=np.full((1, 1), h * TLOC, np.float32),
            hsel=np.full((1, 1), h, np.int32),
            nonce=nonce, shifts=_SHIFTS))
    res = run_bass_kernel_spmd(p.nc, in_maps, core_ids=list(range(NCORE)))
    outv = np.empty((B, S, H), np.float32)
    for c in range(NCORE):
        b, h = c // 2, c % 2
        outv[b, h * TLOC:(h + 1) * TLOC] = res.results[c]["out"]
    return outv
